# version 18
# baseline (speedup 1.0000x reference)
import numpy as np
import ml_dtypes
from contextlib import ExitStack

import concourse.bass as bass
import concourse.mybir as mybir
from concourse.bass_utils import run_bass_kernel_spmd
from concourse.alu_op_type import AluOpType as Op

F32 = mybir.dt.float32
BF16 = mybir.dt.bfloat16
I32 = mybir.dt.int32
AF = mybir.ActivationFunctionType
AX = mybir.AxisListType

NCORES = 8
D = 1024
CB = 8
SEQ = 2048
NB = 2
T = SEQ * NB
TB = SEQ // 128
NEG = -1.0e9
LN_8INV = float(np.log(0.125))
SQ2INV = float(1.0 / np.sqrt(2.0))



def _pos_bias(d):
    d = np.asarray(d, dtype=np.float64)
    return np.exp(-0.5 * (d / 2.0) ** 2) + np.cos(2.0 * np.pi * d / 401.0)


def _hilo(a64):
    hi = a64.astype(np.float32)
    lo = (a64 - hi.astype(np.float64)).astype(np.float32)
    return hi, lo


def host_constants():
    c = {}
    c["ident"] = np.eye(128, dtype=np.float32).astype(ml_dtypes.bfloat16)
    mbs = []
    for variant in range(3):
        mb = np.full((128, 38), NEG, dtype=np.float64)
        for p in range(128):
            j, pp = p // 32, p % 32
            for f in range(36):
                dist = pp + 2 - f
                if abs(dist) <= 2:
                    mb[p, f] = _pos_bias(dist)
            mb[p, 36] = 0.0
            mb[p, 37] = NEG
            if variant == 0 and j == 0:
                mb[p, 0] = NEG
                mb[p, 1] = NEG
                if pp <= 2:
                    mb[p, 36] = NEG
            if variant == 2 and j == 3:
                mb[p, 34] = NEG
                mb[p, 35] = NEG
        mbs.append(mb)
    mb64 = np.concatenate(mbs, axis=1)
    hi, lo = _hilo(mb64)
    c["maskbias"] = np.ascontiguousarray(np.concatenate([hi, lo], axis=1))
    i = (np.arange(128)[:, None] + 128 * np.arange(16)[None, :])
    hi, lo = _hilo(_pos_bias(i))
    c["colbias"] = np.ascontiguousarray(np.concatenate([hi, lo], axis=1))
    r064 = np.broadcast_to(_pos_bias(-np.arange(SEQ)), (16, SEQ))
    hi, lo = _hilo(r064)
    c["r0bias"] = np.ascontiguousarray(np.concatenate([hi, lo], axis=1))
    return c


def host_layout(inputs):
    rows, off = [], {}

    def add(name, arr):
        off[name] = sum(a.size for a in rows)
        rows.append(np.ascontiguousarray(arr, dtype=np.float32).reshape(-1))

    add("l0_bqkv", np.concatenate([np.asarray(inputs["L0_attn_bq"]),
                                   np.asarray(inputs["L0_attn_bk"]),
                                   np.asarray(inputs["L0_attn_bv"])]))
    add("l0_bo", inputs["L0_attn_bo"])
    add("l0_b1", inputs["L0_attn_b1"])
    add("l0_b2", inputs["L0_attn_b2"])
    add("f_bin", inputs["L0_ffn_bin_"])
    add("f_b1", np.asarray(inputs["L0_ffn_b1"]).reshape(-1))
    add("f_b2", inputs["L0_ffn_b2"])
    add("f_bout", inputs["L0_ffn_bout"])
    add("l1_bq", inputs["L1_attn_bq"])
    add("l1_bk", inputs["L1_attn_bk"])
    add("l1_bv", inputs["L1_attn_bv"])
    add("l1_bo", inputs["L1_attn_bo"])
    add("l1f_b", inputs["L1_ffn_bias"])
    d = {"cpack": np.concatenate(rows)[None, :], "_boff": off}
    W1 = np.asarray(inputs["L0_ffn_W1"], dtype=np.float32)
    W2 = np.asarray(inputs["L0_ffn_W2"], dtype=np.float32)
    blk1 = np.zeros((64, 1024), dtype=np.float32)
    blk2 = np.zeros((1024, 64), dtype=np.float32)
    m = np.arange(64)
    for i in range(16):
        blk1[m, 16 * m + i] = W1[:, i]
        blk2[16 * m + i, m] = W2[:, i]
    d["blk1"], d["blk2"] = blk1, blk2
    return d



class Sem:
    def __init__(self, es, nc, name):
        self.h = es.enter_context(nc.semaphore(name))
        self.n = 0

    def inc(self, instr, k=1):
        instr.then_inc(self.h, k)
        self.n += k
        return instr

    def dma(self, instr):
        return self.inc(instr, 16)

    def wait(self, eng, n=None):
        eng.wait_ge(self.h, self.n if n is None else n)


_NAME_CNT = [0]


def _uniq(name):
    _NAME_CNT[0] += 1
    return f"{name}_{_NAME_CNT[0]}"


class _ActIns:
    def __init__(self):
        self.incs = []

    def then_inc(self, h, k):
        self.incs.append((h, k))
        return self


class _ActRec:
    def __init__(self, tape):
        self._tape = tape

    def wait_ge(self, h, n):
        self._tape.append(("w", h, n))

    def activation(self, *args, **kw):
        fi = _ActIns()
        self._tape.append(("act", args, kw, fi))
        return fi


class Ctx:
    def __init__(self, nc, es, blk, sems=None, tape=None):
        self.nc, self.es, self.blk = nc, es, blk
        self._sems = {} if sems is None else sems
        self._tape = [] if tape is None else tape

    def sem(self, name):
        if name not in self._sems:
            self._sems[name] = Sem(self.es, self.nc, _uniq(name))
        return self._sems[name]

    def sb(self, name, shape, dt):
        return self.es.enter_context(self.nc.sbuf_tensor(_uniq(name), shape, dt))

    def ps(self, name, shape, dt=F32):
        return self.es.enter_context(self.nc.psum_tensor(_uniq(name), shape, dt))

    def sub(self, es):
        return Ctx(self.nc, es, self.blk, self._sems, self._tape)

    def act(self, fn):
        fn(_ActRec(self._tape))

    def aflush(self):
        if not self._tape:
            return
        items = list(self._tape)
        self._tape.clear()

        @self.blk.scalar
        def _(a):
            for it in items:
                if it[0] == "w":
                    a.wait_ge(it[1], it[2])
                else:
                    ins = a.activation(*it[1], **it[2])
                    for h, k in it[3].incs:
                        ins.then_inc(h, k)


def v3(t, c=CB):
    return t[:].rearrange("p (c n) -> p c n", c=c)


def hview(ap):
    return ap.rearrange("p (h d) -> p h d", h=16)


def phase_guard(ctx, eng):
    for nm in ("t", "v", "a", "sg", "sx", "sx2"):
        ctx.sem(nm).wait(eng)


def load_crow(ctx, H, lo, hi, tag):
    blk = ctx.blk
    crow = ctx.sb("crow_" + tag, [32, hi - lo], BF16)
    lw = ctx.sem("lw")
    gm = ctx.sem("g")

    @blk.gpsimd
    def _(g):
        phase_guard(ctx, g)
        ins = g.memset(crow[:], 0.0)
        gm.inc(ins)
        gm.wait(g)
        lw.dma(g.dma_start(out=crow[0:1, :], in_=H["ap_cpack"][0:1, lo:hi]))
    return crow


def xtile(dram, n):
    return dram.rearrange("(n p) d -> n p d", p=128)[n, :, :]


def ln_rstd(ctx, blk, stt, eps_t):
    vd, ad = ctx.sem("v"), ctx.sem("a")

    @blk.vector
    def _(v):
        ad.wait(v)
        vd.inc(v.tensor_scalar(stt[:, 2:3], stt[:, 0:1], 1.0 / D, None, Op.mult))
        vd.wait(v)
        vd.inc(v.tensor_tensor(stt[:, 3:4], stt[:, 2:3], stt[:, 2:3], Op.mult))
        vd.wait(v)
        vd.inc(v.scalar_tensor_tensor(stt[:, 4:5], stt[:, 1:2], 1.0 / D,
                                      stt[:, 3:4], Op.mult, Op.subtract))

    @ctx.act
    def _(a):
        vd.wait(a)
        ad.inc(a.activation(stt[:, 5:6], stt[:, 4:5], AF.Ln, bias=eps_t[:, 0:1]))
        ad.wait(a)
        ad.inc(a.activation(stt[:, 5:6], stt[:, 5:6], AF.Exp, scale=-0.5))


def top_ln_residual_store(ctx, blk, C, wbuf, sqs, stt, xbuf, dst_tile, wsrc=None,
                          sxs=None):
    vd, ad = ctx.sem("v"), ctx.sem("a")
    sx = sxs if sxs is not None else ctx.sem("sx")
    if wsrc is None:
        wsrc = wbuf

    @blk.vector
    def _(v):
        vd.wait(v)
        vd.inc(v.tensor_scalar(wbuf[:], wsrc[:], 0.0, None, Op.add,
                               Op.add, accum_out=stt[:, 0:1]))

    @ctx.act
    def _(a):
        vd = ctx.sem("v")
        vd.wait(a)
        ad.inc(a.activation(sqs[:], wbuf[:], AF.Square, accum_out=stt[:, 1:2]))

    ln_rstd(ctx, blk, stt, C["eps5"])

    @blk.vector
    def _(v):
        ad.wait(v)
        vd.inc(v.tensor_scalar(wbuf[:], wbuf[:], stt[:, 2:3], stt[:, 5:6],
                               Op.subtract, Op.mult))
        vd.wait(v)
        vd.inc(v.tensor_tensor(xbuf[:], wbuf[:], xbuf[:], Op.add))

    @blk.sync
    def _(s):
        vd.wait(s)
        sx.dma(s.dma_start(out=dst_tile, in_=xbuf[:]))



def emit_l0a(ctx, C, H, meta, src, dst):
    nc, blk = ctx.nc, ctx.blk
    BO = dict(meta["boff"])
    lw, lx, sx = ctx.sem("lw"), ctx.sem("lx"), ctx.sem("sx")
    vd, ad, td = ctx.sem("v"), ctx.sem("a"), ctx.sem("t")
    sg = ctx.sem("sg")
    lo = BO["l0_bqkv"]
    crow = load_crow(ctx, H, lo, BO["f_bin"], "l0a")

    def cr(name, a, b):
        return C and crow[0:1, BO[name] - lo + a:BO[name] - lo + b]

    for b in range(NB):
        with ExitStack() as esb:
            cb1 = ctx.sub(esb)
            qnT = cb1.sb("qnT", [128, CB * SEQ], BF16)
            kv_sb = cb1.sb("kv_sb", [128, 512], BF16)
            mr = cb1.sb("mr_l0a", [128, 2 * TB], F32)

            with ExitStack() as es:
                c2 = cb1.sub(es)
                wqkv = c2.sb("wqkv", [128, CB * 3072], BF16)
                knp = c2.sb("knpA", [128, 2 * D], BF16)
                vp = c2.sb("vpA", [128, 2 * D], BF16)
                kvtmp = c2.sb("kvtmpA", [128, 512], F32)
                kvacc = c2.sb("kvaccA", [128, 512], F32)
                xbuf = c2.sb("xbufA", [128, 4 * D], F32)
                ybuf = c2.sb("ybufA", [128, 2 * D], BF16)
                yT = c2.sb("yTA", [128, 2 * CB * 128], BF16)
                e1q = c2.sb("e1qA", [128, 2 * D], F32)
                e2q = c2.sb("e2qA", [128, 2 * D], F32)
                e1k = c2.sb("e1kA", [128, 2 * D], F32)
                e2k = c2.sb("e2kA", [128, 2 * D], F32)
                qbf = c2.sb("qbfA", [128, 2 * D], BF16)
                st16 = c2.sb("st16A", [128, 160], F32)
                pqQ = c2.ps("pqQ", [128, 1024], F32)
                pqK = c2.ps("pqK", [128, 1024], F32)
                pqV = c2.ps("pqV", [128, 512], F32)
                pkv = c2.ps("pkvA", [128, 512], F32)
                ptr = c2.ps("ptrA", [128, 2048], BF16)

                @blk.gpsimd
                def _(g):
                    phase_guard(ctx, g)
                    for mi, wn in enumerate(["L0_attn_Wq", "L0_attn_Wk", "L0_attn_Wv"]):
                        o = v3(wqkv)[:, :, mi * 1024:(mi + 1) * 1024]
                        i = H[wn].rearrange("(c p) n -> p c n", p=128)
                        lw.dma(g.dma_start(out=o, in_=i))

                marks = {"ptr": 0}
                ldm, ym = {}, {}
                relm = {}
                lqA = [lx, ctx.sem("lx2")]
                NPAIR = TB // 2

                def emit_load(tpx):
                    par = tpx % 2

                    @blk.sync
                    def _(s):
                        if tpx == 0:
                            phase_guard(ctx, s)
                        if tpx >= 2:
                            s.wait_ge(vd.h, ym[tpx - 2])
                        t0 = b * TB + 2 * tpx
                        lqA[par].dma(s.dma_start(
                            out=xbuf[:, par * 2 * D:(par + 1) * 2 * D].rearrange("p (u d) -> p u d", u=2),
                            in_=src.rearrange("(n p) d -> n p d", p=128)[t0:t0 + 2, :, :].transpose([1, 0, 2])))
                        ldm[tpx] = lqA[par].n

                def emit_stats(tpx):
                    par = tpx % 2
                    xh2 = xbuf[:, par * 2 * D:(par + 1) * 2 * D]

                    @blk.vector
                    def _(v):
                        v.wait_ge(lqA[par].h, ldm[tpx])
                        vd.wait(v)
                        if tpx >= 1:
                            ad.wait(v)
                        for u in range(2):
                            sl = st16[:, 48 * u:48 * u + 16]
                            vd.inc(v.bn_stats(sl[:, 0:6], xh2[:, u * D:u * D + 512]))
                            vd.inc(v.bn_stats(sl[:, 6:12], xh2[:, u * D + 512:u * D + 1024]))
                        vd.wait(v)
                        for u in range(2):
                            sl = st16[:, 48 * u:48 * u + 16]
                            vd.inc(v.bn_aggr(sl[:, 12:14], sl[:, 0:12]))

                    @ctx.act
                    def _(a):
                        vd.wait(a)
                        for u in range(2):
                            sl = st16[:, 48 * u:48 * u + 16]
                            ad.inc(a.activation(sl[:, 14:15], sl[:, 13:14], AF.Ln,
                                                bias=C["eps5"][:, 0:1]))
                        ad.wait(a)
                        for u in range(2):
                            sl = st16[:, 48 * u:48 * u + 16]
                            ad.inc(a.activation(sl[:, 14:15], sl[:, 14:15], AF.Exp,
                                                scale=-0.5))

                    @blk.vector
                    def _(v):
                        ad.wait(v)
                        for u in range(2):
                            tt = 2 * tpx + u
                            sl = st16[:, 48 * u:48 * u + 16]
                            vd.inc(v.tensor_copy(mr[:, 2 * tt:2 * tt + 1], sl[:, 12:13]))
                            vd.inc(v.tensor_copy(mr[:, 2 * tt + 1:2 * tt + 2], sl[:, 14:15]))
                            vd.inc(v.tensor_scalar(ybuf[:, u * D:(u + 1) * D],
                                                   xh2[:, u * D:(u + 1) * D],
                                                   sl[:, 12:13], sl[:, 14:15],
                                                   Op.subtract, Op.mult))
                    ym[tpx] = vd.n

                    for u in range(2):
                        @blk.tensor
                        def _(t):
                            lw.wait(t)
                            vd.wait(t, ym[tpx])
                            if u == 0:
                                vd.wait(t, marks["ptr"])
                            for cb in range(CB):
                                ins = t.transpose(ptr[:, 1024 * u + 128 * cb:1024 * u + 128 * (cb + 1)],
                                                  ybuf[:, u * D + 128 * cb:u * D + 128 * (cb + 1)],
                                                  C["ident"][:])
                            td.inc(ins)
                        t_tr = td.n

                        @blk.vector
                        def _(v):
                            td.wait(v, t_tr)
                            vd.inc(v.tensor_copy(yT[:, 1024 * u:1024 * (u + 1)],
                                                 ptr[:, 1024 * u:1024 * (u + 1)]))
                            marks[("yT", tpx, u)] = vd.n

                emit_load(0)
                emit_stats(0)
                if NPAIR > 1:
                    emit_load(1)

                for tp in range(NPAIR):
                    for u in range(2):
                        tt = 2 * tp + u
                        for mi, (pq, mk) in enumerate([(pqQ, "q"), (pqK, "k")]):
                            @blk.tensor
                            def _(t):
                                vd.wait(t, marks[("yT", tp, u)])
                                if (2 * tp + u - 1, mk) in relm:
                                    t.wait_ge(ad.h, relm[(2 * tp + u - 1, mk)])
                                for cb in range(CB):
                                    mv = v3(wqkv)[:, cb, mi * 1024:(mi + 1) * 1024]
                                    lhsT = yT[:, 1024 * u + 128 * cb:1024 * u + 128 * (cb + 1)]
                                    t.matmul(pq[:, 0:512], lhsT, mv[:, 0:512],
                                             start=(cb == 0), stop=False)
                                    t.matmul(pq[:, 512:1024], lhsT, mv[:, 512:1024],
                                             start=(cb == 0), stop=False)
                                t.matmul(pq[:, 0:512], C["onesM"][0:1, :],
                                         cr("l0_bqkv", mi * 1024, mi * 1024 + 512),
                                         start=False, stop=True)
                                ins = t.matmul(pq[:, 512:1024], C["onesM"][0:1, :],
                                               cr("l0_bqkv", mi * 1024 + 512, mi * 1024 + 1024),
                                               start=False, stop=True)
                                td.inc(ins)
                            t_mm = td.n

                            @ctx.act
                            def _(a):
                                a.wait_ge(td.h, t_mm)
                                vd.wait(a)
                                eb1 = (e1q if mi == 0 else e1k)[:, u * D:(u + 1) * D]
                                eb2 = (e2q if mi == 0 else e2k)[:, u * D:(u + 1) * D]
                                ad.inc(a.activation(eb1, pq[:], AF.Relu))
                                ad.inc(a.activation(eb2, pq[:], AF.Relu, scale=-1.0))
                                relm[(tt, mk)] = ad.n
                                ad.wait(a)
                                ad.inc(a.activation(eb2, eb2, AF.Exp, scale=-1.0))

                        for half in range(2):
                            @blk.tensor
                            def _(t):
                                vd.wait(t, marks[("yT", tp, u)])
                                if (tt, "v%d" % (half - 1)) in relm:
                                    t.wait_ge(ad.h, relm[(tt, "v%d" % (half - 1))])
                                elif (2 * tp + u - 1, "v1") in relm:
                                    t.wait_ge(ad.h, relm[(2 * tp + u - 1, "v1")])
                                for cb in range(CB):
                                    mv = v3(wqkv)[:, cb, 2048:3072]
                                    lhsT = yT[:, 1024 * u + 128 * cb:1024 * u + 128 * (cb + 1)]
                                    t.matmul(pqV[:, 0:512], lhsT,
                                             mv[:, 512 * half:512 * (half + 1)],
                                             start=(cb == 0), stop=False)
                                ins = t.matmul(pqV[:, 0:512], C["onesM"][0:1, :],
                                               cr("l0_bqkv", 2048 + 512 * half,
                                                  2048 + 512 * (half + 1)),
                                               start=False, stop=True)
                                td.inc(ins)
                            t_mmv = td.n

                            @ctx.act
                            def _(a):
                                a.wait_ge(td.h, t_mmv)
                                if half == 0 and tp > 0:
                                    a.wait_ge(td.h, relm[("kv", tp - 1)])
                                ad.inc(a.activation(
                                    vp[:, u * D + 512 * half:u * D + 512 * (half + 1)],
                                    pqV[:, 0:512], AF.Identity))
                                relm[(tt, "v%d" % half)] = ad.n

                    if tp + 1 < NPAIR:
                        emit_stats(tp + 1)
                    if tp + 2 < NPAIR:
                        emit_load(tp + 2)

                    @blk.vector
                    def _(v):
                        ad.wait(v)
                        for u in range(2):
                            vd.inc(v.tensor_tensor(e1q[:, u * D:(u + 1) * D],
                                                   e1q[:, u * D:(u + 1) * D],
                                                   e2q[:, u * D:(u + 1) * D], Op.add))
                            vd.inc(v.tensor_tensor(e1k[:, u * D:(u + 1) * D],
                                                   e1k[:, u * D:(u + 1) * D],
                                                   e2k[:, u * D:(u + 1) * D], Op.add))
                        vd.wait(v)
                        for u in range(2):
                            vd.inc(v.tensor_scalar(e2q[:, u * D:(u + 1) * D],
                                                   e1q[:, u * D:(u + 1) * D], 1.0, None, Op.subtract))
                            vd.inc(v.tensor_scalar(e2k[:, u * D:(u + 1) * D],
                                                   e1k[:, u * D:(u + 1) * D], 1.0, None, Op.subtract))
                        vd.wait(v)
                        for u in range(2):
                            vd.inc(v.tensor_tensor(e1q[:, u * D:(u + 1) * D],
                                                   e2q[:, u * D:(u + 1) * D],
                                                   e2q[:, u * D:(u + 1) * D], Op.mult))
                            vd.inc(v.tensor_tensor(e1k[:, u * D:(u + 1) * D],
                                                   e2k[:, u * D:(u + 1) * D],
                                                   e2k[:, u * D:(u + 1) * D], Op.mult))
                        vd.wait(v)
                        for u in range(2):
                            sl = st16[:, 96 + 32 * u:96 + 32 * u + 32]
                            vd.inc(v.tensor_reduce(sl[:, 0:16], hview(e1q[:, u * D:(u + 1) * D]),
                                                   AX.X, Op.add))
                            vd.inc(v.tensor_reduce(sl[:, 16:32], hview(e1k[:, u * D:(u + 1) * D]),
                                                   AX.X, Op.add))

                    @ctx.act
                    def _(a):
                        vd.wait(a)
                        for u in range(2):
                            sl = st16[:, 96 + 32 * u:96 + 32 * u + 32]
                            ad.inc(a.activation(sl[:, 0:16], sl[:, 0:16], AF.Ln))
                            ad.inc(a.activation(sl[:, 16:32], sl[:, 16:32], AF.Ln))
                        ad.wait(a)
                        for u in range(2):
                            sl = st16[:, 96 + 32 * u:96 + 32 * u + 32]
                            ad.inc(a.activation(sl[:, 0:16], sl[:, 0:16],
                                                AF.Exp, scale=-0.5))
                            ad.inc(a.activation(sl[:, 16:32], sl[:, 16:32],
                                                AF.Exp, scale=-0.5))
                        ad.wait(a)
                        for u in range(2):
                            sl = st16[:, 96 + 32 * u:96 + 32 * u + 32]
                            ad.inc(a.activation(sl[:, 0:16], sl[:, 0:16],
                                                AF.Identity, scale=0.125))

                    @blk.vector
                    def _(v):
                        ad.wait(v)
                        for u in range(2):
                            tt = 2 * tp + u
                            sl = st16[:, 96 + 32 * u:96 + 32 * u + 32]
                            rbq = sl[:, 0:16].unsqueeze(2).to_broadcast([128, 16, 64])
                            rbk = sl[:, 16:32].unsqueeze(2).to_broadcast([128, 16, 64])
                            if u == 0 and tp > 0:
                                v.wait_ge(td.h, relm[("kv", tp - 1)])
                            vd.inc(v.tensor_tensor(hview(qbf[:, u * D:(u + 1) * D]),
                                                   hview(e2q[:, u * D:(u + 1) * D]),
                                                   rbq, Op.mult))
                            vd.inc(v.tensor_tensor(hview(knp[:, u * D:(u + 1) * D]),
                                                   hview(e2k[:, u * D:(u + 1) * D]),
                                                   rbk, Op.mult))
                    knm = vd.n

                    for u in range(2):
                        tt = 2 * tp + u

                        @blk.tensor
                        def _(t):
                            vd.wait(t)
                            for cb in range(CB):
                                ins = t.transpose(ptr[:, 1024 * u + 128 * cb:1024 * u + 128 * (cb + 1)],
                                                  qbf[:, u * D + 128 * cb:u * D + 128 * (cb + 1)],
                                                  C["ident"][:])
                            td.inc(ins)
                        t_qn = td.n

                        @blk.vector
                        def _(v):
                            td.wait(v, t_qn)
                            vd.inc(v.tensor_copy(
                                v3(qnT)[:, :, 128 * tt:128 * (tt + 1)],
                                ptr[:, 1024 * u:1024 * (u + 1)].rearrange("p (c n) -> p c n", c=CB)))
                            marks["ptr"] = vd.n

                    @blk.tensor
                    def _(t):
                        t.wait_ge(vd.h, knm)
                        t.wait_ge(ad.h, relm[(2 * tp + 1, "v1")])
                        if tp > 0:
                            t.wait_ge(ad.h, relm[("kve", tp - 1)])
                        for h in range(16):
                            base, col = 64 * (h % 2), 64 * (h // 2)
                            for u in range(2):
                                ins = t.matmul(
                                    pkv[base:base + 64, col:col + 64],
                                    knp[:, u * D + 64 * h:u * D + 64 * h + 64],
                                    vp[:, u * D + 64 * h:u * D + 64 * h + 64],
                                    start=(u == 0), stop=(u == 1),
                                    skip_group_check=True)
                        td.inc(ins)
                        relm[("kv", tp)] = td.n

                    @ctx.act
                    def _(a):
                        a.wait_ge(td.h, relm[("kv", tp)])
                        if tp > 0:
                            vd.wait(a)
                        ad.inc(a.activation(kvtmp[:], pkv[:], AF.Identity))
                        relm[("kve", tp)] = ad.n

                    @blk.vector
                    def _(v):
                        v.wait_ge(ad.h, relm[("kve", tp)])
                        if tp == 0:
                            vd.inc(v.tensor_copy(kvacc[:], kvtmp[:]))
                        else:
                            vd.inc(v.tensor_tensor(kvacc[:], kvacc[:], kvtmp[:], Op.add))

                @blk.vector
                def _(v):
                    vd.wait(v)
                    vd.inc(v.tensor_copy(kv_sb[:], kvacc[:]))
                ctx.aflush()

            with ExitStack() as es:
                c2 = cb1.sub(es)
                wo = c2.sb("wo", [128, CB * 1024], BF16)
                ctxT = c2.sb("ctxT", [128, CB * SEQ], BF16)
                g1r = c2.sb("g1r", [128, D], F32)
                be1r = c2.sb("be1r", [128, D], F32)
                xbuf = c2.sb("xbufB", [128, 4 * D], F32)
                zbuf = c2.sb("zbufB", [128, 2 * D], F32)
                stt = c2.sb("sttB", [128, 32], F32)
                abuf = c2.sb("abufB", [128, 2 * D], BF16)
                po = [c2.ps("poB0", [128, 1024], F32), c2.ps("poB1", [128, 1024], F32)]
                pc = c2.ps("pcB", [128, 2048], F32)

                @blk.gpsimd
                def _(g):
                    phase_guard(ctx, g)
                    lw.dma(g.dma_start(out=v3(wo),
                                       in_=H["L0_attn_Wo"].rearrange("(c p) n -> p c n", p=128)))

                @blk.sync
                def _(s):
                    phase_guard(ctx, s)
                    lx.dma(s.dma_start(out=g1r[:], in_=H["L0_attn_g1"][None, :].to_broadcast([128, D])))
                    lx.dma(s.dma_start(out=be1r[:], in_=H["L0_attn_be1"][None, :].to_broadcast([128, D])))

                for hp in range(CB):
                    @blk.tensor
                    def _(t):
                        ad.wait(t)
                        vd.wait(t)
                        for half in range(2):
                            h = 2 * hp + half
                            base = 64 * (h % 2)
                            lhsT = kv_sb[base:base + 64, 64 * (h // 2):64 * (h // 2) + 64]
                            for qc in range(4):
                                ins = t.matmul(
                                    pc[base:base + 64, 512 * qc:512 * (qc + 1)],
                                    lhsT,
                                    qnT[base:base + 64,
                                        (h // 2) * SEQ + 512 * qc:(h // 2) * SEQ + 512 * (qc + 1)],
                                    start=True, stop=True, skip_group_check=True)
                        td.inc(ins)

                    @ctx.act
                    def _(a):
                        td.wait(a)
                        ad.inc(a.activation(v3(ctxT)[:, hp, :], pc[:], AF.Identity))

                zxm, zm, ldm = {}, {}, {}
                vd_base = vd.n
                lq = [lx, ctx.sem("lx2")]
                for tp in range(TB // 2):
                    q = tp % 2
                    xh2 = xbuf[:, q * 2 * D:(q + 1) * 2 * D]

                    @blk.sync
                    def _(s):
                        if tp == 0:
                            vd.wait(s)
                            sx.wait(s)
                        s.wait_ge(vd.h, zxm.get(tp - 2, vd_base))
                        for u in range(2):
                            lq[q].dma(s.dma_start(out=xh2[:, u * D:(u + 1) * D],
                                                  in_=xtile(src, b * TB + 2 * tp + u)))
                        ldm[tp] = lq[q].n

                    for u in range(2):
                        tt = 2 * tp + u

                        @blk.tensor
                        def _(t):
                            lw.wait(t)
                            if tp == 0 and u == 0:
                                ad.wait(t)
                                vd.wait(t)
                            else:
                                t.wait_ge(vd.h, zm.get(tp - 1, vd_base))
                            for cb in range(CB):
                                lhsT = v3(ctxT)[:, cb, 128 * tt:128 * (tt + 1)]
                                t.matmul(po[u][:, 0:512], lhsT, v3(wo)[:, cb, 0:512],
                                         start=(cb == 0), stop=False)
                                t.matmul(po[u][:, 512:1024], lhsT, v3(wo)[:, cb, 512:1024],
                                         start=(cb == 0), stop=False)
                            t.matmul(po[u][:, 0:512], C["onesM"][0:1, :],
                                     cr("l0_bo", 0, 512), start=False, stop=True)
                            ins = t.matmul(po[u][:, 512:1024], C["onesM"][0:1, :],
                                           cr("l0_bo", 512, 1024), start=False, stop=True)
                            td.inc(ins)
                    omm = td.n

                    @blk.vector
                    def _(v):
                        v.wait_ge(lq[q].h, ldm[tp])
                        vd.wait(v)
                        for u in range(2):
                            tt = 2 * tp + u
                            vd.inc(v.tensor_scalar(zbuf[:, u * D:(u + 1) * D],
                                                   xh2[:, u * D:(u + 1) * D],
                                                   mr[:, 2 * tt:2 * tt + 1],
                                                   mr[:, 2 * tt + 1:2 * tt + 2],
                                                   Op.subtract, Op.mult))
                    zxm[tp] = vd.n

                    @blk.vector
                    def _(v):
                        v.wait_ge(td.h, omm)
                        ad.wait(v)
                        vd.wait(v)
                        for u in range(2):
                            vd.inc(v.scalar_tensor_tensor(zbuf[:, u * D:(u + 1) * D],
                                                          po[u][:], 1.0,
                                                          zbuf[:, u * D:(u + 1) * D],
                                                          Op.mult, Op.add))
                    zm[tp] = vd.n

                    for u in range(2):
                        tt = 2 * tp + u
                        su = stt[:, 16 * u:16 * u + 16]
                        zu = zbuf[:, u * D:(u + 1) * D]

                        @blk.vector
                        def _(v):
                            vd.wait(v)
                            if u == 1:
                                ad.wait(v)
                            vd.inc(v.bn_stats(su[:, 0:6], zu[:, 0:512]))
                            vd.inc(v.bn_stats(su[:, 6:12], zu[:, 512:1024]))
                            vd.wait(v)
                            vd.inc(v.bn_aggr(su[:, 12:14], su[:, 0:12]))

                        @ctx.act
                        def _(a):
                            vd.wait(a)
                            ad.inc(a.activation(su[:, 14:15], su[:, 13:14], AF.Ln,
                                                bias=C["eps12"][:, 0:1]))
                            ad.wait(a)
                            ad.inc(a.activation(su[:, 14:15], su[:, 14:15], AF.Exp,
                                                scale=-0.5))

                        @blk.vector
                        def _(v):
                            ad.wait(v)
                            sg.wait(v)
                            vd.inc(v.tensor_scalar(zu, zu, su[:, 12:13], su[:, 14:15],
                                                   Op.subtract, Op.mult))
                            vd.wait(v)
                            vd.inc(v.tensor_tensor(zu, zu, g1r[:], Op.mult))
                            vd.wait(v)
                            vd.inc(v.tensor_tensor(abuf[:, u * D:(u + 1) * D], zu,
                                                   be1r[:], Op.add))

                        @blk.gpsimd
                        def _(g):
                            vd.wait(g)
                            sg.dma(g.dma_start(
                                out=H["att_st"].rearrange("(n p) d -> n p d", p=128)[b * TB + tt, :, :],
                                in_=abuf[:, u * D:(u + 1) * D]))
                ctx.aflush()
            ctx.aflush()

    if True:
        with ExitStack() as es:
            c2 = ctx.sub(es)
            w1 = c2.sb("w1", [128, CB * 4096], BF16)
            w2 = c2.sb("w2", [128, 32 * 1024], BF16)
            g2r = c2.sb("g2r", [128, D], F32)
            be2r = c2.sb("be2r", [128, D], F32)
            abuf = c2.sb("abufC", [128, 4 * D], BF16)
            aT = c2.sb("aTC", [128, 2 * CB * 256], BF16)
            hT = c2.sb("hTC", [128, 2 * 512], BF16)
            xbuf = c2.sb("xbufC", [128, 4 * D], F32)
            zb2 = c2.sb("zb2C", [128, 2 * D], F32)
            sqs = c2.sb("sqsC", [128, D], F32)
            stt = c2.sb("sttC", [128, 32], F32)
            ptr = c2.ps("ptrC", [128, 2048], BF16)
            ph = [c2.ps("phC0", [128, 512], F32), c2.ps("phC1", [128, 512], F32)]
            po2 = [c2.ps("po2C0", [128, 1024], F32), c2.ps("po2C1", [128, 1024], F32)]

            @blk.gpsimd
            def _(g):
                phase_guard(ctx, g)
                lw.dma(g.dma_start(out=v3(w1),
                                   in_=H["L0_attn_W1"].rearrange("(c p) n -> p c n", p=128)))
                lw.dma(g.dma_start(out=v3(w2, 32),
                                   in_=H["L0_attn_W2"].rearrange("(c p) n -> p c n", p=128)))

            @blk.sync
            def _(s):
                phase_guard(ctx, s)
                lx.dma(s.dma_start(out=g2r[:], in_=H["L0_attn_g2"][None, :].to_broadcast([128, D])))
                lx.dma(s.dma_start(out=be2r[:], in_=H["L0_attn_be2"][None, :].to_broadcast([128, D])))

            for b in range(NB):
                zm, trm, atcpm, sxm, ldm, w2e = {}, {}, {}, {}, {}, {}
                pgel = {}
                w2all = []
                vd_base, td_base = vd.n, td.n
                lq = [lx, ctx.sem("lx2")]
                sxq = [ctx.sem("sx"), ctx.sem("sx2")]
                sxb = [sxq[0].n, sxq[1].n]
                NFG = 16

                def emit_tail(tpx):
                    q_ = tpx % 2
                    xh2_ = xbuf[:, q_ * 2 * D:(q_ + 1) * 2 * D]
                    ah2_ = abuf[:, q_ * 2 * D:(q_ + 1) * 2 * D]

                    @blk.vector
                    def _(v):
                        v.wait_ge(td.h, w2e[tpx])
                        ad.wait(v)
                        vd.wait(v)
                        for u in range(2):
                            vd.inc(v.scalar_tensor_tensor(zb2[:, u * D:(u + 1) * D],
                                                          po2[u][:], 1.0,
                                                          ah2_[:, u * D:(u + 1) * D],
                                                          Op.mult, Op.add))
                    zm[tpx] = vd.n

                    for u in range(2):
                        su = stt[:, 16 * u:16 * u + 16]
                        zu = zb2[:, u * D:(u + 1) * D]
                        for ln_i, eps in ((0, C["eps12"]), (1, C["eps5"])):
                            @blk.vector
                            def _(v):
                                vd.wait(v)
                                if ln_i == 0 and u == 1:
                                    ad.wait(v)
                                vd.inc(v.bn_stats(su[:, 0:6], zu[:, 0:512]))
                                vd.inc(v.bn_stats(su[:, 6:12], zu[:, 512:1024]))
                                vd.wait(v)
                                vd.inc(v.bn_aggr(su[:, 12:14], su[:, 0:12]))

                            @ctx.act
                            def _(a):
                                vd.wait(a)
                                ad.inc(a.activation(su[:, 14:15], su[:, 13:14], AF.Ln,
                                                    bias=eps[:, 0:1]))
                                ad.wait(a)
                                ad.inc(a.activation(su[:, 14:15], su[:, 14:15], AF.Exp,
                                                    scale=-0.5))

                            @blk.vector
                            def _(v):
                                ad.wait(v)
                                vd.inc(v.tensor_scalar(zu, zu, su[:, 12:13], su[:, 14:15],
                                                       Op.subtract, Op.mult))
                                vd.wait(v)
                                if ln_i == 0:
                                    vd.inc(v.tensor_tensor(zu, zu, g2r[:], Op.mult))
                                    vd.wait(v)
                                    vd.inc(v.tensor_tensor(zu, zu, be2r[:], Op.add))
                                else:
                                    vd.inc(v.tensor_tensor(xh2_[:, u * D:(u + 1) * D], zu,
                                                           xh2_[:, u * D:(u + 1) * D],
                                                           Op.add))

                        @blk.sync
                        def _(s):
                            vd.wait(s)
                            sxq[q_].dma(s.dma_start(
                                out=xtile(dst, b * TB + 2 * tpx + u),
                                in_=xh2_[:, u * D:(u + 1) * D]))
                    sxm[tpx] = sxq[q_].n

                for tp in range(TB // 2):
                    q = tp % 2
                    xh2 = xbuf[:, q * 2 * D:(q + 1) * 2 * D]
                    ah2 = abuf[:, q * 2 * D:(q + 1) * 2 * D]
                    aTh = aT[:, q * CB * 256:(q + 1) * CB * 256]
                    ptrh = ptr[:, q * 1024:(q + 1) * 1024]

                    @blk.sync
                    def _(s):
                        sg.wait(s)
                        s.wait_ge(sxq[q].h, sxm.get(tp - 2, sxb[q]))
                        s.wait_ge(td.h, trm.get(tp - 2, td_base))
                        s.wait_ge(vd.h, zm.get(tp - 2, vd_base))
                        for u in range(2):
                            tt = 2 * tp + u
                            lq[q].dma(s.dma_start(
                                out=ah2[:, u * D:(u + 1) * D],
                                in_=H["att_st"].rearrange("(n p) d -> n p d", p=128)[b * TB + tt, :, :]))
                            lq[q].dma(s.dma_start(out=xh2[:, u * D:(u + 1) * D],
                                                  in_=xtile(src, b * TB + tt)))
                        ldm[tp] = lq[q].n

                    @blk.tensor
                    def _(t):
                        lw.wait(t)
                        t.wait_ge(lq[q].h, ldm[tp])
                        t.wait_ge(vd.h, atcpm.get(tp - 2, vd_base))
                        for u in range(2):
                            for cb in range(CB):
                                ins = t.transpose(ptrh[:, 128 * cb:128 * (cb + 1)],
                                                  ah2[:, u * D + 128 * cb:u * D + 128 * (cb + 1)],
                                                  C["ident"][:])
                            if u == 0:
                                td.inc(ins)
                                vd.wait(t, vd.n + 1)
                            else:
                                td.inc(ins)
                    trm[tp] = td.n

                    @blk.vector
                    def _(v):
                        v.wait_ge(td.h, trm[tp] - 1)
                        vd.inc(v.tensor_copy(
                            aTh.rearrange("p (c n) -> p c n", c=CB)[:, :, 0:128],
                            ptrh.rearrange("p (c n) -> p c n", c=CB)))
                        v.wait_ge(td.h, trm[tp])
                        vd.inc(v.tensor_copy(
                            aTh.rearrange("p (c n) -> p c n", c=CB)[:, :, 128:256],
                            ptrh.rearrange("p (c n) -> p c n", c=CB)))
                    atcpm[tp] = vd.n

                    if tp >= 1:
                        emit_tail(tp - 1)

                    gelu_marks = []
                    for fg in range(NFG + 1):
                        if fg < NFG:
                            pbuf = ph[fg % 2]

                            @blk.tensor
                            def _(t):
                                if fg == 0:
                                    t.wait_ge(vd.h, atcpm[tp])
                                    if tp >= 1:
                                        t.wait_ge(ad.h, pgel[tp - 1][-2])
                                if fg == 1:
                                    t.wait_ge(vd.h, zm.get(tp - 1, vd_base))
                                    if tp >= 1:
                                        t.wait_ge(ad.h, pgel[tp - 1][-1])
                                if fg >= 2:
                                    ad.wait(t, gelu_marks[fg - 2])
                                for sub in range(2):
                                    ff = 2 * fg + sub
                                    sl = pbuf[:, 256 * sub:256 * (sub + 1)]
                                    for cb in range(CB):
                                        t.matmul(sl, v3(w1)[:, cb, 128 * ff:128 * (ff + 1)],
                                                 v3(aTh, CB)[:, cb, :],
                                                 start=(cb == 0), stop=False,
                                                 skip_group_check=True)
                                    ins = t.matmul(sl, cr("l0_b1", 128 * ff, 128 * (ff + 1)),
                                                   C["onesN"][0:1, 0:256], start=False,
                                                   stop=True, skip_group_check=True)
                                td.inc(ins)

                            @ctx.act
                            def _(a):
                                td.wait(a)
                                gfg = tp * NFG + fg
                                if gfg >= 2:
                                    a.wait_ge(td.h, w2all[gfg - 2])
                                ad.inc(a.activation(hT[:, 512 * (fg % 2):512 * (fg % 2) + 512],
                                                    pbuf[:], AF.Gelu))
                                gelu_marks.append(ad.n)

                        if fg >= 1:
                            fgw = fg - 1

                            @blk.tensor
                            def _(t):
                                ad.wait(t, gelu_marks[fgw])
                                for u in range(2):
                                    pw = po2[u]
                                    for sub in range(2):
                                        ff = 2 * fgw + sub
                                        hsl = hT[:, 512 * (fgw % 2) + 256 * sub + 128 * u:
                                                 512 * (fgw % 2) + 256 * sub + 128 * (u + 1)]
                                        t.matmul(pw[:, 0:512], hsl,
                                                 v3(w2, 32)[:, ff, 0:512],
                                                 start=(ff == 0), stop=False)
                                        ins = t.matmul(pw[:, 512:1024], hsl,
                                                       v3(w2, 32)[:, ff, 512:1024],
                                                       start=(ff == 0), stop=False)
                                if fgw == NFG - 1:
                                    for u in range(2):
                                        pw = po2[u]
                                        t.matmul(pw[:, 0:512], C["onesM"][0:1, :],
                                                 cr("l0_b2", 0, 512), start=False, stop=True)
                                        ins = t.matmul(pw[:, 512:1024], C["onesM"][0:1, :],
                                                       cr("l0_b2", 512, 1024), start=False, stop=True)
                                td.inc(ins)
                                w2all.append(td.n)
                    pgel[tp] = list(gelu_marks)
                    w2e[tp] = td.n

                emit_tail(TB // 2 - 1)
            ctx.aflush()
    ctx.aflush()



def ln_rstd_v(ctx, blk, var_col, out_col, eps_t):
    vd, ad = ctx.sem("v"), ctx.sem("a")

    @ctx.act
    def _(a):
        vd.wait(a)
        ad.inc(a.activation(out_col, var_col, AF.Ln, bias=eps_t[:, 0:1]))
        ad.wait(a)
        ad.inc(a.activation(out_col, out_col, AF.Exp, scale=-0.5))


def emit_l0f(ctx, C, H, meta, src, dst):
    nc, blk = ctx.nc, ctx.blk
    BO = meta["boff"]
    lw, lx, sx = ctx.sem("lw"), ctx.sem("lx"), ctx.sem("sx")
    vd, ad, td = ctx.sem("v"), ctx.sem("a"), ctx.sem("t")
    lo = BO["f_bin"]
    crow = load_crow(ctx, H, lo, BO["l1_bq"], "l0f")

    def cr(name, a, b):
        return crow[0:1, BO[name] - lo + a:BO[name] - lo + b]

    win = ctx.sb("win", [128, CB * 64], BF16)
    wout = ctx.sb("wout", [64, 1024], BF16)
    b1s = ctx.sb("blk1s", [64, 1024], BF16)
    b2s = ctx.sb("blk2s", [128, CB * 64], BF16)
    xbuf = ctx.sb("xbufF", [128, 2 * D], F32)
    ybuf = ctx.sb("ybufF", [128, 2 * D], BF16)
    yT = ctx.sb("yTF", [128, CB * 256], BF16)
    sbf = ctx.sb("sbfF", [128, 128], BF16)
    sT = ctx.sb("sTF", [64, 256], BF16)
    hmi = ctx.sb("hmiF", [128, 2048], BF16)
    gT = ctx.sb("gTF", [64, 256], BF16)
    zbuf = ctx.sb("zbufF", [128, D], F32)
    sqs = ctx.sb("sqsF", [128, D], F32)
    stt = ctx.sb("sttF", [128, 40], F32)
    ptr = ctx.ps("ptrF", [128, 1024], BF16)
    psA = ctx.ps("psAF", [128, 512], F32)
    ph2 = ctx.ps("ph2F", [128, 1024], F32)
    po = [ctx.ps("poF0", [128, 1024], F32), ctx.ps("poF1", [128, 1024], F32)]

    @blk.gpsimd
    def _(g):
        phase_guard(ctx, g)
        lw.dma(g.dma_start(out=v3(win)[:, :, :],
                           in_=H["L0_ffn_Win"].rearrange("(c p) n -> p c n", p=128)))
        lw.dma(g.dma_start(out=wout[:], in_=H["L0_ffn_Wout"][:, :]))
        lw.dma(g.dma_start(out=b1s[:], in_=H["c_blk1"][:, :]))
        lw.dma(g.dma_start(out=v3(b2s)[:, :, :],
                           in_=H["c_blk2"].rearrange("(c p) n -> p c n", p=128)))

    for tp in range(NB * TB // 2):
        @blk.sync
        def _(s):
            if tp == 0:
                phase_guard(ctx, s)
            vd.wait(s)
            ad.wait(s)
            sx.wait(s)
            for u in range(2):
                lx.dma(s.dma_start(out=xbuf[:, u * D:(u + 1) * D],
                                   in_=xtile(src, 2 * tp + u)))

        @blk.vector
        def _(v):
            lx.wait(v)
            ad.wait(v)
            vd.wait(v)
            for u in range(2):
                x_u = xbuf[:, u * D:(u + 1) * D]
                st_u = stt[:, 16 * u:16 * u + 16]
                vd.inc(v.bn_stats(st_u[:, 0:6], x_u[:, 0:512]))
                vd.inc(v.bn_stats(st_u[:, 6:12], x_u[:, 512:1024]))
            vd.wait(v)
            for u in range(2):
                st_u = stt[:, 16 * u:16 * u + 16]
                vd.inc(v.bn_aggr(st_u[:, 12:14], st_u[:, 0:12]))

        for u in range(2):
            st_u = stt[:, 16 * u:16 * u + 16]
            ln_rstd_v(ctx, blk, st_u[:, 13:14], st_u[:, 14:15], C["eps5"])

        @blk.vector
        def _(v):
            ad.wait(v)
            td.wait(v)
            for u in range(2):
                st_u = stt[:, 16 * u:16 * u + 16]
                vd.inc(v.tensor_scalar(ybuf[:, u * D:(u + 1) * D],
                                       xbuf[:, u * D:(u + 1) * D],
                                       st_u[:, 12:13], st_u[:, 14:15],
                                       Op.subtract, Op.mult))

        @blk.tensor
        def _(t):
            lw.wait(t)
            vd.wait(t)
            for u in range(2):
                for cb in range(CB):
                    ins = t.transpose(ptr[:, 128 * cb:128 * (cb + 1)],
                                      ybuf[:, u * D + 128 * cb:u * D + 128 * (cb + 1)],
                                      C["ident"][:])
                td.inc(ins)
                if u == 0:
                    vd.wait(t, vd.n + 1)

        @blk.vector
        def _(v):
            td.wait(v, td.n - 1)
            vd.inc(v.tensor_copy(
                yT[:].rearrange("p (c n) -> p c n", c=CB)[:, :, 0:128],
                ptr[:].rearrange("p (c n) -> p c n", c=CB)))
            td.wait(v)
            vd.inc(v.tensor_copy(
                yT[:].rearrange("p (c n) -> p c n", c=CB)[:, :, 128:256],
                ptr[:].rearrange("p (c n) -> p c n", c=CB)))

        @blk.tensor
        def _(t):
            vd.wait(t)
            ad.wait(t)
            for u in range(2):
                sl = psA[:, 64 * u:64 * (u + 1)]
                for cb in range(CB):
                    t.matmul(sl, v3(yT)[:, cb, 128 * u:128 * (u + 1)],
                             v3(win)[:, cb, :], start=(cb == 0), stop=False,
                             skip_group_check=True)
                ins = t.matmul(sl, C["onesM"][0:1, :], cr("f_bin", 0, 64),
                               start=False, stop=True, skip_group_check=True)
            td.inc(ins)

        @ctx.act
        def _(a):
            td.wait(a)
            ad.inc(a.activation(sbf[:], psA[:, 0:128], AF.Identity))

        @blk.tensor
        def _(t):
            ad.wait(t)
            for u in range(2):
                ins = t.transpose(ptr[0:64, 128 * u:128 * (u + 1)],
                                  sbf[:, 64 * u:64 * (u + 1)], C["ident"][:])
            td.inc(ins)

        @ctx.act
        def _(a):
            td.wait(a)
            ad.inc(a.activation(sT[:], ptr[0:64, 0:256], AF.Identity))

        for u in range(2):
            @blk.tensor
            def _(t):
                ad.wait(t)
                if u == 1:
                    ad.wait(t, ad.n)
                for slc in range(8):
                    t.matmul(ph2[:, 128 * slc:128 * (slc + 1)],
                             b1s[0:64, 128 * slc:128 * (slc + 1)],
                             sT[0:64, 128 * u:128 * (u + 1)],
                             start=True, stop=False, skip_group_check=True)
                    ins = t.matmul(ph2[:, 128 * slc:128 * (slc + 1)],
                                   cr("f_b1", 128 * slc, 128 * (slc + 1)),
                                   C["onesN"][0:1, 128 * u:128 * (u + 1)],
                                   start=False, stop=True, skip_group_check=True)
                td.inc(ins)

            @ctx.act
            def _(a):
                td.wait(a)
                ad.inc(a.activation(hmi[:, 1024 * u:1024 * (u + 1)], ph2[:], AF.Relu))

        for u in range(2):
            @blk.tensor
            def _(t):
                ad.wait(t)
                for slc in range(8):
                    ins = t.matmul(psA[0:64, 128 + 128 * u:256 + 128 * u],
                                   v3(b2s)[:, slc, :],
                                   hmi[:, 1024 * u + 128 * slc:1024 * u + 128 * (slc + 1)],
                                   start=(slc == 0), stop=False,
                                   skip_group_check=True)
                ins = t.matmul(psA[0:64, 128 + 128 * u:256 + 128 * u],
                               cr("f_b2", 0, 64),
                               C["onesN"][0:1, 128 * u:128 * (u + 1)],
                               start=False, stop=True, skip_group_check=True)
                td.inc(ins)

        @ctx.act
        def _(a):
            td.wait(a)
            ad.inc(a.activation(gT[:], psA[0:64, 128:384], AF.Identity))

        for u in range(2):
            @blk.tensor
            def _(t):
                ad.wait(t)
                if u == 0:
                    vd.wait(t)
                pw = po[u]
                t.matmul(pw[:, 0:512], gT[0:64, 128 * u:128 * (u + 1)],
                         wout[0:64, 0:512], start=True, stop=False)
                t.matmul(pw[:, 512:1024], gT[0:64, 128 * u:128 * (u + 1)],
                         wout[0:64, 512:1024], start=True, stop=False)
                t.matmul(pw[:, 0:512], C["onesM"][0:1, :], cr("f_bout", 0, 512),
                         start=False, stop=True)
                ins = t.matmul(pw[:, 512:1024], C["onesM"][0:1, :], cr("f_bout", 512, 1024),
                               start=False, stop=True)
                td.inc(ins)

        for u in range(2):
            @blk.vector
            def _(v):
                td.wait(v)
                ad.wait(v)

            top_ln_residual_store(ctx, blk, C, zbuf, sqs, stt[:, 32:40],
                                  xbuf[:, u * D:(u + 1) * D],
                                  xtile(dst, 2 * tp + u), wsrc=po[u])
    ctx.aflush()


def emit_l1a(ctx, C, H, meta, src, dst):
    nc, blk = ctx.nc, ctx.blk
    BO = meta["boff"]
    lw, lx, sx = ctx.sem("lw"), ctx.sem("lx"), ctx.sem("sx")
    vd, ad, td = ctx.sem("v"), ctx.sem("a"), ctx.sem("t")
    sg = ctx.sem("sg")
    lo = BO["l1_bq"]
    crow = load_crow(ctx, H, lo, BO["l1f_b"], "l1a")

    def cr(name, a, b):
        return crow[0:1, BO[name] - lo + a:BO[name] - lo + b]

    KW = 2052
    mb_sb = ctx.sb("mb_l1a", [128, 228], F32)
    cb_sb = ctx.sb("cb_l1a", [128, 32], F32)

    with ExitStack() as es:
        c2 = ctx.sub(es)
        wnat = c2.sb("wnat", [128, CB * 1024], BF16)
        wtmp = c2.sb("wtmp", [128, CB * 1024], BF16)
        ptrw = c2.ps("ptrw", [128, 1024], BF16)

        @blk.sync
        def _(s):
            phase_guard(ctx, s)
            lx.dma(s.dma_start(out=mb_sb[:], in_=H["c_maskbias"][:, :]))
            lx.dma(s.dma_start(out=cb_sb[:], in_=H["c_colbias"][:, :]))

        for mi, wn in enumerate(["L1_attn_wq", "L1_attn_wk", "L1_attn_wv", "L1_attn_wo"]):
            @blk.gpsimd
            def _(g):
                if mi == 0:
                    phase_guard(ctx, g)
                vd.wait(g)
                td.wait(g)
                sg.wait(g)
                lw.dma(g.dma_start(out=v3(wnat),
                                   in_=H[wn].rearrange("(c p) n -> p c n", p=128)))

            for fb in range(CB):
                @blk.tensor
                def _(t):
                    lw.wait(t)
                    if fb > 0 or mi > 0:
                        vd.wait(t)
                    for cb in range(CB):
                        ins = t.transpose(ptrw[:, 128 * cb:128 * (cb + 1)],
                                          v3(wnat)[:, fb, 128 * cb:128 * (cb + 1)],
                                          C["ident"][:])
                    td.inc(ins)

                @blk.vector
                def _(v):
                    td.wait(v)
                    vd.inc(v.tensor_copy(
                        v3(wtmp)[:, :, 128 * fb:128 * (fb + 1)],
                        ptrw[:].rearrange("p (c n) -> p c n", c=CB)))

            @blk.gpsimd
            def _(g):
                vd.wait(g)
                sg.dma(g.dma_start(
                    out=H["wtscr"][1024 * mi:1024 * (mi + 1), :].rearrange("(c p) n -> p c n", p=128),
                    in_=v3(wtmp)))
        ctx.aflush()

    zrow = ctx.sb("zrow_l1a", [8, D], BF16)

    @blk.gpsimd
    def _(g):
        gmm = ctx.sem("g")
        ins = g.memset(zrow[:], 0.0)
        gmm.inc(ins)
        gmm.wait(g)
        sg.dma(g.dma_start(out=H["vscr"][0:2, :], in_=zrow[0:2, :]))
        sg.dma(g.dma_start(out=H["vscr"][2050:2056, :], in_=zrow[0:6, :]))

    for b in range(NB):
        with ExitStack() as esb:
            cbx = ctx.sub(esb)
            yT = cbx.sb("yT_l1a", [128, CB * SEQ], BF16)
            kT = cbx.sb("kT_l1a", [128, CB * KW], BF16)
            wq_sb = cbx.sb("wq_sb", [128, CB * 1024], BF16)
            qTc = cbx.sb("qTc", [128, CB * 1024], BF16)
            ysum = cbx.sb("ysum", [128, CB], F32)
            ysbf = cbx.sb("ysbf", [128, CB], BF16)
            mvrow = cbx.sb("mvrow", [1, D], BF16)
            ctx0 = cbx.sb("ctx0", [128, CB], BF16)
            q0blk = cbx.sb("q0blk", [128, CB * 16], BF16)
            xbuf = cbx.sb("xbufL", [128, D], F32)
            sqs = cbx.sb("sqsL", [128, D], F32)
            stt = cbx.sb("sttL", [128, 8], F32)

            @blk.sync
            def _(s):
                phase_guard(ctx, s)
                lx.dma(s.dma_start(out=v3(wq_sb),
                                   in_=H["wtscr"][0:1024, :].rearrange("(c p) n -> p c n", p=128)))

            with ExitStack() as es:
                c2 = cbx.sub(es)
                xb2 = c2.sb("xb2L", [128, 2 * D], F32)
                yb2 = c2.sb("yb2L", [128, 2 * D], BF16)
                st = c2.sb("stP1", [128, 32], F32)
                ptr = c2.ps("ptrL1", [128, 2048], BF16)

                @blk.gpsimd
                def _(g):
                    gmm = ctx.sem("g")
                    td.wait(g); vd.wait(g); ad.wait(g)
                    ins = g.memset(kT[:], 0.0)
                    gmm.inc(ins)

                ldm, stm, rsm, ym, trm, cpm = {}, {}, {}, {}, {}, {}
                lx2 = ctx.sem("lx2")
                lq = [lx, lx2]

                @blk.sync
                def _(s):
                    phase_guard(ctx, s)
                    lq[0].dma(s.dma_start(out=xb2[:, 0:D], in_=xtile(src, b * TB)))
                    ldm[0] = lq[0].n

                for tt in range(TB):
                    p = tt % 2
                    sl = st[:, 16 * p:16 * p + 16]
                    xh = xb2[:, p * D:(p + 1) * D]
                    yh = yb2[:, p * D:(p + 1) * D]

                    if tt + 1 < TB:
                        @blk.sync
                        def _(s):
                            if tt >= 1:
                                s.wait_ge(vd.h, ym[tt - 1])
                            q = lq[(tt + 1) % 2]
                            q.dma(s.dma_start(out=xb2[:, ((tt + 1) % 2) * D:
                                                        (((tt + 1) % 2) + 1) * D],
                                               in_=xtile(src, b * TB + tt + 1)))
                            ldm[tt + 1] = q.n

                    @blk.vector
                    def _(v):
                        v.wait_ge(lq[p].h, ldm[tt])
                        if tt >= 2:
                            v.wait_ge(ad.h, rsm[tt - 2])
                        vd.inc(v.bn_stats(sl[:, 0:6], xh[:, 0:512]))
                        vd.inc(v.bn_stats(sl[:, 6:12], xh[:, 512:1024]))
                        vd.wait(v)
                        vd.inc(v.bn_aggr(sl[:, 12:14], sl[:, 0:12]))
                        stm[tt] = vd.n

                    @ctx.act
                    def _(a):
                        a.wait_ge(vd.h, stm[tt])
                        ad.inc(a.activation(sl[:, 14:15], sl[:, 13:14], AF.Ln,
                                            bias=C["eps5"][:, 0:1]))
                        ad.wait(a)
                        ad.inc(a.activation(sl[:, 14:15], sl[:, 14:15], AF.Exp,
                                            scale=-0.5))
                        rsm[tt] = ad.n

                    @blk.vector
                    def _(v):
                        v.wait_ge(ad.h, rsm[tt])
                        vd.inc(v.tensor_scalar(yh[:], xh[:], sl[:, 12:13],
                                               sl[:, 14:15], Op.subtract, Op.mult))
                        ym[tt] = vd.n

                    @blk.tensor
                    def _(t):
                        t.wait_ge(vd.h, ym[tt])
                        if tt >= 2:
                            t.wait_ge(vd.h, cpm[tt - 2])
                        for cb in range(CB):
                            ins = t.transpose(ptr[:, 1024 * p + 128 * cb:
                                                  1024 * p + 128 * (cb + 1)],
                                              yh[:, 128 * cb:128 * (cb + 1)],
                                              C["ident"][:])
                        td.inc(ins)
                        trm[tt] = td.n

                    @blk.vector
                    def _(v):
                        v.wait_ge(td.h, trm[tt])
                        vd.inc(v.tensor_copy(
                            v3(yT)[:, :, 128 * tt:128 * (tt + 1)],
                            ptr[:, 1024 * p:1024 * (p + 1)].rearrange(
                                "p (c n) -> p c n", c=CB)))
                        cpm[tt] = vd.n

                @blk.vector
                def _(v):
                    vd.wait(v)
                    for cb in range(CB):
                        vd.inc(v.tensor_reduce(ysum[:, cb:cb + 1], v3(yT)[:, cb, :],
                                               AX.X, Op.add))
                    vd.wait(v)
                    vd.inc(v.tensor_scalar(ysbf[:], ysum[:], 1.0 / SEQ, None, Op.mult))
                ctx.aflush()

            with ExitStack() as es:
                c2 = cbx.sub(es)
                v_sb = c2.sb("v_sb_l1a", [128, TB * D], BF16)
                wcur = c2.sb("wcur", [128, CB * 1024], BF16)
                pr0 = c2.sb("pr0", [16, SEQ], BF16)
                gm0 = pr0
                p0T = c2.sb("p0T", [128, 256], BF16)
                s8 = c2.sb("s8_l1a", [16, 8], F32)
                r0b = c2.sb("r0b_l1a", [16, 2 * SEQ], F32)
                pbig = c2.ps("pbigL", [128, 2048], F32)
                ptr = c2.ps("ptrL2", [128, 1024], BF16)
                psc0 = c2.ps("psc0L", [128, 512], F32)

                @blk.sync
                def _(s):
                    phase_guard(ctx, s)
                    lx.dma(s.dma_start(out=v3(wcur),
                                       in_=H["wtscr"][2048:3072, :].rearrange("(c p) n -> p c n", p=128)))
                    lx.dma(s.dma_start(out=r0b[:], in_=H["c_r0bias"][:, :]))

                for tt in range(TB):
                    @blk.tensor
                    def _(t):
                        lw.wait(t)
                        lx.wait(t)
                        vd.wait(t)
                        if tt > 0:
                            ad.wait(t)
                        half = tt % 2
                        pv = pbig[:, 1024 * half:1024 * half + 1024]
                        for cb in range(CB):
                            t.matmul(pv[:, 0:512], v3(yT)[:, cb, 128 * tt:128 * (tt + 1)],
                                     v3(wcur)[:, cb, 0:512], start=(cb == 0), stop=False)
                            t.matmul(pv[:, 512:1024], v3(yT)[:, cb, 128 * tt:128 * (tt + 1)],
                                     v3(wcur)[:, cb, 512:1024], start=(cb == 0), stop=False)
                        t.matmul(pv[:, 0:512], C["onesM"][0:1, :], cr("l1_bv", 0, 512),
                                 start=False, stop=True)
                        ins = t.matmul(pv[:, 512:1024], C["onesM"][0:1, :], cr("l1_bv", 512, 1024),
                                       start=False, stop=True)
                        td.inc(ins)

                    @ctx.act
                    def _(a):
                        td.wait(a)
                        half = tt % 2
                        ad.inc(a.activation(v_sb[:, tt * D:(tt + 1) * D],
                                            pbig[:, 1024 * half:1024 * half + 1024],
                                            AF.Identity))

                @blk.gpsimd
                def _(g):
                    ad.wait(g)
                    sg.dma(g.dma_start(
                        out=H["vscr"][2:2050, :].rearrange("(n p) d -> p n d", p=128),
                        in_=v_sb[:].rearrange("p (n d) -> p n d", n=TB)))

                @blk.tensor
                def _(t):
                    vd.wait(t)
                    ad.wait(t)
                    for cb in range(CB):
                        t.matmul(pbig[0:1, 0:512], ysbf[:, cb:cb + 1],
                                 v3(wcur)[:, cb, 0:512], start=(cb == 0), stop=False)
                        t.matmul(pbig[0:1, 512:1024], ysbf[:, cb:cb + 1],
                                 v3(wcur)[:, cb, 512:1024], start=(cb == 0), stop=False)
                    t.matmul(pbig[0:1, 0:512], C["onesM"][0:1, 0:1], cr("l1_bv", 0, 512),
                             start=False, stop=True)
                    ins = t.matmul(pbig[0:1, 512:1024], C["onesM"][0:1, 0:1], cr("l1_bv", 512, 1024),
                                   start=False, stop=True)
                    td.inc(ins)

                @ctx.act
                def _(a):
                    td.wait(a)
                    ad.inc(a.activation(mvrow[:], pbig[0:1, 0:1024], AF.Identity))

                @blk.sync
                def _(s):
                    td.wait(s); ad.wait(s); vd.wait(s)
                    lx.dma(s.dma_start(out=v3(wcur),
                                       in_=H["wtscr"][1024:2048, :].rearrange("(c p) n -> p c n", p=128)))

                for fo in range(CB):
                    @blk.tensor
                    def _(t):
                        ad.wait(t)
                        lx.wait(t)
                        for cb in range(CB):
                            for qq in range(4):
                                ins = t.matmul(
                                    pbig[:, 512 * qq:512 * (qq + 1)],
                                    v3(wcur)[:, cb, 128 * fo:128 * (fo + 1)],
                                    v3(yT)[:, cb, 512 * qq:512 * (qq + 1)],
                                    start=(cb == 0), stop=False)
                        for qq in range(4):
                            ins = t.matmul(pbig[:, 512 * qq:512 * (qq + 1)],
                                           cr("l1_bk", 128 * fo, 128 * (fo + 1)),
                                           C["onesN"][0:1, 0:512],
                                           start=False, stop=True)
                        td.inc(ins)

                    @ctx.act
                    def _(a):
                        td.wait(a)
                        ctx.sem("g").wait(a)
                        ad.inc(a.activation(kT[:, KW * fo + 2:KW * fo + 2050],
                                            pbig[:], AF.Identity))

                def emit_qT(qc, pqt, use_half=True):
                    for fo in range(CB):
                        @blk.tensor
                        def _(t):
                            ad.wait(t)
                            vd.wait(t)
                            lx.wait(t)
                            half = (fo % 2) if use_half else 0
                            pq = pqt[:, 1024 * half:1024 * half + 1024]
                            for cb in range(CB):
                                t.matmul(pq[:, 0:512],
                                         v3(wq_sb)[:, cb, 128 * fo:128 * (fo + 1)],
                                         v3(yT)[:, cb, 1024 * qc:1024 * qc + 512],
                                         start=(cb == 0), stop=False)
                                t.matmul(pq[:, 512:1024],
                                         v3(wq_sb)[:, cb, 128 * fo:128 * (fo + 1)],
                                         v3(yT)[:, cb, 1024 * qc + 512:1024 * (qc + 1)],
                                         start=(cb == 0), stop=False)
                            t.matmul(pq[:, 0:512], cr("l1_bq", 128 * fo, 128 * (fo + 1)),
                                     C["onesN"][0:1, 0:512], start=False, stop=True)
                            ins = t.matmul(pq[:, 512:1024], cr("l1_bq", 128 * fo, 128 * (fo + 1)),
                                           C["onesN"][0:1, 0:512], start=False, stop=True)
                            td.inc(ins)

                        @ctx.act
                        def _(a):
                            td.wait(a)
                            half = (fo % 2) if use_half else 0
                            ad.inc(a.activation(v3(qTc)[:, fo, :],
                                                pqt[:, 1024 * half:1024 * half + 1024],
                                                AF.Identity, scale=0.125))
                    ctx.aflush()

                cbx.emit_qT = emit_qT
                emit_qT(0, pbig)

                @blk.gpsimd
                def _(g):
                    gmm = ctx.sem("g")
                    vd.wait(g); ad.wait(g)
                    ins = g.memset(q0blk[:], 0.0)
                    gmm.inc(ins)

                @blk.vector
                def _(v):
                    ctx.sem("g").wait(v)
                    ad.wait(v)
                    for cb in range(CB):
                        vd.inc(v.tensor_copy(
                            v3(q0blk, CB)[0:64, cb, 2 * cb:2 * cb + 1],
                            v3(qTc)[0:64, cb, 0:1]))
                        vd.inc(v.tensor_copy(
                            v3(q0blk, CB)[64:128, cb, 2 * cb + 1:2 * cb + 2],
                            v3(qTc)[64:128, cb, 0:1]))

                @blk.tensor
                def _(t):
                    vd.wait(t)
                    ad.wait(t)
                    for cb in range(CB):
                        for qq in range(4):
                            ins = t.matmul(pbig[0:16, 512 * qq:512 * (qq + 1)],
                                           v3(q0blk, CB)[:, cb, :],
                                           kT[:, KW * cb + 2 + 512 * qq:KW * cb + 2 + 512 * (qq + 1)],
                                           start=(cb == 0), stop=(cb == CB - 1))
                    td.inc(ins)

                s0 = pbig[0:16, 0:2048]
                e0 = s0

                @blk.vector
                def _(v):
                    td.wait(v)
                    vd.wait(v)
                    vd.inc(v.tensor_tensor(s0, s0, r0b[0:16, 0:SEQ], Op.add))
                    vd.wait(v)
                    vd.inc(v.tensor_tensor(s0, s0, r0b[0:16, SEQ:2 * SEQ], Op.add))
                    vd.wait(v)
                    vd.inc(v.tensor_scalar(gm0[:], s0, 0.0, None, Op.is_ge))
                    vd.wait(v)
                    vd.inc(v.tensor_reduce(s8[:, 0:1], s0, AX.X, Op.max, negate=True))

                @ctx.act
                def _(a):
                    vd.wait(a)
                    ad.inc(a.activation(e0, s0, AF.Exp, bias=s8[:, 0:1]))

                @blk.vector
                def _(v):
                    ad.wait(v)
                    vd.inc(v.tensor_tensor(e0, e0, gm0[:], Op.mult))
                    vd.wait(v)
                    vd.inc(v.tensor_reduce(s8[:, 1:2], e0, AX.X, Op.add))
                    vd.wait(v)
                    vd.inc(v.tensor_scalar(s8[:, 2:3], s8[:, 0:1], 0.0, None, Op.is_gt))
                    vd.wait(v)
                    vd.inc(v.tensor_tensor(s8[:, 3:4], s8[:, 1:2], s8[:, 2:3], Op.add))
                    vd.wait(v)
                    vd.inc(v.reciprocal(s8[:, 4:5], s8[:, 3:4]))
                    vd.wait(v)
                    vd.inc(v.tensor_scalar(s8[:, 5:6], s8[:, 2:3], -1.0, 1.0, Op.mult, Op.add))
                    vd.wait(v)
                    vd.inc(v.tensor_tensor(s8[:, 6:7], s8[:, 4:5], s8[:, 5:6], Op.mult))
                    vd.wait(v)
                    vd.inc(v.tensor_scalar(s8[:, 7:8], s8[:, 2:3], 1.0 / SEQ, None, Op.mult))
                    vd.wait(v)
                    vd.inc(v.tensor_scalar(pr0[:], e0, s8[:, 6:7], s8[:, 7:8],
                                           Op.mult, Op.add))

                for blki in range(16):
                    @blk.tensor
                    def _(t):
                        vd.wait(t)
                        if blki > 0:
                            pass
                        ins = t.transpose(ptr[:, 16 * blki:16 * (blki + 1)],
                                          pr0[:, 128 * blki:128 * (blki + 1)],
                                          C["ident"][0:16, 0:16])
                        td.inc(ins) if blki == 15 else None

                @blk.vector
                def _(v):
                    td.wait(v)
                    vd.inc(v.tensor_copy(p0T[:], ptr[:, 0:256]))

                @blk.tensor
                def _(t):
                    vd.wait(t)
                    for h in range(16):
                        base = 64 * (h % 2)
                        col = h // 2
                        for blki in range(16):
                            ins = t.matmul(
                                psc0[base:base + 64, col:col + 1],
                                v_sb[:, 1024 * blki + 64 * h:1024 * blki + 64 * h + 64],
                                p0T[:].rearrange("p (w h) -> p w h", w=16)[:, blki, h:h + 1],
                                start=(blki == 0), stop=(blki == 15),
                                skip_group_check=True)
                    td.inc(ins)

                @ctx.act
                def _(a):
                    td.wait(a)
                    ad.inc(a.activation(ctx0[:], psc0[:, 0:8], AF.Identity))
                ctx.aflush()

            with ExitStack() as es:
                c2 = cbx.sub(es)
                vh = c2.sb("vh", [64, 2 * 4096], BF16)
                wov = c2.sb("wov", [128, CB * 1024], BF16)
                pT = c2.sb("pT_l1a", [64, 2 * 2048], BF16)
                s_sb = c2.sb("s_sb", [128, 2 * 608], F32)
                e_sb = s_sb
                gmb = c2.sb("gmb", [128, 2 * 608], BF16)
                prb = c2.sb("prb", [128, 2 * 608], BF16)
                st16 = c2.sb("st16L", [128, 2 * 96], F32)
                cbuf = c2.sb("cbufL", [128, 2 * D], BF16)
                cT = c2.sb("cTL", [128, 2 * CB * 128], BF16)
                zq = c2.sb("zqL", [128, 2 * D], F32)
                xbuf2 = c2.sb("xbuf2L", [128, 2 * D], F32)
                stt2 = c2.sb("stt2L", [128, 32], F32)
                psP = [c2.ps("psP0", [128, 1024], F32), c2.ps("psP1", [128, 1024], F32)]
                ppT = c2.ps("ppTL", [128, 2048], BF16)
                po = c2.ps("poL", [128, 1024], F32)

                @blk.gpsimd
                def _(g):
                    gmm = ctx.sem("g")
                    td.wait(g); vd.wait(g); ad.wait(g); sg.wait(g)
                    g.memset(pT[:], 0.0)
                    ins = g.memset(vh[:], 0.0)
                    gmm.inc(ins)

                @blk.sync
                def _(s):
                    phase_guard(ctx, s)
                    lx.dma(s.dma_start(out=v3(wov),
                                       in_=H["wtscr"][3072:4096, :].rearrange("(c p) n -> p c n", p=128)))

                @blk.vector
                def _(v):
                    sg.wait(v)

                @ctx.act
                def _(a):
                    sg.wait(a)

                HCOL = [38 * h if h < 8 else 512 + 38 * (h - 8) for h in range(16)]

                for pj in range(8):
                    gpair = [2 * pj, 2 * pj + 1]
                    if pj == 4:
                        cbx.emit_qT(1, po, use_half=False)

                    @blk.sync
                    def _(s):
                        vd.wait(s)
                        ad.wait(s)
                        sx.wait(s)
                        sg.wait(s)
                        ctx.sem("g").wait(s)
                        for w, gi in enumerate(gpair):
                            lx.dma(s.dma_start(out=xbuf2[:, w * D:(w + 1) * D],
                                               in_=xtile(src, b * TB + gi)))
                            vs3 = H["vscr"].rearrange("(a p) d -> a p d", p=32)
                            vhw = vh[:, w * 4096:(w + 1) * 4096]
                            lx.dma(s.dma_start(out=vhw[0:32, :].rearrange("p (j d) -> p j d", j=4),
                                               in_=vs3[4 * gi:4 * gi + 4, :, :].transpose([1, 0, 2])))
                            lx.dma(s.dma_start(out=vhw[32:36, :].rearrange("p (j d) -> p j d", j=4),
                                               in_=vs3[4 * gi + 1:4 * gi + 5, 0:4, :].transpose([1, 0, 2])))
                            lx.dma(s.dma_start(out=vhw[36:37, :].rearrange("p (j d) -> p j d", j=4),
                                               in_=H["vscr"][2:3, :].unsqueeze(1).to_broadcast([1, 4, D])))
                            lx.dma(s.dma_start(out=vhw[37:38, :].rearrange("p (j d) -> p j d", j=4),
                                               in_=mvrow[0:1, :].unsqueeze(1).to_broadcast([1, 4, D])))

                    sc_marks = []
                    for w, gi in enumerate(gpair):
                        qc = gi // 8
                        qcol = 128 * (gi - 8 * qc)

                        @blk.tensor
                        def _(t):
                            vd.wait(t)
                            ad.wait(t)
                            for h in range(16):
                                base = 64 * (h % 2)
                                fb = h // 2
                                pw = psP[w]
                                for j in range(4):
                                    a_w = 4 * gi + j
                                    t.matmul(pw[32 * j:32 * (j + 1), HCOL[h]:HCOL[h] + 36],
                                             v3(qTc)[:, fb, qcol + 32 * j:qcol + 32 * (j + 1)][base:base + 64, :],
                                             kT[base:base + 64, KW * fb + 32 * a_w:KW * fb + 32 * a_w + 36],
                                             start=True, stop=True, skip_group_check=True,
                                             tile_position=(base, 32 * j))
                                ins = t.matmul(pw[:, HCOL[h] + 36:HCOL[h] + 38],
                                               v3(qTc)[:, fb, qcol:qcol + 128][base:base + 64, :],
                                               kT[base:base + 64, KW * fb + 2:KW * fb + 4],
                                               start=True, stop=True, skip_group_check=True)
                            td.inc(ins)

                        mvar = 0 if gi == 0 else (2 if gi == 15 else 1)
                        mb1 = mb_sb[:, 38 * mvar:38 * (mvar + 1)]
                        mb1lo = mb_sb[:, 114 + 38 * mvar:114 + 38 * (mvar + 1)]

                        @blk.vector
                        def _(v):
                            td.wait(v)
                            vd.wait(v)
                            so = s_sb[:, 608 * w:608 * (w + 1)].rearrange("p (h x) -> p h x", x=38)
                            vd.inc(v.tensor_tensor(so[:, 0:8, :],
                                                   psP[w][:, 0:304].rearrange("p (h x) -> p h x", x=38),
                                                   mb1.unsqueeze(1).to_broadcast([128, 8, 38]), Op.add))
                            vd.inc(v.tensor_tensor(so[:, 8:16, :],
                                                   psP[w][:, 512:816].rearrange("p (h x) -> p h x", x=38),
                                                   mb1.unsqueeze(1).to_broadcast([128, 8, 38]), Op.add))
                            vd.wait(v)
                            vd.inc(v.tensor_tensor(so, so,
                                                   mb1lo.unsqueeze(1).to_broadcast([128, 16, 38]), Op.add))
                            vd.wait(v)
                            vd.inc(v.tensor_scalar(so[:, :, 36:37], so[:, :, 36:37],
                                                   cb_sb[:, gi:gi + 1], cb_sb[:, 16 + gi:17 + gi],
                                                   Op.add, Op.add))
                            sc_marks.append(vd.n)

                    S3 = [s_sb[:, 608 * w:608 * (w + 1)].rearrange("p (h x) -> p h x", x=38) for w in range(2)]
                    E3 = [e_sb[:, 608 * w:608 * (w + 1)].rearrange("p (h x) -> p h x", x=38) for w in range(2)]
                    P3 = [prb[:, 608 * w:608 * (w + 1)].rearrange("p (h x) -> p h x", x=38) for w in range(2)]
                    SS = [s_sb[:, 608 * w:608 * (w + 1)] for w in range(2)]
                    EE = [e_sb[:, 608 * w:608 * (w + 1)] for w in range(2)]
                    GG = [gmb[:, 608 * w:608 * (w + 1)] for w in range(2)]
                    ST = [st16[:, 96 * w:96 * (w + 1)] for w in range(2)]

                    @blk.vector
                    def _(v):
                        vd.wait(v)
                        for w in range(2):
                            vd.inc(v.tensor_scalar(GG[w][:], SS[w][:], 0.0, None, Op.is_ge))
                        vd.wait(v)
                        for w in range(2):
                            vd.inc(v.tensor_reduce(ST[w][:, 0:16], S3[w][:, :, 0:37], AX.X,
                                                   Op.max, negate=True))
                        vd.wait(v)
                        for w in range(2):
                            vd.inc(v.tensor_tensor(S3[w], S3[w],
                                                   ST[w][:, 0:16].unsqueeze(2).to_broadcast([128, 16, 38]),
                                                   Op.add))

                    @ctx.act
                    def _(a):
                        vd.wait(a)
                        for w in range(2):
                            ad.inc(a.activation(EE[w][:], SS[w][:], AF.Exp))

                    @blk.vector
                    def _(v):
                        ad.wait(v)
                        for w in range(2):
                            vd.inc(v.tensor_tensor(EE[w][:], EE[w][:], GG[w][:], Op.mult))
                        vd.wait(v)
                        for w in range(2):
                            vd.inc(v.tensor_reduce(ST[w][:, 16:32], E3[w][:, :, 0:37], AX.X, Op.add))
                        vd.wait(v)
                        for w in range(2):
                            vd.inc(v.tensor_scalar(ST[w][:, 32:48], ST[w][:, 0:16], 0.0, None, Op.is_gt))
                        vd.wait(v)
                        for w in range(2):
                            vd.inc(v.tensor_tensor(ST[w][:, 48:64], ST[w][:, 16:32], ST[w][:, 32:48], Op.add))
                        vd.wait(v)
                        for w in range(2):
                            vd.inc(v.reciprocal(ST[w][:, 64:80], ST[w][:, 48:64]))
                        vd.wait(v)
                        for w in range(2):
                            vd.inc(v.tensor_scalar(ST[w][:, 80:96], ST[w][:, 32:48], -1.0, 1.0,
                                                   Op.mult, Op.add))
                        vd.wait(v)
                        for w in range(2):
                            vd.inc(v.tensor_tensor(ST[w][:, 64:80], ST[w][:, 64:80], ST[w][:, 80:96], Op.mult))
                        vd.wait(v)
                        for w in range(2):
                            vd.inc(v.tensor_tensor(P3[w][:, :, 0:37], E3[w][:, :, 0:37],
                                                   ST[w][:, 64:80].unsqueeze(2).to_broadcast([128, 16, 37]),
                                                   Op.mult))
                        vd.wait(v)
                        for w in range(2):
                            vd.inc(v.tensor_scalar(P3[w][:, :, 37:38], ST[w][:, 32:48].unsqueeze(2),
                                                   1.0, None, Op.mult))

                    pow_ = [po, psP[0]]
                    evm, ctm, cpm2, opm = {}, {}, {}, {}

                    for w, gi in enumerate(gpair):
                        @blk.tensor
                        def _(t):
                            vd.wait(t)
                            if w == 0:
                                ad.wait(t)
                            else:
                                t.wait_ge(ad.h, evm[0])
                            for h in range(16):
                                ins = t.transpose(ppT[0:38, 128 * h:128 * (h + 1)],
                                                  prb[:, 608 * w + 38 * h:608 * w + 38 * (h + 1)],
                                                  C["ident"][:])
                            td.inc(ins)

                        @ctx.act
                        def _(a):
                            td.wait(a)
                            ctx.sem("g").wait(a)
                            ad.inc(a.activation(pT[0:38, 2048 * w:2048 * (w + 1)],
                                                ppT[0:38, :], AF.Identity))
                            evm[w] = ad.n

                    for w, gi in enumerate(gpair):
                        @blk.tensor
                        def _(t):
                            t.wait_ge(ad.h, evm[w])
                            lx.wait(t)
                            vd.wait(t)
                            pw = psP[w]
                            vhw = vh[:, w * 4096:(w + 1) * 4096]
                            for h in range(16):
                                for j in range(4):
                                    ins = t.matmul(pw[32 * j:32 * (j + 1), 64 * h:64 * (h + 1)],
                                                   pT[0:64, 2048 * w + 128 * h + 32 * j:2048 * w + 128 * h + 32 * (j + 1)],
                                                   vhw[0:64, 1024 * j + 64 * h:1024 * j + 64 * (h + 1)],
                                                   start=True, stop=True, skip_group_check=True,
                                                   tile_position=(0, 32 * j))
                            td.inc(ins)

                        @ctx.act
                        def _(a):
                            td.wait(a)
                            ad.inc(a.activation(cbuf[:, w * D:(w + 1) * D], psP[w][:, 0:1024],
                                                AF.Identity))
                            ctm[w] = ad.n

                    for w, gi in enumerate(gpair):
                        @blk.tensor
                        def _(t):
                            t.wait_ge(ad.h, ctm[w])
                            if w == 0:
                                vd.wait(t)
                            else:
                                t.wait_ge(vd.h, cpm2[0])
                            for cb in range(CB):
                                ins = t.transpose(ppT[:, 128 * cb:128 * (cb + 1)],
                                                  cbuf[:, w * D + 128 * cb:w * D + 128 * (cb + 1)],
                                                  C["ident"][:])
                            td.inc(ins)

                        @blk.vector
                        def _(v):
                            td.wait(v)
                            vd.inc(v.tensor_copy(cT[:, 1024 * w:1024 * (w + 1)], ppT[:, 0:1024]))
                            if gi == 0:
                                vd.wait(v)
                                vd.inc(v.tensor_copy(
                                    cT[:, 0:1024].rearrange("p (c n) -> p c n", c=CB)[:, :, 0:1],
                                    ctx0[:].unsqueeze(2)))
                            cpm2[w] = vd.n

                    for w, gi in enumerate(gpair):
                        @blk.tensor
                        def _(t):
                            t.wait_ge(vd.h, cpm2[w])
                            lx.wait(t)
                            if w == 1:
                                vd.wait(t)
                            dst_ps = pow_[w]
                            for cb in range(CB):
                                t.matmul(dst_ps[:, 0:512], cT[:, 1024 * w + 128 * cb:1024 * w + 128 * (cb + 1)],
                                         v3(wov)[:, cb, 0:512], start=(cb == 0), stop=False)
                                t.matmul(dst_ps[:, 512:1024], cT[:, 1024 * w + 128 * cb:1024 * w + 128 * (cb + 1)],
                                         v3(wov)[:, cb, 512:1024], start=(cb == 0), stop=False)
                            t.matmul(dst_ps[:, 0:512], C["onesM"][0:1, :], cr("l1_bo", 0, 512),
                                     start=False, stop=True)
                            ins = t.matmul(dst_ps[:, 512:1024], C["onesM"][0:1, :], cr("l1_bo", 512, 1024),
                                           start=False, stop=True)
                            td.inc(ins)
                            opm[w] = td.n

                    @blk.vector
                    def _(v):
                        v.wait_ge(td.h, opm[1])
                        ad.wait(v)
                        vd.wait(v)
                        for w in range(2):
                            vd.inc(v.tensor_copy(zq[:, w * D:(w + 1) * D], pow_[w][:, 0:1024]))

                    for w, gi in enumerate(gpair):
                        su2 = stt2[:, 16 * w:16 * w + 16]
                        zu2 = zq[:, w * D:(w + 1) * D]

                        @blk.vector
                        def _(v):
                            vd.wait(v)
                            if w == 1:
                                ad.wait(v)
                            vd.inc(v.bn_stats(su2[:, 0:6], zu2[:, 0:512]))
                            vd.inc(v.bn_stats(su2[:, 6:12], zu2[:, 512:1024]))
                            vd.wait(v)
                            vd.inc(v.bn_aggr(su2[:, 12:14], su2[:, 0:12]))

                        @ctx.act
                        def _(a):
                            vd.wait(a)
                            ad.inc(a.activation(su2[:, 14:15], su2[:, 13:14], AF.Ln,
                                                bias=C["eps5"][:, 0:1]))
                            ad.wait(a)
                            ad.inc(a.activation(su2[:, 14:15], su2[:, 14:15], AF.Exp,
                                                scale=-0.5))

                        @blk.vector
                        def _(v):
                            ad.wait(v)
                            vd.inc(v.tensor_scalar(zu2, zu2, su2[:, 12:13], su2[:, 14:15],
                                                   Op.subtract, Op.mult))
                            vd.wait(v)
                            vd.inc(v.tensor_tensor(xbuf2[:, w * D:(w + 1) * D], zu2,
                                                   xbuf2[:, w * D:(w + 1) * D], Op.add))

                        @blk.sync
                        def _(s):
                            vd.wait(s)
                            sx.dma(s.dma_start(out=xtile(dst, b * TB + gi),
                                               in_=xbuf2[:, w * D:(w + 1) * D]))
                ctx.aflush()
            ctx.aflush()
    ctx.aflush()


def emit_l1f(ctx, C, H, meta, src, dst):
    nc, blk = ctx.nc, ctx.blk
    BO = meta["boff"]
    lw, lx, sx = ctx.sem("lw"), ctx.sem("lx"), ctx.sem("sx")
    vd, ad, td = ctx.sem("v"), ctx.sem("a"), ctx.sem("t")
    lf = BO["l1f_b"]

    pk = ctx.sb("pk", [128, 2 * 1024], I32)
    tmp32 = ctx.sb("tmp32", [128, 1024], I32)
    tbuf = ctx.sb("ternrow", [128, 1024], BF16)
    ternT = ctx.sb("ternT", [128, CB * 1024], BF16)
    scr = ctx.sb("scr_l1f", [128, 1], F32)
    brep = ctx.sb("brepF2", [128, D], F32)
    xbuf = ctx.sb("xbufG", [128, 4 * D], F32)
    ybuf = ctx.sb("ybufG", [128, 4 * D], F32)
    ubuf = ctx.sb("ubufG", [128, 4 * D], F32)
    u0buf = ctx.sb("u0bufG", [128, 4 * D], F32)
    gbuf = ctx.sb("gbufG", [128, 4 * D], BF16)
    nbuf = ctx.sb("nbufG", [128, 4 * D], BF16)
    nT = ctx.sb("nTG", [128, D], BF16)
    oq = ctx.sb("oqG", [128, D], F32)
    zbuf = ctx.sb("zbufG", [128, 4 * D], F32)
    sqs = ctx.sb("sqsG", [128, D], F32)
    stt = ctx.sb("sttG", [128, 40], F32)
    stt2 = ctx.sb("stt2G", [128, 16], F32)
    ptr = ctx.ps("ptrG", [128, 1024], BF16)
    po = ctx.ps("poG", [128, 1024], F32)

    @blk.gpsimd
    def _(g):
        phase_guard(ctx, g)
        lw.dma(g.dma_start(
            out=pk[:].rearrange("p (c n) -> p c n", c=2),
            in_=H["L1_ffn_packed_weight"].rearrange("(c p n) -> p c n", p=128, n=1024)))

    @blk.sync
    def _(s):
        phase_guard(ctx, s)
        lx.dma(s.dma_start(out=scr[:], in_=H["L1_ffn_weight_scale"][None, :].to_broadcast([128, 1])))
        lx.dma(s.dma_start(out=brep[:], in_=H["ap_cpack"][0:1, lf:lf + D].to_broadcast([128, D])))

    for t8 in range(8):
        sh, half = 2 * (t8 // 2), t8 % 2

        @blk.vector
        def _(v):
            lw.wait(v)
            td.wait(v)
            vd.wait(v)
            vd.inc(v.tensor_scalar(tmp32[:], pk[:, 1024 * half:1024 * (half + 1)],
                                   sh, 3, Op.arith_shift_right, Op.bitwise_and))
            vd.wait(v)
            vd.inc(v.tensor_scalar(tbuf[:], tmp32[:], 1.0, None, Op.subtract))

        @blk.tensor
        def _(t):
            vd.wait(t)
            for cb in range(CB):
                ins = t.transpose(ptr[:, 128 * cb:128 * (cb + 1)],
                                  tbuf[:, 128 * cb:128 * (cb + 1)], C["ident"][:])
            td.inc(ins)

        @blk.vector
        def _(v):
            td.wait(v)
            vd.inc(v.tensor_copy(v3(ternT)[:, :, 128 * t8:128 * (t8 + 1)],
                                 ptr[:].rearrange("p (c n) -> p c n", c=CB)))

    PAIR = 4
    for tp in range(NB * TB // PAIR):
        @blk.sync
        def _(s):
            if tp == 0:
                phase_guard(ctx, s)
            vd.wait(s)
            ad.wait(s)
            sx.wait(s)
            for u in range(PAIR):
                lx.dma(s.dma_start(out=xbuf[:, u * D:(u + 1) * D],
                                   in_=xtile(src, PAIR * tp + u)))

        @blk.vector
        def _(v):
            lx.wait(v)
            ad.wait(v)
            vd.wait(v)
            for u in range(PAIR):
                vd.inc(v.tensor_scalar(zbuf[:, u * D:(u + 1) * D],
                                       xbuf[:, u * D:(u + 1) * D], 0.0, None,
                                       Op.add, Op.add,
                                       accum_out=stt[:, 8 * u:8 * u + 1]))

        @ctx.act
        def _(a):
            vd.wait(a)
            for u in range(PAIR):
                if u:
                    ad.wait(a)
                ad.inc(a.activation(sqs[:], xbuf[:, u * D:(u + 1) * D], AF.Square,
                                    accum_out=stt[:, 8 * u + 1:8 * u + 2]))

        for u in range(PAIR):
            ln_rstd(ctx, blk, stt[:, 8 * u:8 * u + 8], C["eps5"])

        @blk.vector
        def _(v):
            ad.wait(v)
            S = [stt[:, 8 * u_:8 * u_ + 8] for u_ in range(PAIR)]
            Y = [ybuf[:, u_ * D:(u_ + 1) * D] for u_ in range(PAIR)]
            U0 = [u0buf[:, u_ * D:(u_ + 1) * D] for u_ in range(PAIR)]
            UB = [ubuf[:, u_ * D:(u_ + 1) * D] for u_ in range(PAIR)]
            GB = [gbuf[:, u_ * D:(u_ + 1) * D] for u_ in range(PAIR)]
            NBF = [nbuf[:, u_ * D:(u_ + 1) * D] for u_ in range(PAIR)]
            for u in range(PAIR):
                vd.inc(v.tensor_scalar(Y[u], xbuf[:, u * D:(u + 1) * D],
                                       S[u][:, 2:3], S[u][:, 5:6],
                                       Op.subtract, Op.mult))
            vd.wait(v)
            for u in range(PAIR):
                vd.inc(v.tensor_reduce(S[u][:, 6:7], Y[u], AX.X, Op.max,
                                       apply_absolute_value=True))
            vd.wait(v)
            for u in range(PAIR):
                vd.inc(v.tensor_scalar(S[u][:, 7:8], S[u][:, 6:7], 1e-5, None, Op.max))
            vd.wait(v)
            for u in range(PAIR):
                vd.inc(v.reciprocal(stt2[:, 4 * u:4 * u + 1], S[u][:, 7:8]))
            vd.wait(v)
            for u in range(PAIR):
                vd.inc(v.tensor_scalar(stt2[:, 4 * u + 1:4 * u + 2],
                                       stt2[:, 4 * u:4 * u + 1], 127.0, None, Op.mult))
            vd.wait(v)
            for u in range(PAIR):
                vd.inc(v.tensor_scalar(stt2[:, 4 * u + 2:4 * u + 3], S[u][:, 7:8],
                                       scr[:, 0:1], -1.0 / 127.0, Op.mult, Op.mult))
            vd.wait(v)
            for u in range(PAIR):
                vd.inc(v.tensor_scalar(U0[u], Y[u], stt2[:, 4 * u + 1:4 * u + 2],
                                       128.5, Op.mult, Op.add))
            usrc = u0buf
            for bit in [128, 64, 32, 16, 8, 4, 2, 1]:
                vd.wait(v)
                vd.inc(v.tensor_scalar(gbuf[:], usrc[:], float(bit), float(bit),
                                       Op.is_ge, Op.mult))
                vd.wait(v)
                vd.inc(v.scalar_tensor_tensor(ubuf[:], gbuf[:], -1.0, usrc[:],
                                              Op.mult, Op.add))
                usrc = ubuf
            vd.wait(v)
            vd.inc(v.scalar_tensor_tensor(nbuf[:], u0buf[:], -1.0, ubuf[:],
                                          Op.mult, Op.add))
            vd.wait(v)
            vd.inc(v.tensor_scalar(nbuf[:], nbuf[:], 128.0, None, Op.add))

        for u in range(PAIR):
            @blk.tensor
            def _(t):
                vd.wait(t)
                ad.wait(t)
                for cb in range(CB):
                    ins = t.transpose(ptr[:, 128 * cb:128 * (cb + 1)],
                                      nbuf[:, u * D + 128 * cb:u * D + 128 * (cb + 1)],
                                      C["ident"][:])
                td.inc(ins)

            @ctx.act
            def _(a):
                td.wait(a)
                ad.inc(a.activation(nT[:], ptr[:], AF.Identity))

            @blk.tensor
            def _(t):
                ad.wait(t)
                ins = None
                for cb in range(CB):
                    t.matmul(po[:, 0:512], nT[:, 128 * cb:128 * (cb + 1)],
                             v3(ternT)[:, cb, 0:512],
                             start=(cb == 0), stop=(cb == CB - 1))
                    ins = t.matmul(po[:, 512:1024], nT[:, 128 * cb:128 * (cb + 1)],
                                   v3(ternT)[:, cb, 512:1024],
                                   start=(cb == 0), stop=(cb == CB - 1))
                td.inc(ins)

            @ctx.act
            def _(a):
                td.wait(a)
                vd.wait(a)
                ad.inc(a.activation(oq[:], po[:], AF.Identity,
                                    scale=stt2[:, 4 * u + 2:4 * u + 3]))

            @blk.vector
            def _(v):
                ad.wait(v)
                vd.wait(v)
                vd.inc(v.scalar_tensor_tensor(zbuf[:, 0:D], oq[:], 1.0, brep[:],
                                              Op.mult, Op.add,
                                              accum_out=stt[:, 32:33]))

            @ctx.act
            def _(a):
                vd.wait(a)
                ad.inc(a.activation(sqs[:], zbuf[:, 0:D], AF.Square,
                                    accum_out=stt[:, 33:34]))

            ln_rstd(ctx, blk, stt[:, 32:40], C["eps5"])

            @blk.vector
            def _(v):
                ad.wait(v)
                vd.inc(v.tensor_scalar(zbuf[:, 0:D], zbuf[:, 0:D],
                                       stt[:, 34:35], stt[:, 37:38],
                                       Op.subtract, Op.mult))
                vd.wait(v)
                vd.inc(v.tensor_tensor(xbuf[:, u * D:(u + 1) * D], zbuf[:, 0:D],
                                       xbuf[:, u * D:(u + 1) * D], Op.add))

            @blk.sync
            def _(s):
                vd.wait(s)
                sx.dma(s.dma_start(out=xtile(dst, PAIR * tp + u),
                                   in_=xbuf[:, u * D:(u + 1) * D]))
    ctx.aflush()



def emit_common_consts(ctx, H, meta):
    nc, blk = ctx.nc, ctx.blk
    C = {}
    C["ident"] = ctx.sb("identC", [128, 128], BF16)
    C["onesM"] = ctx.sb("onesMC", [32, 128], BF16)
    C["onesN"] = ctx.sb("onesNC", [32, 512], BF16)
    C["eps5"] = ctx.sb("eps5C", [128, 1], F32)
    C["eps12"] = ctx.sb("eps12C", [128, 1], F32)
    C["neg1"] = ctx.sb("neg1C", [128, 1], F32)
    C["ln8"] = ctx.sb("ln8C", [128, 1], F32)
    lc = ctx.sem("lc")
    gm = ctx.sem("g")
    vd = ctx.sem("v")

    @blk.gpsimd
    def _(g):
        g.memset(C["onesM"][:], 0.0)
        g.memset(C["eps5"][:], 1e-5)
        g.memset(C["eps12"][:], 1e-12)
        g.memset(C["neg1"][:], -1.0)
        g.memset(C["ln8"][:], LN_8INV)
        ins = g.memset(C["onesN"][:], 0.0)
        gm.inc(ins)

    @blk.sync
    def _(s):
        lc.dma(s.dma_start(out=C["ident"][:], in_=H["ap_ident"][:, :]))

    @blk.vector
    def _(v):
        gm.wait(v)
        lc.wait(v)
        vd.inc(v.tensor_scalar(C["onesM"][0:1, :], C["onesM"][0:1, :], 1.0, None, Op.add))
        vd.inc(v.tensor_scalar(C["onesN"][0:1, :], C["onesN"][0:1, :], 1.0, None, Op.add))
    return C


STAGE_FN = {"l0a": emit_l0a, "l0f": emit_l0f, "l1a": emit_l1a, "l1f": emit_l1f}


def build_stage(stages, meta):
    nc = bass.Bass("TRN2", target_bir_lowering=False)
    H = {}
    H["x_in"] = nc.dram_tensor("x_in", [T, D], F32, kind="ExternalInput").ap()
    H["y_out"] = nc.dram_tensor("y_out", [T, D], F32, kind="ExternalOutput").ap()
    H["xs"] = nc.dram_tensor("xs", [T, D], F32, kind="Internal").ap()
    H["ap_ident"] = nc.dram_tensor("c_ident", [128, 128], BF16, kind="ExternalInput").ap()
    H["ap_cpack"] = nc.dram_tensor("c_cpack", [1, meta["cw"]], F32, kind="ExternalInput").ap()
    for nm, shape, dt in [("L0_attn_Wq", [D, D], F32), ("L0_attn_Wk", [D, D], F32),
                          ("L0_attn_Wv", [D, D], F32), ("L0_attn_Wo", [D, D], F32),
                          ("L0_attn_W1", [D, 4096], F32), ("L0_attn_W2", [4096, D], F32),
                          ("L0_attn_g1", [D], F32), ("L0_attn_be1", [D], F32),
                          ("L0_attn_g2", [D], F32), ("L0_attn_be2", [D], F32),
                          ("L0_ffn_Win", [D, 64], F32), ("L0_ffn_Wout", [64, D], F32),
                          ("c_blk1", [64, 1024], F32), ("c_blk2", [1024, 64], F32),
                          ("L1_attn_wq", [D, D], F32), ("L1_attn_wk", [D, D], F32),
                          ("L1_attn_wv", [D, D], F32), ("L1_attn_wo", [D, D], F32),
                          ("c_maskbias", [128, 228], F32), ("c_colbias", [128, 32], F32),
                          ("c_r0bias", [16, 2 * SEQ], F32),
                          ("L1_ffn_packed_weight", [262144], I32),
                          ("L1_ffn_weight_scale", [1], F32)]:
        H[nm] = nc.dram_tensor(nm, shape, dt, kind="ExternalInput").ap()
    H["att_st"] = nc.dram_tensor("att_st", [T, D], BF16, kind="Internal").ap()
    H["wtscr"] = nc.dram_tensor("wtscr", [4 * D, D], BF16, kind="Internal").ap()
    H["vscr"] = nc.dram_tensor("vscr", [SEQ + 64, D], BF16, kind="Internal").ap()

    es = ExitStack()
    with es:
        blk = es.enter_context(nc.Block())
        ctx = Ctx(nc, es, blk)
        C = emit_common_consts(ctx, H, meta)
        n = len(stages)
        for i, st in enumerate(stages):
            src = H["x_in"] if i == 0 else H["xs"]
            dst = H["y_out"] if i == n - 1 else H["xs"]
            with ExitStack() as ses:
                STAGE_FN[st](ctx.sub(ses), C, H, meta, src, dst)
        sx = ctx.sem("sx")
        sx2 = ctx.sem("sx2")
        sg = ctx.sem("sg")

        @blk.sync
        def _(s):
            sx.wait(s)
            sx2.wait(s)
            sg.wait(s)
    return nc


WNAMES = ["L0_attn_Wq", "L0_attn_Wk", "L0_attn_Wv", "L0_attn_Wo",
          "L0_attn_W1", "L0_attn_W2", "L0_attn_g1", "L0_attn_be1",
          "L0_attn_g2", "L0_attn_be2", "L0_ffn_Win", "L0_ffn_Wout",
          "L1_attn_wq", "L1_attn_wk", "L1_attn_wv", "L1_attn_wo",
          "L1_ffn_packed_weight", "L1_ffn_weight_scale"]


def prep(inputs):
    consts = host_constants()
    lay = host_layout(inputs)
    meta = {"cw": lay["cpack"].shape[1], "boff": lay["_boff"]}
    x = np.ascontiguousarray(np.asarray(inputs["L0_attn_x"], dtype=np.float32))
    shards = [np.ascontiguousarray(x[2 * c:2 * c + 2].reshape(T, D))
              for c in range(NCORES)]
    base_map = {
        "c_ident": consts["ident"],
        "c_cpack": np.ascontiguousarray(lay["cpack"], dtype=np.float32),
        "c_maskbias": consts["maskbias"], "c_colbias": consts["colbias"],
        "c_r0bias": consts["r0bias"],
        "c_blk1": lay["blk1"], "c_blk2": lay["blk2"],
    }
    for nm in WNAMES:
        base_map[nm] = np.ascontiguousarray(np.asarray(inputs[nm]))
    return meta, shards, base_map


def run_stages(stage_groups, meta, shards, base_map):
    cur = shards
    for sg in stage_groups:
        nc = build_stage(sg, meta)
        in_maps = [dict(base_map, x_in=cur[c]) for c in range(NCORES)]
        res = run_bass_kernel_spmd(nc, in_maps, core_ids=list(range(NCORES)))
        cur = [np.asarray(res.results[c]["y_out"]) for c in range(NCORES)]
    return cur


def kernel(**inputs):
    meta, shards, base_map = prep(inputs)
    cur = run_stages([["l0a", "l0f", "l1a", "l1f"]], meta, shards, base_map)
    out = np.stack(cur, 0).reshape(16, SEQ, D)
    return np.ascontiguousarray(out.astype(np.float32))


# version 19 (final)
# speedup vs baseline: 1.1941x; 1.1941x over previous
import numpy as np
import ml_dtypes
from contextlib import ExitStack

import concourse.bass as bass
import concourse.mybir as mybir
from concourse.bass_utils import run_bass_kernel_spmd
from concourse.alu_op_type import AluOpType as Op

F32 = mybir.dt.float32
BF16 = mybir.dt.bfloat16
I32 = mybir.dt.int32
AF = mybir.ActivationFunctionType
AX = mybir.AxisListType

NCORES = 8
D = 1024
CB = 8
SEQ = 2048
NB = 2
T = SEQ * NB
TB = SEQ // 128
NEG = -1.0e9
LN_8INV = float(np.log(0.125))
SQ2INV = float(1.0 / np.sqrt(2.0))



def _pos_bias(d):
    d = np.asarray(d, dtype=np.float64)
    return np.exp(-0.5 * (d / 2.0) ** 2) + np.cos(2.0 * np.pi * d / 401.0)


def _hilo(a64):
    hi = a64.astype(np.float32)
    lo = (a64 - hi.astype(np.float64)).astype(np.float32)
    return hi, lo


def host_constants():
    c = {}
    c["ident"] = np.eye(128, dtype=np.float32).astype(ml_dtypes.bfloat16)
    mbs = []
    for variant in range(3):
        mb = np.full((128, 38), NEG, dtype=np.float64)
        for p in range(128):
            j, pp = p // 32, p % 32
            for f in range(36):
                dist = pp + 2 - f
                if abs(dist) <= 2:
                    mb[p, f] = _pos_bias(dist)
            mb[p, 36] = 0.0
            mb[p, 37] = NEG
            if variant == 0 and j == 0:
                mb[p, 0] = NEG
                mb[p, 1] = NEG
                if pp <= 2:
                    mb[p, 36] = NEG
            if variant == 2 and j == 3:
                mb[p, 34] = NEG
                mb[p, 35] = NEG
        mbs.append(mb)
    mb64 = np.concatenate(mbs, axis=1)
    hi, lo = _hilo(mb64)
    c["maskbias"] = np.ascontiguousarray(np.concatenate([hi, lo], axis=1))
    i = (np.arange(128)[:, None] + 128 * np.arange(16)[None, :])
    hi, lo = _hilo(_pos_bias(i))
    c["colbias"] = np.ascontiguousarray(np.concatenate([hi, lo], axis=1))
    r064 = np.broadcast_to(_pos_bias(-np.arange(SEQ)), (16, SEQ))
    hi, lo = _hilo(r064)
    c["r0bias"] = np.ascontiguousarray(np.concatenate([hi, lo], axis=1))
    return c


def host_layout(inputs):
    rows, off = [], {}

    def add(name, arr):
        off[name] = sum(a.size for a in rows)
        rows.append(np.ascontiguousarray(arr, dtype=np.float32).reshape(-1))

    add("l0_bqkv", np.concatenate([np.asarray(inputs["L0_attn_bq"]),
                                   np.asarray(inputs["L0_attn_bk"]),
                                   np.asarray(inputs["L0_attn_bv"])]))
    add("l0_bo", inputs["L0_attn_bo"])
    add("l0_b1", inputs["L0_attn_b1"])
    add("l0_b2", inputs["L0_attn_b2"])
    add("f_bin", inputs["L0_ffn_bin_"])
    add("f_b1", np.asarray(inputs["L0_ffn_b1"]).reshape(-1))
    add("f_b2", inputs["L0_ffn_b2"])
    add("f_bout", inputs["L0_ffn_bout"])
    add("l1_bq", inputs["L1_attn_bq"])
    add("l1_bk", inputs["L1_attn_bk"])
    add("l1_bv", inputs["L1_attn_bv"])
    add("l1_bo", inputs["L1_attn_bo"])
    add("l1f_b", inputs["L1_ffn_bias"])
    d = {"cpack": np.concatenate(rows)[None, :], "_boff": off}
    W1 = np.asarray(inputs["L0_ffn_W1"], dtype=np.float32)
    W2 = np.asarray(inputs["L0_ffn_W2"], dtype=np.float32)
    blk1 = np.zeros((64, 1024), dtype=np.float32)
    blk2 = np.zeros((1024, 64), dtype=np.float32)
    m = np.arange(64)
    for i in range(16):
        blk1[m, 16 * m + i] = W1[:, i]
        blk2[16 * m + i, m] = W2[:, i]
    d["blk1"], d["blk2"] = blk1, blk2
    return d



class Sem:
    def __init__(self, es, nc, name):
        self.h = es.enter_context(nc.semaphore(name))
        self.n = 0

    def inc(self, instr, k=1):
        instr.then_inc(self.h, k)
        self.n += k
        return instr

    def dma(self, instr):
        return self.inc(instr, 16)

    def wait(self, eng, n=None):
        eng.wait_ge(self.h, self.n if n is None else n)


_NAME_CNT = [0]


def _uniq(name):
    _NAME_CNT[0] += 1
    return f"{name}_{_NAME_CNT[0]}"


class _ActIns:
    def __init__(self):
        self.incs = []

    def then_inc(self, h, k):
        self.incs.append((h, k))
        return self


class _ActRec:
    def __init__(self, tape):
        self._tape = tape

    def wait_ge(self, h, n):
        self._tape.append(("w", h, n))

    def activation(self, *args, **kw):
        fi = _ActIns()
        self._tape.append(("act", args, kw, fi))
        return fi


class Ctx:
    def __init__(self, nc, es, blk, sems=None, tape=None):
        self.nc, self.es, self.blk = nc, es, blk
        self._sems = {} if sems is None else sems
        self._tape = [] if tape is None else tape

    def sem(self, name):
        if name not in self._sems:
            self._sems[name] = Sem(self.es, self.nc, _uniq(name))
        return self._sems[name]

    def sb(self, name, shape, dt):
        return self.es.enter_context(self.nc.sbuf_tensor(_uniq(name), shape, dt))

    def ps(self, name, shape, dt=F32):
        return self.es.enter_context(self.nc.psum_tensor(_uniq(name), shape, dt))

    def sub(self, es):
        return Ctx(self.nc, es, self.blk, self._sems, self._tape)

    def act(self, fn):
        fn(_ActRec(self._tape))

    def aflush(self):
        if not self._tape:
            return
        items = list(self._tape)
        self._tape.clear()

        @self.blk.scalar
        def _(a):
            for it in items:
                if it[0] == "w":
                    a.wait_ge(it[1], it[2])
                else:
                    ins = a.activation(*it[1], **it[2])
                    for h, k in it[3].incs:
                        ins.then_inc(h, k)


def v3(t, c=CB):
    return t[:].rearrange("p (c n) -> p c n", c=c)


def hview(ap):
    return ap.rearrange("p (h d) -> p h d", h=16)


def phase_guard(ctx, eng):
    for nm in ("t", "v", "a", "sg", "sx", "sx2"):
        ctx.sem(nm).wait(eng)


def load_crow(ctx, H, lo, hi, tag):
    blk = ctx.blk
    crow = ctx.sb("crow_" + tag, [32, hi - lo], BF16)
    lw = ctx.sem("lw")
    gm = ctx.sem("g")

    @blk.gpsimd
    def _(g):
        phase_guard(ctx, g)
        ins = g.memset(crow[:], 0.0)
        gm.inc(ins)
        gm.wait(g)
        lw.dma(g.dma_start(out=crow[0:1, :], in_=H["ap_cpack"][0:1, lo:hi]))
    return crow


def xtile(dram, n):
    return dram.rearrange("(n p) d -> n p d", p=128)[n, :, :]


def ln_rstd(ctx, blk, stt, eps_t):
    vd, ad = ctx.sem("v"), ctx.sem("a")

    @blk.vector
    def _(v):
        ad.wait(v)
        vd.inc(v.tensor_scalar(stt[:, 2:3], stt[:, 0:1], 1.0 / D, None, Op.mult))
        vd.wait(v)
        vd.inc(v.tensor_tensor(stt[:, 3:4], stt[:, 2:3], stt[:, 2:3], Op.mult))
        vd.wait(v)
        vd.inc(v.scalar_tensor_tensor(stt[:, 4:5], stt[:, 1:2], 1.0 / D,
                                      stt[:, 3:4], Op.mult, Op.subtract))

    @ctx.act
    def _(a):
        vd.wait(a)
        ad.inc(a.activation(stt[:, 5:6], stt[:, 4:5], AF.Ln, bias=eps_t[:, 0:1]))
        ad.wait(a)
        ad.inc(a.activation(stt[:, 5:6], stt[:, 5:6], AF.Exp, scale=-0.5))


def top_ln_residual_store(ctx, blk, C, wbuf, sqs, stt, xbuf, dst_tile, wsrc=None,
                          sxs=None):
    vd, ad = ctx.sem("v"), ctx.sem("a")
    sx = sxs if sxs is not None else ctx.sem("sx")
    if wsrc is None:
        wsrc = wbuf

    @blk.vector
    def _(v):
        vd.wait(v)
        vd.inc(v.tensor_scalar(wbuf[:], wsrc[:], 0.0, None, Op.add,
                               Op.add, accum_out=stt[:, 0:1]))

    @ctx.act
    def _(a):
        vd = ctx.sem("v")
        vd.wait(a)
        ad.inc(a.activation(sqs[:], wbuf[:], AF.Square, accum_out=stt[:, 1:2]))

    ln_rstd(ctx, blk, stt, C["eps5"])

    @blk.vector
    def _(v):
        ad.wait(v)
        vd.inc(v.tensor_scalar(wbuf[:], wbuf[:], stt[:, 2:3], stt[:, 5:6],
                               Op.subtract, Op.mult))
        vd.wait(v)
        vd.inc(v.tensor_tensor(xbuf[:], wbuf[:], xbuf[:], Op.add))

    @blk.sync
    def _(s):
        vd.wait(s)
        sx.dma(s.dma_start(out=dst_tile, in_=xbuf[:]))



def emit_l0a(ctx, C, H, meta, src, dst):
    nc, blk = ctx.nc, ctx.blk
    BO = dict(meta["boff"])
    lw, lx, sx = ctx.sem("lw"), ctx.sem("lx"), ctx.sem("sx")
    vd, ad, td = ctx.sem("v"), ctx.sem("a"), ctx.sem("t")
    sg = ctx.sem("sg")
    lo = BO["l0_bqkv"]
    crow = load_crow(ctx, H, lo, BO["f_bin"], "l0a")

    def cr(name, a, b):
        return C and crow[0:1, BO[name] - lo + a:BO[name] - lo + b]

    for b in range(NB):
        with ExitStack() as esb:
            cb1 = ctx.sub(esb)
            qnT = cb1.sb("qnT", [128, CB * SEQ], BF16)
            kv_sb = cb1.sb("kv_sb", [128, 512], BF16)
            mr = cb1.sb("mr_l0a", [128, 2 * TB], F32)

            with ExitStack() as es:
                c2 = cb1.sub(es)
                wqkv = c2.sb("wqkv", [128, CB * 3072], BF16)
                knp = c2.sb("knpA", [128, 2 * D], BF16)
                vp = c2.sb("vpA", [128, 2 * D], BF16)
                kvtmp = c2.sb("kvtmpA", [128, 512], F32)
                kvacc = c2.sb("kvaccA", [128, 512], F32)
                xbuf = c2.sb("xbufA", [128, 4 * D], F32)
                ybuf = c2.sb("ybufA", [128, 2 * D], BF16)
                yT = c2.sb("yTA", [128, 2 * CB * 128], BF16)
                e1q = c2.sb("e1qA", [128, 2 * D], F32)
                e2q = c2.sb("e2qA", [128, 2 * D], F32)
                e1k = c2.sb("e1kA", [128, 2 * D], F32)
                e2k = c2.sb("e2kA", [128, 2 * D], F32)
                qbf = c2.sb("qbfA", [128, 2 * D], BF16)
                st16 = c2.sb("st16A", [128, 160], F32)
                pqQ = c2.ps("pqQ", [128, 1024], F32)
                pqK = c2.ps("pqK", [128, 1024], F32)
                pqV = c2.ps("pqV", [128, 512], F32)
                pkv = c2.ps("pkvA", [128, 512], F32)
                ptr = c2.ps("ptrA", [128, 2048], BF16)

                @blk.gpsimd
                def _(g):
                    phase_guard(ctx, g)
                    for mi, wn in enumerate(["L0_attn_Wq", "L0_attn_Wk", "L0_attn_Wv"]):
                        o = v3(wqkv)[:, :, mi * 1024:(mi + 1) * 1024]
                        i = H[wn].rearrange("(c p) n -> p c n", p=128)
                        lw.dma(g.dma_start(out=o, in_=i))

                marks = {"ptr": 0}
                ldm, ym = {}, {}
                relm = {}
                lqA = [lx, ctx.sem("lx2")]
                NPAIR = TB // 2

                def emit_load(tpx):
                    par = tpx % 2

                    @blk.sync
                    def _(s):
                        if tpx == 0:
                            phase_guard(ctx, s)
                        if tpx >= 2:
                            s.wait_ge(vd.h, ym[tpx - 2])
                        t0 = b * TB + 2 * tpx
                        lqA[par].dma(s.dma_start(
                            out=xbuf[:, par * 2 * D:(par + 1) * 2 * D].rearrange("p (u d) -> p u d", u=2),
                            in_=src.rearrange("(n p) d -> n p d", p=128)[t0:t0 + 2, :, :].transpose([1, 0, 2])))
                        ldm[tpx] = lqA[par].n

                def emit_stats(tpx):
                    par = tpx % 2
                    xh2 = xbuf[:, par * 2 * D:(par + 1) * 2 * D]

                    @blk.vector
                    def _(v):
                        v.wait_ge(lqA[par].h, ldm[tpx])
                        vd.wait(v)
                        if tpx >= 1:
                            ad.wait(v)
                        for u in range(2):
                            sl = st16[:, 48 * u:48 * u + 16]
                            vd.inc(v.bn_stats(sl[:, 0:6], xh2[:, u * D:u * D + 512]))
                            vd.inc(v.bn_stats(sl[:, 6:12], xh2[:, u * D + 512:u * D + 1024]))
                        vd.wait(v)
                        for u in range(2):
                            sl = st16[:, 48 * u:48 * u + 16]
                            vd.inc(v.bn_aggr(sl[:, 12:14], sl[:, 0:12]))

                    @ctx.act
                    def _(a):
                        vd.wait(a)
                        for u in range(2):
                            sl = st16[:, 48 * u:48 * u + 16]
                            ad.inc(a.activation(sl[:, 14:15], sl[:, 13:14], AF.Ln,
                                                bias=C["eps5"][:, 0:1]))
                        ad.wait(a)
                        for u in range(2):
                            sl = st16[:, 48 * u:48 * u + 16]
                            ad.inc(a.activation(sl[:, 14:15], sl[:, 14:15], AF.Exp,
                                                scale=-0.5))

                    @blk.vector
                    def _(v):
                        ad.wait(v)
                        for u in range(2):
                            tt = 2 * tpx + u
                            sl = st16[:, 48 * u:48 * u + 16]
                            vd.inc(v.tensor_copy(mr[:, 2 * tt:2 * tt + 1], sl[:, 12:13]))
                            vd.inc(v.tensor_copy(mr[:, 2 * tt + 1:2 * tt + 2], sl[:, 14:15]))
                            vd.inc(v.tensor_scalar(ybuf[:, u * D:(u + 1) * D],
                                                   xh2[:, u * D:(u + 1) * D],
                                                   sl[:, 12:13], sl[:, 14:15],
                                                   Op.subtract, Op.mult))
                    ym[tpx] = vd.n

                    for u in range(2):
                        @blk.tensor
                        def _(t):
                            lw.wait(t)
                            vd.wait(t, ym[tpx])
                            if u == 0:
                                vd.wait(t, marks["ptr"])
                            for cb in range(CB):
                                ins = t.transpose(ptr[:, 1024 * u + 128 * cb:1024 * u + 128 * (cb + 1)],
                                                  ybuf[:, u * D + 128 * cb:u * D + 128 * (cb + 1)],
                                                  C["ident"][:])
                            td.inc(ins)
                        t_tr = td.n

                        @blk.vector
                        def _(v):
                            td.wait(v, t_tr)
                            vd.inc(v.tensor_copy(yT[:, 1024 * u:1024 * (u + 1)],
                                                 ptr[:, 1024 * u:1024 * (u + 1)]))
                            marks[("yT", tpx, u)] = vd.n

                emit_load(0)
                emit_stats(0)
                if NPAIR > 1:
                    emit_load(1)

                for tp in range(NPAIR):
                    for u in range(2):
                        tt = 2 * tp + u
                        for mi, (pq, mk) in enumerate([(pqQ, "q"), (pqK, "k")]):
                            @blk.tensor
                            def _(t):
                                vd.wait(t, marks[("yT", tp, u)])
                                if (2 * tp + u - 1, mk) in relm:
                                    t.wait_ge(ad.h, relm[(2 * tp + u - 1, mk)])
                                for cb in range(CB):
                                    mv = v3(wqkv)[:, cb, mi * 1024:(mi + 1) * 1024]
                                    lhsT = yT[:, 1024 * u + 128 * cb:1024 * u + 128 * (cb + 1)]
                                    t.matmul(pq[:, 0:512], lhsT, mv[:, 0:512],
                                             start=(cb == 0), stop=False)
                                    t.matmul(pq[:, 512:1024], lhsT, mv[:, 512:1024],
                                             start=(cb == 0), stop=False)
                                t.matmul(pq[:, 0:512], C["onesM"][0:1, :],
                                         cr("l0_bqkv", mi * 1024, mi * 1024 + 512),
                                         start=False, stop=True)
                                ins = t.matmul(pq[:, 512:1024], C["onesM"][0:1, :],
                                               cr("l0_bqkv", mi * 1024 + 512, mi * 1024 + 1024),
                                               start=False, stop=True)
                                td.inc(ins)
                            t_mm = td.n

                            @ctx.act
                            def _(a):
                                a.wait_ge(td.h, t_mm)
                                vd.wait(a)
                                eb1 = (e1q if mi == 0 else e1k)[:, u * D:(u + 1) * D]
                                eb2 = (e2q if mi == 0 else e2k)[:, u * D:(u + 1) * D]
                                ad.inc(a.activation(eb1, pq[:], AF.Relu))
                                ad.inc(a.activation(eb2, pq[:], AF.Relu, scale=-1.0))
                                relm[(tt, mk)] = ad.n
                                ad.wait(a)
                                ad.inc(a.activation(eb2, eb2, AF.Exp, scale=-1.0))

                        for half in range(2):
                            @blk.tensor
                            def _(t):
                                vd.wait(t, marks[("yT", tp, u)])
                                if (tt, "v%d" % (half - 1)) in relm:
                                    t.wait_ge(ad.h, relm[(tt, "v%d" % (half - 1))])
                                elif (2 * tp + u - 1, "v1") in relm:
                                    t.wait_ge(ad.h, relm[(2 * tp + u - 1, "v1")])
                                for cb in range(CB):
                                    mv = v3(wqkv)[:, cb, 2048:3072]
                                    lhsT = yT[:, 1024 * u + 128 * cb:1024 * u + 128 * (cb + 1)]
                                    t.matmul(pqV[:, 0:512], lhsT,
                                             mv[:, 512 * half:512 * (half + 1)],
                                             start=(cb == 0), stop=False)
                                ins = t.matmul(pqV[:, 0:512], C["onesM"][0:1, :],
                                               cr("l0_bqkv", 2048 + 512 * half,
                                                  2048 + 512 * (half + 1)),
                                               start=False, stop=True)
                                td.inc(ins)
                            t_mmv = td.n

                            @ctx.act
                            def _(a):
                                a.wait_ge(td.h, t_mmv)
                                if half == 0 and tp > 0:
                                    a.wait_ge(td.h, relm[("kv", tp - 1)])
                                ad.inc(a.activation(
                                    vp[:, u * D + 512 * half:u * D + 512 * (half + 1)],
                                    pqV[:, 0:512], AF.Identity))
                                relm[(tt, "v%d" % half)] = ad.n

                    if tp + 1 < NPAIR:
                        emit_stats(tp + 1)
                    if tp + 2 < NPAIR:
                        emit_load(tp + 2)

                    @blk.vector
                    def _(v):
                        ad.wait(v)
                        for u in range(2):
                            vd.inc(v.tensor_tensor(e1q[:, u * D:(u + 1) * D],
                                                   e1q[:, u * D:(u + 1) * D],
                                                   e2q[:, u * D:(u + 1) * D], Op.add))
                            vd.inc(v.tensor_tensor(e1k[:, u * D:(u + 1) * D],
                                                   e1k[:, u * D:(u + 1) * D],
                                                   e2k[:, u * D:(u + 1) * D], Op.add))
                        vd.wait(v)
                        for u in range(2):
                            vd.inc(v.tensor_scalar(e2q[:, u * D:(u + 1) * D],
                                                   e1q[:, u * D:(u + 1) * D], 1.0, None, Op.subtract))
                            vd.inc(v.tensor_scalar(e2k[:, u * D:(u + 1) * D],
                                                   e1k[:, u * D:(u + 1) * D], 1.0, None, Op.subtract))
                        vd.wait(v)
                        for u in range(2):
                            vd.inc(v.tensor_tensor(e1q[:, u * D:(u + 1) * D],
                                                   e2q[:, u * D:(u + 1) * D],
                                                   e2q[:, u * D:(u + 1) * D], Op.mult))
                            vd.inc(v.tensor_tensor(e1k[:, u * D:(u + 1) * D],
                                                   e2k[:, u * D:(u + 1) * D],
                                                   e2k[:, u * D:(u + 1) * D], Op.mult))
                        vd.wait(v)
                        for u in range(2):
                            sl = st16[:, 96 + 32 * u:96 + 32 * u + 32]
                            vd.inc(v.tensor_reduce(sl[:, 0:16], hview(e1q[:, u * D:(u + 1) * D]),
                                                   AX.X, Op.add))
                            vd.inc(v.tensor_reduce(sl[:, 16:32], hview(e1k[:, u * D:(u + 1) * D]),
                                                   AX.X, Op.add))

                    @ctx.act
                    def _(a):
                        vd.wait(a)
                        for u in range(2):
                            sl = st16[:, 96 + 32 * u:96 + 32 * u + 32]
                            ad.inc(a.activation(sl[:, 0:16], sl[:, 0:16], AF.Ln))
                            ad.inc(a.activation(sl[:, 16:32], sl[:, 16:32], AF.Ln))
                        ad.wait(a)
                        for u in range(2):
                            sl = st16[:, 96 + 32 * u:96 + 32 * u + 32]
                            ad.inc(a.activation(sl[:, 0:16], sl[:, 0:16],
                                                AF.Exp, scale=-0.5))
                            ad.inc(a.activation(sl[:, 16:32], sl[:, 16:32],
                                                AF.Exp, scale=-0.5))
                        ad.wait(a)
                        for u in range(2):
                            sl = st16[:, 96 + 32 * u:96 + 32 * u + 32]
                            ad.inc(a.activation(sl[:, 0:16], sl[:, 0:16],
                                                AF.Identity, scale=0.125))

                    @blk.vector
                    def _(v):
                        ad.wait(v)
                        for u in range(2):
                            tt = 2 * tp + u
                            sl = st16[:, 96 + 32 * u:96 + 32 * u + 32]
                            rbq = sl[:, 0:16].unsqueeze(2).to_broadcast([128, 16, 64])
                            rbk = sl[:, 16:32].unsqueeze(2).to_broadcast([128, 16, 64])
                            if u == 0 and tp > 0:
                                v.wait_ge(td.h, relm[("kv", tp - 1)])
                            vd.inc(v.tensor_tensor(hview(qbf[:, u * D:(u + 1) * D]),
                                                   hview(e2q[:, u * D:(u + 1) * D]),
                                                   rbq, Op.mult))
                            vd.inc(v.tensor_tensor(hview(knp[:, u * D:(u + 1) * D]),
                                                   hview(e2k[:, u * D:(u + 1) * D]),
                                                   rbk, Op.mult))
                    knm = vd.n

                    for u in range(2):
                        tt = 2 * tp + u

                        @blk.tensor
                        def _(t):
                            vd.wait(t)
                            for cb in range(CB):
                                ins = t.transpose(ptr[:, 1024 * u + 128 * cb:1024 * u + 128 * (cb + 1)],
                                                  qbf[:, u * D + 128 * cb:u * D + 128 * (cb + 1)],
                                                  C["ident"][:])
                            td.inc(ins)
                        t_qn = td.n

                        @blk.vector
                        def _(v):
                            td.wait(v, t_qn)
                            vd.inc(v.tensor_copy(
                                v3(qnT)[:, :, 128 * tt:128 * (tt + 1)],
                                ptr[:, 1024 * u:1024 * (u + 1)].rearrange("p (c n) -> p c n", c=CB)))
                            marks["ptr"] = vd.n

                    @blk.tensor
                    def _(t):
                        t.wait_ge(vd.h, knm)
                        t.wait_ge(ad.h, relm[(2 * tp + 1, "v1")])
                        if tp > 0:
                            t.wait_ge(ad.h, relm[("kve", tp - 1)])
                        for h in range(16):
                            base, col = 64 * (h % 2), 64 * (h // 2)
                            for u in range(2):
                                ins = t.matmul(
                                    pkv[base:base + 64, col:col + 64],
                                    knp[:, u * D + 64 * h:u * D + 64 * h + 64],
                                    vp[:, u * D + 64 * h:u * D + 64 * h + 64],
                                    start=(u == 0), stop=(u == 1),
                                    skip_group_check=True)
                        td.inc(ins)
                        relm[("kv", tp)] = td.n

                    @ctx.act
                    def _(a):
                        a.wait_ge(td.h, relm[("kv", tp)])
                        if tp > 0:
                            vd.wait(a)
                        ad.inc(a.activation(kvtmp[:], pkv[:], AF.Identity))
                        relm[("kve", tp)] = ad.n

                    @blk.vector
                    def _(v):
                        v.wait_ge(ad.h, relm[("kve", tp)])
                        if tp == 0:
                            vd.inc(v.tensor_copy(kvacc[:], kvtmp[:]))
                        else:
                            vd.inc(v.tensor_tensor(kvacc[:], kvacc[:], kvtmp[:], Op.add))

                @blk.vector
                def _(v):
                    vd.wait(v)
                    vd.inc(v.tensor_copy(kv_sb[:], kvacc[:]))
                ctx.aflush()

            with ExitStack() as es:
                c2 = cb1.sub(es)
                wo = c2.sb("wo", [128, CB * 1024], BF16)
                ctxT = c2.sb("ctxT", [128, CB * SEQ], BF16)
                g1r = c2.sb("g1r", [128, D], F32)
                be1r = c2.sb("be1r", [128, D], F32)
                xbuf = c2.sb("xbufB", [128, 4 * D], F32)
                zbuf = c2.sb("zbufB", [128, 2 * D], F32)
                stt = c2.sb("sttB", [128, 32], F32)
                abuf = c2.sb("abufB", [128, 2 * D], BF16)
                po = [c2.ps("poB0", [128, 1024], F32), c2.ps("poB1", [128, 1024], F32)]
                pc = c2.ps("pcB", [128, 2048], F32)

                @blk.gpsimd
                def _(g):
                    phase_guard(ctx, g)
                    lw.dma(g.dma_start(out=v3(wo),
                                       in_=H["L0_attn_Wo"].rearrange("(c p) n -> p c n", p=128)))

                @blk.sync
                def _(s):
                    phase_guard(ctx, s)
                    lx.dma(s.dma_start(out=g1r[:], in_=H["L0_attn_g1"][None, :].to_broadcast([128, D])))
                    lx.dma(s.dma_start(out=be1r[:], in_=H["L0_attn_be1"][None, :].to_broadcast([128, D])))

                for hp in range(CB):
                    @blk.tensor
                    def _(t):
                        ad.wait(t)
                        vd.wait(t)
                        for half in range(2):
                            h = 2 * hp + half
                            base = 64 * (h % 2)
                            lhsT = kv_sb[base:base + 64, 64 * (h // 2):64 * (h // 2) + 64]
                            for qc in range(4):
                                ins = t.matmul(
                                    pc[base:base + 64, 512 * qc:512 * (qc + 1)],
                                    lhsT,
                                    qnT[base:base + 64,
                                        (h // 2) * SEQ + 512 * qc:(h // 2) * SEQ + 512 * (qc + 1)],
                                    start=True, stop=True, skip_group_check=True)
                        td.inc(ins)

                    @ctx.act
                    def _(a):
                        td.wait(a)
                        ad.inc(a.activation(v3(ctxT)[:, hp, :], pc[:], AF.Identity))

                zxm, zm, ldm = {}, {}, {}
                vd_base = vd.n
                lq = [lx, ctx.sem("lx2")]
                for tp in range(TB // 2):
                    q = tp % 2
                    xh2 = xbuf[:, q * 2 * D:(q + 1) * 2 * D]

                    @blk.sync
                    def _(s):
                        if tp == 0:
                            vd.wait(s)
                            sx.wait(s)
                        s.wait_ge(vd.h, zxm.get(tp - 2, vd_base))
                        for u in range(2):
                            lq[q].dma(s.dma_start(out=xh2[:, u * D:(u + 1) * D],
                                                  in_=xtile(src, b * TB + 2 * tp + u)))
                        ldm[tp] = lq[q].n

                    for u in range(2):
                        tt = 2 * tp + u

                        @blk.tensor
                        def _(t):
                            lw.wait(t)
                            if tp == 0 and u == 0:
                                ad.wait(t)
                                vd.wait(t)
                            else:
                                t.wait_ge(vd.h, zm.get(tp - 1, vd_base))
                            for cb in range(CB):
                                lhsT = v3(ctxT)[:, cb, 128 * tt:128 * (tt + 1)]
                                t.matmul(po[u][:, 0:512], lhsT, v3(wo)[:, cb, 0:512],
                                         start=(cb == 0), stop=False)
                                t.matmul(po[u][:, 512:1024], lhsT, v3(wo)[:, cb, 512:1024],
                                         start=(cb == 0), stop=False)
                            t.matmul(po[u][:, 0:512], C["onesM"][0:1, :],
                                     cr("l0_bo", 0, 512), start=False, stop=True)
                            ins = t.matmul(po[u][:, 512:1024], C["onesM"][0:1, :],
                                           cr("l0_bo", 512, 1024), start=False, stop=True)
                            td.inc(ins)
                    omm = td.n

                    @blk.vector
                    def _(v):
                        v.wait_ge(lq[q].h, ldm[tp])
                        vd.wait(v)
                        for u in range(2):
                            tt = 2 * tp + u
                            vd.inc(v.tensor_scalar(zbuf[:, u * D:(u + 1) * D],
                                                   xh2[:, u * D:(u + 1) * D],
                                                   mr[:, 2 * tt:2 * tt + 1],
                                                   mr[:, 2 * tt + 1:2 * tt + 2],
                                                   Op.subtract, Op.mult))
                    zxm[tp] = vd.n

                    @blk.vector
                    def _(v):
                        v.wait_ge(td.h, omm)
                        ad.wait(v)
                        vd.wait(v)
                        for u in range(2):
                            vd.inc(v.scalar_tensor_tensor(zbuf[:, u * D:(u + 1) * D],
                                                          po[u][:], 1.0,
                                                          zbuf[:, u * D:(u + 1) * D],
                                                          Op.mult, Op.add))
                    zm[tp] = vd.n

                    for u in range(2):
                        tt = 2 * tp + u
                        su = stt[:, 16 * u:16 * u + 16]
                        zu = zbuf[:, u * D:(u + 1) * D]

                        @blk.vector
                        def _(v):
                            vd.wait(v)
                            if u == 1:
                                ad.wait(v)
                            vd.inc(v.bn_stats(su[:, 0:6], zu[:, 0:512]))
                            vd.inc(v.bn_stats(su[:, 6:12], zu[:, 512:1024]))
                            vd.wait(v)
                            vd.inc(v.bn_aggr(su[:, 12:14], su[:, 0:12]))

                        @ctx.act
                        def _(a):
                            vd.wait(a)
                            ad.inc(a.activation(su[:, 14:15], su[:, 13:14], AF.Ln,
                                                bias=C["eps12"][:, 0:1]))
                            ad.wait(a)
                            ad.inc(a.activation(su[:, 14:15], su[:, 14:15], AF.Exp,
                                                scale=-0.5))

                        @blk.vector
                        def _(v):
                            ad.wait(v)
                            sg.wait(v)
                            vd.inc(v.tensor_scalar(zu, zu, su[:, 12:13], su[:, 14:15],
                                                   Op.subtract, Op.mult))
                            vd.wait(v)
                            vd.inc(v.tensor_tensor(zu, zu, g1r[:], Op.mult))
                            vd.wait(v)
                            vd.inc(v.tensor_tensor(abuf[:, u * D:(u + 1) * D], zu,
                                                   be1r[:], Op.add))

                        @blk.gpsimd
                        def _(g):
                            vd.wait(g)
                            sg.dma(g.dma_start(
                                out=H["att_st"].rearrange("(n p) d -> n p d", p=128)[b * TB + tt, :, :],
                                in_=abuf[:, u * D:(u + 1) * D]))
                ctx.aflush()
            ctx.aflush()

    if True:
        with ExitStack() as es:
            c2 = ctx.sub(es)
            w1 = c2.sb("w1", [128, CB * 4096], BF16)
            w2 = c2.sb("w2", [128, 32 * 1024], BF16)
            g2r = c2.sb("g2r", [128, D], F32)
            be2r = c2.sb("be2r", [128, D], F32)
            abuf = c2.sb("abufC", [128, 4 * D], BF16)
            aT = c2.sb("aTC", [128, 2 * CB * 256], BF16)
            hT = c2.sb("hTC", [128, 2 * 512], BF16)
            xbuf = c2.sb("xbufC", [128, 4 * D], F32)
            zb2 = c2.sb("zb2C", [128, 2 * D], F32)
            sqs = c2.sb("sqsC", [128, D], F32)
            stt = c2.sb("sttC", [128, 32], F32)
            ptr = c2.ps("ptrC", [128, 2048], BF16)
            ph = [c2.ps("phC0", [128, 512], F32), c2.ps("phC1", [128, 512], F32)]
            po2 = [c2.ps("po2C0", [128, 1024], F32), c2.ps("po2C1", [128, 1024], F32)]

            @blk.gpsimd
            def _(g):
                phase_guard(ctx, g)
                lw.dma(g.dma_start(out=v3(w1),
                                   in_=H["L0_attn_W1"].rearrange("(c p) n -> p c n", p=128)))
                lw.dma(g.dma_start(out=v3(w2, 32),
                                   in_=H["L0_attn_W2"].rearrange("(c p) n -> p c n", p=128)))

            @blk.sync
            def _(s):
                phase_guard(ctx, s)
                lx.dma(s.dma_start(out=g2r[:], in_=H["L0_attn_g2"][None, :].to_broadcast([128, D])))
                lx.dma(s.dma_start(out=be2r[:], in_=H["L0_attn_be2"][None, :].to_broadcast([128, D])))

            for b in range(NB):
                zm, trm, atcpm, sxm, ldm, w2e = {}, {}, {}, {}, {}, {}
                pgel = {}
                w2all = []
                vd_base, td_base = vd.n, td.n
                lq = [lx, ctx.sem("lx2")]
                sxq = [ctx.sem("sx"), ctx.sem("sx2")]
                sxb = [sxq[0].n, sxq[1].n]
                NFG = 16

                def emit_tail(tpx):
                    q_ = tpx % 2
                    xh2_ = xbuf[:, q_ * 2 * D:(q_ + 1) * 2 * D]
                    ah2_ = abuf[:, q_ * 2 * D:(q_ + 1) * 2 * D]

                    @blk.vector
                    def _(v):
                        v.wait_ge(td.h, w2e[tpx])
                        ad.wait(v)
                        vd.wait(v)
                        for u in range(2):
                            vd.inc(v.scalar_tensor_tensor(zb2[:, u * D:(u + 1) * D],
                                                          po2[u][:], 1.0,
                                                          ah2_[:, u * D:(u + 1) * D],
                                                          Op.mult, Op.add))
                    zm[tpx] = vd.n

                    for u in range(2):
                        su = stt[:, 16 * u:16 * u + 16]
                        zu = zb2[:, u * D:(u + 1) * D]
                        for ln_i, eps in ((0, C["eps12"]), (1, C["eps5"])):
                            @blk.vector
                            def _(v):
                                vd.wait(v)
                                if ln_i == 0 and u == 1:
                                    ad.wait(v)
                                vd.inc(v.bn_stats(su[:, 0:6], zu[:, 0:512]))
                                vd.inc(v.bn_stats(su[:, 6:12], zu[:, 512:1024]))
                                vd.wait(v)
                                vd.inc(v.bn_aggr(su[:, 12:14], su[:, 0:12]))

                            @ctx.act
                            def _(a):
                                vd.wait(a)
                                ad.inc(a.activation(su[:, 14:15], su[:, 13:14], AF.Ln,
                                                    bias=eps[:, 0:1]))
                                ad.wait(a)
                                ad.inc(a.activation(su[:, 14:15], su[:, 14:15], AF.Exp,
                                                    scale=-0.5))

                            @blk.vector
                            def _(v):
                                ad.wait(v)
                                vd.inc(v.tensor_scalar(zu, zu, su[:, 12:13], su[:, 14:15],
                                                       Op.subtract, Op.mult))
                                vd.wait(v)
                                if ln_i == 0:
                                    vd.inc(v.tensor_tensor(zu, zu, g2r[:], Op.mult))
                                    vd.wait(v)
                                    vd.inc(v.tensor_tensor(zu, zu, be2r[:], Op.add))
                                else:
                                    vd.inc(v.tensor_tensor(xh2_[:, u * D:(u + 1) * D], zu,
                                                           xh2_[:, u * D:(u + 1) * D],
                                                           Op.add))

                        @blk.sync
                        def _(s):
                            vd.wait(s)
                            sxq[q_].dma(s.dma_start(
                                out=xtile(dst, b * TB + 2 * tpx + u),
                                in_=xh2_[:, u * D:(u + 1) * D]))
                    sxm[tpx] = sxq[q_].n

                for tp in range(TB // 2):
                    q = tp % 2
                    xh2 = xbuf[:, q * 2 * D:(q + 1) * 2 * D]
                    ah2 = abuf[:, q * 2 * D:(q + 1) * 2 * D]
                    aTh = aT[:, q * CB * 256:(q + 1) * CB * 256]
                    ptrh = ptr[:, q * 1024:(q + 1) * 1024]

                    @blk.sync
                    def _(s):
                        sg.wait(s)
                        s.wait_ge(sxq[q].h, sxm.get(tp - 2, sxb[q]))
                        s.wait_ge(td.h, trm.get(tp - 2, td_base))
                        s.wait_ge(vd.h, zm.get(tp - 2, vd_base))
                        for u in range(2):
                            tt = 2 * tp + u
                            lq[q].dma(s.dma_start(
                                out=ah2[:, u * D:(u + 1) * D],
                                in_=H["att_st"].rearrange("(n p) d -> n p d", p=128)[b * TB + tt, :, :]))
                            lq[q].dma(s.dma_start(out=xh2[:, u * D:(u + 1) * D],
                                                  in_=xtile(src, b * TB + tt)))
                        ldm[tp] = lq[q].n

                    @blk.tensor
                    def _(t):
                        lw.wait(t)
                        t.wait_ge(lq[q].h, ldm[tp])
                        t.wait_ge(vd.h, atcpm.get(tp - 2, vd_base))
                        for u in range(2):
                            for cb in range(CB):
                                ins = t.transpose(ptrh[:, 128 * cb:128 * (cb + 1)],
                                                  ah2[:, u * D + 128 * cb:u * D + 128 * (cb + 1)],
                                                  C["ident"][:])
                            if u == 0:
                                td.inc(ins)
                                vd.wait(t, vd.n + 1)
                            else:
                                td.inc(ins)
                    trm[tp] = td.n

                    @blk.vector
                    def _(v):
                        v.wait_ge(td.h, trm[tp] - 1)
                        vd.inc(v.tensor_copy(
                            aTh.rearrange("p (c n) -> p c n", c=CB)[:, :, 0:128],
                            ptrh.rearrange("p (c n) -> p c n", c=CB)))
                        v.wait_ge(td.h, trm[tp])
                        vd.inc(v.tensor_copy(
                            aTh.rearrange("p (c n) -> p c n", c=CB)[:, :, 128:256],
                            ptrh.rearrange("p (c n) -> p c n", c=CB)))
                    atcpm[tp] = vd.n

                    if tp >= 1:
                        emit_tail(tp - 1)

                    gelu_marks = []
                    for fg in range(NFG + 1):
                        if fg < NFG:
                            pbuf = ph[fg % 2]

                            @blk.tensor
                            def _(t):
                                if fg == 0:
                                    t.wait_ge(vd.h, atcpm[tp])
                                    if tp >= 1:
                                        t.wait_ge(ad.h, pgel[tp - 1][-2])
                                if fg == 1:
                                    t.wait_ge(vd.h, zm.get(tp - 1, vd_base))
                                    if tp >= 1:
                                        t.wait_ge(ad.h, pgel[tp - 1][-1])
                                if fg >= 2:
                                    ad.wait(t, gelu_marks[fg - 2])
                                for sub in range(2):
                                    ff = 2 * fg + sub
                                    sl = pbuf[:, 256 * sub:256 * (sub + 1)]
                                    for cb in range(CB):
                                        t.matmul(sl, v3(w1)[:, cb, 128 * ff:128 * (ff + 1)],
                                                 v3(aTh, CB)[:, cb, :],
                                                 start=(cb == 0), stop=False,
                                                 skip_group_check=True)
                                    ins = t.matmul(sl, cr("l0_b1", 128 * ff, 128 * (ff + 1)),
                                                   C["onesN"][0:1, 0:256], start=False,
                                                   stop=True, skip_group_check=True)
                                td.inc(ins)

                            @ctx.act
                            def _(a):
                                td.wait(a)
                                gfg = tp * NFG + fg
                                if gfg >= 2:
                                    a.wait_ge(td.h, w2all[gfg - 2])
                                ad.inc(a.activation(hT[:, 512 * (fg % 2):512 * (fg % 2) + 512],
                                                    pbuf[:], AF.Gelu))
                                gelu_marks.append(ad.n)

                        if fg >= 1:
                            fgw = fg - 1

                            @blk.tensor
                            def _(t):
                                ad.wait(t, gelu_marks[fgw])
                                for u in range(2):
                                    pw = po2[u]
                                    for sub in range(2):
                                        ff = 2 * fgw + sub
                                        hsl = hT[:, 512 * (fgw % 2) + 256 * sub + 128 * u:
                                                 512 * (fgw % 2) + 256 * sub + 128 * (u + 1)]
                                        t.matmul(pw[:, 0:512], hsl,
                                                 v3(w2, 32)[:, ff, 0:512],
                                                 start=(ff == 0), stop=False)
                                        ins = t.matmul(pw[:, 512:1024], hsl,
                                                       v3(w2, 32)[:, ff, 512:1024],
                                                       start=(ff == 0), stop=False)
                                if fgw == NFG - 1:
                                    for u in range(2):
                                        pw = po2[u]
                                        t.matmul(pw[:, 0:512], C["onesM"][0:1, :],
                                                 cr("l0_b2", 0, 512), start=False, stop=True)
                                        ins = t.matmul(pw[:, 512:1024], C["onesM"][0:1, :],
                                                       cr("l0_b2", 512, 1024), start=False, stop=True)
                                td.inc(ins)
                                w2all.append(td.n)
                    pgel[tp] = list(gelu_marks)
                    w2e[tp] = td.n

                emit_tail(TB // 2 - 1)
            ctx.aflush()
    ctx.aflush()



def ln_rstd_v(ctx, blk, var_col, out_col, eps_t):
    vd, ad = ctx.sem("v"), ctx.sem("a")

    @ctx.act
    def _(a):
        vd.wait(a)
        ad.inc(a.activation(out_col, var_col, AF.Ln, bias=eps_t[:, 0:1]))
        ad.wait(a)
        ad.inc(a.activation(out_col, out_col, AF.Exp, scale=-0.5))


def emit_l0f(ctx, C, H, meta, src, dst):
    nc, blk = ctx.nc, ctx.blk
    BO = meta["boff"]
    lw, lx, sx = ctx.sem("lw"), ctx.sem("lx"), ctx.sem("sx")
    vd, ad, td = ctx.sem("v"), ctx.sem("a"), ctx.sem("t")
    lo = BO["f_bin"]
    crow = load_crow(ctx, H, lo, BO["l1_bq"], "l0f")

    def cr(name, a, b):
        return crow[0:1, BO[name] - lo + a:BO[name] - lo + b]

    win = ctx.sb("win", [128, CB * 64], BF16)
    wout = ctx.sb("wout", [64, 1024], BF16)
    b1s = ctx.sb("blk1s", [64, 1024], BF16)
    b2s = ctx.sb("blk2s", [128, CB * 64], BF16)
    xbuf = ctx.sb("xbufF", [128, 2 * D], F32)
    ybuf = ctx.sb("ybufF", [128, 2 * D], BF16)
    yT = ctx.sb("yTF", [128, CB * 256], BF16)
    sbf = ctx.sb("sbfF", [128, 128], BF16)
    sT = ctx.sb("sTF", [64, 256], BF16)
    hmi = ctx.sb("hmiF", [128, 2048], BF16)
    gT = ctx.sb("gTF", [64, 256], BF16)
    zbuf = ctx.sb("zbufF", [128, D], F32)
    sqs = ctx.sb("sqsF", [128, D], F32)
    stt = ctx.sb("sttF", [128, 40], F32)
    ptr = ctx.ps("ptrF", [128, 1024], BF16)
    psA = ctx.ps("psAF", [128, 512], F32)
    ph2 = ctx.ps("ph2F", [128, 1024], F32)
    po = [ctx.ps("poF0", [128, 1024], F32), ctx.ps("poF1", [128, 1024], F32)]

    @blk.gpsimd
    def _(g):
        phase_guard(ctx, g)
        lw.dma(g.dma_start(out=v3(win)[:, :, :],
                           in_=H["L0_ffn_Win"].rearrange("(c p) n -> p c n", p=128)))
        lw.dma(g.dma_start(out=wout[:], in_=H["L0_ffn_Wout"][:, :]))
        lw.dma(g.dma_start(out=b1s[:], in_=H["c_blk1"][:, :]))
        lw.dma(g.dma_start(out=v3(b2s)[:, :, :],
                           in_=H["c_blk2"].rearrange("(c p) n -> p c n", p=128)))

    for tp in range(NB * TB // 2):
        @blk.sync
        def _(s):
            if tp == 0:
                phase_guard(ctx, s)
            vd.wait(s)
            ad.wait(s)
            sx.wait(s)
            for u in range(2):
                lx.dma(s.dma_start(out=xbuf[:, u * D:(u + 1) * D],
                                   in_=xtile(src, 2 * tp + u)))

        @blk.vector
        def _(v):
            lx.wait(v)
            ad.wait(v)
            vd.wait(v)
            for u in range(2):
                x_u = xbuf[:, u * D:(u + 1) * D]
                st_u = stt[:, 16 * u:16 * u + 16]
                vd.inc(v.bn_stats(st_u[:, 0:6], x_u[:, 0:512]))
                vd.inc(v.bn_stats(st_u[:, 6:12], x_u[:, 512:1024]))
            vd.wait(v)
            for u in range(2):
                st_u = stt[:, 16 * u:16 * u + 16]
                vd.inc(v.bn_aggr(st_u[:, 12:14], st_u[:, 0:12]))

        for u in range(2):
            st_u = stt[:, 16 * u:16 * u + 16]
            ln_rstd_v(ctx, blk, st_u[:, 13:14], st_u[:, 14:15], C["eps5"])

        @blk.vector
        def _(v):
            ad.wait(v)
            td.wait(v)
            for u in range(2):
                st_u = stt[:, 16 * u:16 * u + 16]
                vd.inc(v.tensor_scalar(ybuf[:, u * D:(u + 1) * D],
                                       xbuf[:, u * D:(u + 1) * D],
                                       st_u[:, 12:13], st_u[:, 14:15],
                                       Op.subtract, Op.mult))

        @blk.tensor
        def _(t):
            lw.wait(t)
            vd.wait(t)
            for u in range(2):
                for cb in range(CB):
                    ins = t.transpose(ptr[:, 128 * cb:128 * (cb + 1)],
                                      ybuf[:, u * D + 128 * cb:u * D + 128 * (cb + 1)],
                                      C["ident"][:])
                td.inc(ins)
                if u == 0:
                    vd.wait(t, vd.n + 1)

        @blk.vector
        def _(v):
            td.wait(v, td.n - 1)
            vd.inc(v.tensor_copy(
                yT[:].rearrange("p (c n) -> p c n", c=CB)[:, :, 0:128],
                ptr[:].rearrange("p (c n) -> p c n", c=CB)))
            td.wait(v)
            vd.inc(v.tensor_copy(
                yT[:].rearrange("p (c n) -> p c n", c=CB)[:, :, 128:256],
                ptr[:].rearrange("p (c n) -> p c n", c=CB)))

        @blk.tensor
        def _(t):
            vd.wait(t)
            ad.wait(t)
            for u in range(2):
                sl = psA[:, 64 * u:64 * (u + 1)]
                for cb in range(CB):
                    t.matmul(sl, v3(yT)[:, cb, 128 * u:128 * (u + 1)],
                             v3(win)[:, cb, :], start=(cb == 0), stop=False,
                             skip_group_check=True)
                ins = t.matmul(sl, C["onesM"][0:1, :], cr("f_bin", 0, 64),
                               start=False, stop=True, skip_group_check=True)
            td.inc(ins)

        @ctx.act
        def _(a):
            td.wait(a)
            ad.inc(a.activation(sbf[:], psA[:, 0:128], AF.Identity))

        @blk.tensor
        def _(t):
            ad.wait(t)
            for u in range(2):
                ins = t.transpose(ptr[0:64, 128 * u:128 * (u + 1)],
                                  sbf[:, 64 * u:64 * (u + 1)], C["ident"][:])
            td.inc(ins)

        @ctx.act
        def _(a):
            td.wait(a)
            ad.inc(a.activation(sT[:], ptr[0:64, 0:256], AF.Identity))

        for u in range(2):
            @blk.tensor
            def _(t):
                ad.wait(t)
                if u == 1:
                    ad.wait(t, ad.n)
                for slc in range(8):
                    t.matmul(ph2[:, 128 * slc:128 * (slc + 1)],
                             b1s[0:64, 128 * slc:128 * (slc + 1)],
                             sT[0:64, 128 * u:128 * (u + 1)],
                             start=True, stop=False, skip_group_check=True)
                    ins = t.matmul(ph2[:, 128 * slc:128 * (slc + 1)],
                                   cr("f_b1", 128 * slc, 128 * (slc + 1)),
                                   C["onesN"][0:1, 128 * u:128 * (u + 1)],
                                   start=False, stop=True, skip_group_check=True)
                td.inc(ins)

            @ctx.act
            def _(a):
                td.wait(a)
                ad.inc(a.activation(hmi[:, 1024 * u:1024 * (u + 1)], ph2[:], AF.Relu))

        for u in range(2):
            @blk.tensor
            def _(t):
                ad.wait(t)
                for slc in range(8):
                    ins = t.matmul(psA[0:64, 128 + 128 * u:256 + 128 * u],
                                   v3(b2s)[:, slc, :],
                                   hmi[:, 1024 * u + 128 * slc:1024 * u + 128 * (slc + 1)],
                                   start=(slc == 0), stop=False,
                                   skip_group_check=True)
                ins = t.matmul(psA[0:64, 128 + 128 * u:256 + 128 * u],
                               cr("f_b2", 0, 64),
                               C["onesN"][0:1, 128 * u:128 * (u + 1)],
                               start=False, stop=True, skip_group_check=True)
                td.inc(ins)

        @ctx.act
        def _(a):
            td.wait(a)
            ad.inc(a.activation(gT[:], psA[0:64, 128:384], AF.Identity))

        for u in range(2):
            @blk.tensor
            def _(t):
                ad.wait(t)
                if u == 0:
                    vd.wait(t)
                pw = po[u]
                t.matmul(pw[:, 0:512], gT[0:64, 128 * u:128 * (u + 1)],
                         wout[0:64, 0:512], start=True, stop=False)
                t.matmul(pw[:, 512:1024], gT[0:64, 128 * u:128 * (u + 1)],
                         wout[0:64, 512:1024], start=True, stop=False)
                t.matmul(pw[:, 0:512], C["onesM"][0:1, :], cr("f_bout", 0, 512),
                         start=False, stop=True)
                ins = t.matmul(pw[:, 512:1024], C["onesM"][0:1, :], cr("f_bout", 512, 1024),
                               start=False, stop=True)
                td.inc(ins)

        for u in range(2):
            @blk.vector
            def _(v):
                td.wait(v)
                ad.wait(v)

            top_ln_residual_store(ctx, blk, C, zbuf, sqs, stt[:, 32:40],
                                  xbuf[:, u * D:(u + 1) * D],
                                  xtile(dst, 2 * tp + u), wsrc=po[u])
    ctx.aflush()


def emit_l1a(ctx, C, H, meta, src, dst):
    nc, blk = ctx.nc, ctx.blk
    BO = meta["boff"]
    lw, lx, sx = ctx.sem("lw"), ctx.sem("lx"), ctx.sem("sx")
    vd, ad, td = ctx.sem("v"), ctx.sem("a"), ctx.sem("t")
    sg = ctx.sem("sg")
    lo = BO["l1_bq"]
    crow = load_crow(ctx, H, lo, BO["l1f_b"], "l1a")

    def cr(name, a, b):
        return crow[0:1, BO[name] - lo + a:BO[name] - lo + b]

    KW = 2052
    mb_sb = ctx.sb("mb_l1a", [128, 228], F32)
    cb_sb = ctx.sb("cb_l1a", [128, 32], F32)

    with ExitStack() as es:
        c2 = ctx.sub(es)
        wnat = c2.sb("wnat", [128, CB * 1024], BF16)
        wtmp = c2.sb("wtmp", [128, CB * 1024], BF16)
        ptrw = c2.ps("ptrw", [128, 1024], BF16)

        @blk.sync
        def _(s):
            phase_guard(ctx, s)
            lx.dma(s.dma_start(out=mb_sb[:], in_=H["c_maskbias"][:, :]))
            lx.dma(s.dma_start(out=cb_sb[:], in_=H["c_colbias"][:, :]))

        for mi, wn in enumerate(["L1_attn_wq", "L1_attn_wk", "L1_attn_wv", "L1_attn_wo"]):
            @blk.gpsimd
            def _(g):
                if mi == 0:
                    phase_guard(ctx, g)
                vd.wait(g)
                td.wait(g)
                sg.wait(g)
                lw.dma(g.dma_start(out=v3(wnat),
                                   in_=H[wn].rearrange("(c p) n -> p c n", p=128)))

            for fb in range(CB):
                @blk.tensor
                def _(t):
                    lw.wait(t)
                    if fb > 0 or mi > 0:
                        vd.wait(t)
                    for cb in range(CB):
                        ins = t.transpose(ptrw[:, 128 * cb:128 * (cb + 1)],
                                          v3(wnat)[:, fb, 128 * cb:128 * (cb + 1)],
                                          C["ident"][:])
                    td.inc(ins)

                @blk.vector
                def _(v):
                    td.wait(v)
                    vd.inc(v.tensor_copy(
                        v3(wtmp)[:, :, 128 * fb:128 * (fb + 1)],
                        ptrw[:].rearrange("p (c n) -> p c n", c=CB)))

            @blk.gpsimd
            def _(g):
                vd.wait(g)
                sg.dma(g.dma_start(
                    out=H["wtscr"][1024 * mi:1024 * (mi + 1), :].rearrange("(c p) n -> p c n", p=128),
                    in_=v3(wtmp)))
        ctx.aflush()

    zrow = ctx.sb("zrow_l1a", [8, D], BF16)

    @blk.gpsimd
    def _(g):
        gmm = ctx.sem("g")
        ins = g.memset(zrow[:], 0.0)
        gmm.inc(ins)
        gmm.wait(g)
        sg.dma(g.dma_start(out=H["vscr"][0:2, :], in_=zrow[0:2, :]))
        sg.dma(g.dma_start(out=H["vscr"][2050:2056, :], in_=zrow[0:6, :]))

    for b in range(NB):
        with ExitStack() as esb:
            cbx = ctx.sub(esb)
            yT = cbx.sb("yT_l1a", [128, CB * SEQ], BF16)
            kT = cbx.sb("kT_l1a", [128, CB * KW], BF16)
            wq_sb = cbx.sb("wq_sb", [128, CB * 1024], BF16)
            qTc = cbx.sb("qTc", [128, CB * 1024], BF16)
            ysum = cbx.sb("ysum", [128, CB], F32)
            ysbf = cbx.sb("ysbf", [128, CB], BF16)
            mvrow = cbx.sb("mvrow", [1, D], BF16)
            ctx0 = cbx.sb("ctx0", [128, CB], BF16)
            q0blk = cbx.sb("q0blk", [128, CB * 16], BF16)
            xbuf = cbx.sb("xbufL", [128, D], F32)
            sqs = cbx.sb("sqsL", [128, D], F32)
            stt = cbx.sb("sttL", [128, 8], F32)

            @blk.sync
            def _(s):
                phase_guard(ctx, s)
                lx.dma(s.dma_start(out=v3(wq_sb),
                                   in_=H["wtscr"][0:1024, :].rearrange("(c p) n -> p c n", p=128)))

            with ExitStack() as es:
                c2 = cbx.sub(es)
                xb2 = c2.sb("xb2L", [128, 2 * D], F32)
                yb2 = c2.sb("yb2L", [128, 2 * D], BF16)
                st = c2.sb("stP1", [128, 32], F32)
                ptr = c2.ps("ptrL1", [128, 2048], BF16)

                @blk.gpsimd
                def _(g):
                    gmm = ctx.sem("g")
                    td.wait(g); vd.wait(g); ad.wait(g)
                    ins = g.memset(kT[:], 0.0)
                    gmm.inc(ins)

                ldm, stm, rsm, ym, trm, cpm = {}, {}, {}, {}, {}, {}
                lx2 = ctx.sem("lx2")
                lq = [lx, lx2]

                @blk.sync
                def _(s):
                    phase_guard(ctx, s)
                    lq[0].dma(s.dma_start(out=xb2[:, 0:D], in_=xtile(src, b * TB)))
                    ldm[0] = lq[0].n

                for tt in range(TB):
                    p = tt % 2
                    sl = st[:, 16 * p:16 * p + 16]
                    xh = xb2[:, p * D:(p + 1) * D]
                    yh = yb2[:, p * D:(p + 1) * D]

                    if tt + 1 < TB:
                        @blk.sync
                        def _(s):
                            if tt >= 1:
                                s.wait_ge(vd.h, ym[tt - 1])
                            q = lq[(tt + 1) % 2]
                            q.dma(s.dma_start(out=xb2[:, ((tt + 1) % 2) * D:
                                                        (((tt + 1) % 2) + 1) * D],
                                               in_=xtile(src, b * TB + tt + 1)))
                            ldm[tt + 1] = q.n

                    @blk.vector
                    def _(v):
                        v.wait_ge(lq[p].h, ldm[tt])
                        if tt >= 2:
                            v.wait_ge(ad.h, rsm[tt - 2])
                        vd.inc(v.bn_stats(sl[:, 0:6], xh[:, 0:512]))
                        vd.inc(v.bn_stats(sl[:, 6:12], xh[:, 512:1024]))
                        vd.wait(v)
                        vd.inc(v.bn_aggr(sl[:, 12:14], sl[:, 0:12]))
                        stm[tt] = vd.n

                    @ctx.act
                    def _(a):
                        a.wait_ge(vd.h, stm[tt])
                        ad.inc(a.activation(sl[:, 14:15], sl[:, 13:14], AF.Ln,
                                            bias=C["eps5"][:, 0:1]))
                        ad.wait(a)
                        ad.inc(a.activation(sl[:, 14:15], sl[:, 14:15], AF.Exp,
                                            scale=-0.5))
                        rsm[tt] = ad.n

                    @blk.vector
                    def _(v):
                        v.wait_ge(ad.h, rsm[tt])
                        vd.inc(v.tensor_scalar(yh[:], xh[:], sl[:, 12:13],
                                               sl[:, 14:15], Op.subtract, Op.mult))
                        ym[tt] = vd.n

                    @blk.tensor
                    def _(t):
                        t.wait_ge(vd.h, ym[tt])
                        if tt >= 2:
                            t.wait_ge(vd.h, cpm[tt - 2])
                        for cb in range(CB):
                            ins = t.transpose(ptr[:, 1024 * p + 128 * cb:
                                                  1024 * p + 128 * (cb + 1)],
                                              yh[:, 128 * cb:128 * (cb + 1)],
                                              C["ident"][:])
                        td.inc(ins)
                        trm[tt] = td.n

                    @blk.vector
                    def _(v):
                        v.wait_ge(td.h, trm[tt])
                        vd.inc(v.tensor_copy(
                            v3(yT)[:, :, 128 * tt:128 * (tt + 1)],
                            ptr[:, 1024 * p:1024 * (p + 1)].rearrange(
                                "p (c n) -> p c n", c=CB)))
                        cpm[tt] = vd.n

                @blk.vector
                def _(v):
                    vd.wait(v)
                    for cb in range(CB):
                        vd.inc(v.tensor_reduce(ysum[:, cb:cb + 1], v3(yT)[:, cb, :],
                                               AX.X, Op.add))
                    vd.wait(v)
                    vd.inc(v.tensor_scalar(ysbf[:], ysum[:], 1.0 / SEQ, None, Op.mult))
                ctx.aflush()

            with ExitStack() as es:
                c2 = cbx.sub(es)
                v_sb = c2.sb("v_sb_l1a", [128, TB * D], BF16)
                wcur = c2.sb("wcur", [128, CB * 1024], BF16)
                pr0 = c2.sb("pr0", [16, SEQ], BF16)
                gm0 = pr0
                p0T = c2.sb("p0T", [128, 256], BF16)
                s8 = c2.sb("s8_l1a", [16, 8], F32)
                r0b = c2.sb("r0b_l1a", [16, 2 * SEQ], F32)
                pbig = c2.ps("pbigL", [128, 2048], F32)
                ptr = c2.ps("ptrL2", [128, 1024], BF16)
                psc0 = c2.ps("psc0L", [128, 512], F32)

                @blk.sync
                def _(s):
                    phase_guard(ctx, s)
                    lx.dma(s.dma_start(out=v3(wcur),
                                       in_=H["wtscr"][2048:3072, :].rearrange("(c p) n -> p c n", p=128)))
                    lx.dma(s.dma_start(out=r0b[:], in_=H["c_r0bias"][:, :]))

                for tt in range(TB):
                    @blk.tensor
                    def _(t):
                        lw.wait(t)
                        lx.wait(t)
                        vd.wait(t)
                        if tt > 0:
                            ad.wait(t)
                        half = tt % 2
                        pv = pbig[:, 1024 * half:1024 * half + 1024]
                        for cb in range(CB):
                            t.matmul(pv[:, 0:512], v3(yT)[:, cb, 128 * tt:128 * (tt + 1)],
                                     v3(wcur)[:, cb, 0:512], start=(cb == 0), stop=False)
                            t.matmul(pv[:, 512:1024], v3(yT)[:, cb, 128 * tt:128 * (tt + 1)],
                                     v3(wcur)[:, cb, 512:1024], start=(cb == 0), stop=False)
                        t.matmul(pv[:, 0:512], C["onesM"][0:1, :], cr("l1_bv", 0, 512),
                                 start=False, stop=True)
                        ins = t.matmul(pv[:, 512:1024], C["onesM"][0:1, :], cr("l1_bv", 512, 1024),
                                       start=False, stop=True)
                        td.inc(ins)

                    @ctx.act
                    def _(a):
                        td.wait(a)
                        half = tt % 2
                        ad.inc(a.activation(v_sb[:, tt * D:(tt + 1) * D],
                                            pbig[:, 1024 * half:1024 * half + 1024],
                                            AF.Identity))

                @blk.gpsimd
                def _(g):
                    ad.wait(g)
                    sg.dma(g.dma_start(
                        out=H["vscr"][2:2050, :].rearrange("(n p) d -> p n d", p=128),
                        in_=v_sb[:].rearrange("p (n d) -> p n d", n=TB)))

                @blk.tensor
                def _(t):
                    vd.wait(t)
                    ad.wait(t)
                    for cb in range(CB):
                        t.matmul(pbig[0:1, 0:512], ysbf[:, cb:cb + 1],
                                 v3(wcur)[:, cb, 0:512], start=(cb == 0), stop=False)
                        t.matmul(pbig[0:1, 512:1024], ysbf[:, cb:cb + 1],
                                 v3(wcur)[:, cb, 512:1024], start=(cb == 0), stop=False)
                    t.matmul(pbig[0:1, 0:512], C["onesM"][0:1, 0:1], cr("l1_bv", 0, 512),
                             start=False, stop=True)
                    ins = t.matmul(pbig[0:1, 512:1024], C["onesM"][0:1, 0:1], cr("l1_bv", 512, 1024),
                                   start=False, stop=True)
                    td.inc(ins)

                @ctx.act
                def _(a):
                    td.wait(a)
                    ad.inc(a.activation(mvrow[:], pbig[0:1, 0:1024], AF.Identity))

                @blk.sync
                def _(s):
                    td.wait(s); ad.wait(s); vd.wait(s)
                    lx.dma(s.dma_start(out=v3(wcur),
                                       in_=H["wtscr"][1024:2048, :].rearrange("(c p) n -> p c n", p=128)))

                ktm = {}
                for fo in range(CB):
                    for qq in range(4):
                        @blk.tensor
                        def _(t):
                            if fo == 0 and qq == 0:
                                ad.wait(t)
                                lx.wait(t)
                            if (fo - 1, qq) in ktm:
                                t.wait_ge(ad.h, ktm[(fo - 1, qq)])
                            for cb in range(CB):
                                t.matmul(
                                    pbig[:, 512 * qq:512 * (qq + 1)],
                                    v3(wcur)[:, cb, 128 * fo:128 * (fo + 1)],
                                    v3(yT)[:, cb, 512 * qq:512 * (qq + 1)],
                                    start=(cb == 0), stop=False)
                            ins = t.matmul(pbig[:, 512 * qq:512 * (qq + 1)],
                                           cr("l1_bk", 128 * fo, 128 * (fo + 1)),
                                           C["onesN"][0:1, 0:512],
                                           start=False, stop=True)
                            td.inc(ins)
                        tmark = td.n

                        @ctx.act
                        def _(a):
                            a.wait_ge(td.h, tmark)
                            if fo == 0 and qq == 0:
                                ctx.sem("g").wait(a)
                            ad.inc(a.activation(
                                kT[:, KW * fo + 2 + 512 * qq:KW * fo + 2 + 512 * (qq + 1)],
                                pbig[:, 512 * qq:512 * (qq + 1)], AF.Identity))
                            ktm[(fo, qq)] = ad.n

                def emit_qT(qc, pqt, use_half=True):
                    for fo in range(CB):
                        @blk.tensor
                        def _(t):
                            ad.wait(t)
                            vd.wait(t)
                            lx.wait(t)
                            half = (fo % 2) if use_half else 0
                            pq = pqt[:, 1024 * half:1024 * half + 1024]
                            for cb in range(CB):
                                t.matmul(pq[:, 0:512],
                                         v3(wq_sb)[:, cb, 128 * fo:128 * (fo + 1)],
                                         v3(yT)[:, cb, 1024 * qc:1024 * qc + 512],
                                         start=(cb == 0), stop=False)
                                t.matmul(pq[:, 512:1024],
                                         v3(wq_sb)[:, cb, 128 * fo:128 * (fo + 1)],
                                         v3(yT)[:, cb, 1024 * qc + 512:1024 * (qc + 1)],
                                         start=(cb == 0), stop=False)
                            t.matmul(pq[:, 0:512], cr("l1_bq", 128 * fo, 128 * (fo + 1)),
                                     C["onesN"][0:1, 0:512], start=False, stop=True)
                            ins = t.matmul(pq[:, 512:1024], cr("l1_bq", 128 * fo, 128 * (fo + 1)),
                                           C["onesN"][0:1, 0:512], start=False, stop=True)
                            td.inc(ins)

                        @ctx.act
                        def _(a):
                            td.wait(a)
                            half = (fo % 2) if use_half else 0
                            ad.inc(a.activation(v3(qTc)[:, fo, :],
                                                pqt[:, 1024 * half:1024 * half + 1024],
                                                AF.Identity, scale=0.125))
                    ctx.aflush()

                cbx.emit_qT = emit_qT
                emit_qT(0, pbig)

                @blk.gpsimd
                def _(g):
                    gmm = ctx.sem("g")
                    vd.wait(g); ad.wait(g)
                    ins = g.memset(q0blk[:], 0.0)
                    gmm.inc(ins)

                @blk.vector
                def _(v):
                    ctx.sem("g").wait(v)
                    ad.wait(v)
                    for cb in range(CB):
                        vd.inc(v.tensor_copy(
                            v3(q0blk, CB)[0:64, cb, 2 * cb:2 * cb + 1],
                            v3(qTc)[0:64, cb, 0:1]))
                        vd.inc(v.tensor_copy(
                            v3(q0blk, CB)[64:128, cb, 2 * cb + 1:2 * cb + 2],
                            v3(qTc)[64:128, cb, 0:1]))

                @blk.tensor
                def _(t):
                    vd.wait(t)
                    ad.wait(t)
                    for cb in range(CB):
                        for qq in range(4):
                            ins = t.matmul(pbig[0:16, 512 * qq:512 * (qq + 1)],
                                           v3(q0blk, CB)[:, cb, :],
                                           kT[:, KW * cb + 2 + 512 * qq:KW * cb + 2 + 512 * (qq + 1)],
                                           start=(cb == 0), stop=(cb == CB - 1))
                    td.inc(ins)

                s0 = pbig[0:16, 0:2048]
                e0 = s0

                @blk.vector
                def _(v):
                    td.wait(v)
                    vd.wait(v)
                    vd.inc(v.tensor_tensor(s0, s0, r0b[0:16, 0:SEQ], Op.add))
                    vd.wait(v)
                    vd.inc(v.tensor_tensor(s0, s0, r0b[0:16, SEQ:2 * SEQ], Op.add))
                    vd.wait(v)
                    vd.inc(v.tensor_scalar(gm0[:], s0, 0.0, None, Op.is_ge))
                    vd.wait(v)
                    vd.inc(v.tensor_reduce(s8[:, 0:1], s0, AX.X, Op.max, negate=True))

                @ctx.act
                def _(a):
                    vd.wait(a)
                    ad.inc(a.activation(e0, s0, AF.Exp, bias=s8[:, 0:1]))

                @blk.vector
                def _(v):
                    ad.wait(v)
                    vd.inc(v.tensor_tensor(e0, e0, gm0[:], Op.mult))
                    vd.wait(v)
                    vd.inc(v.tensor_reduce(s8[:, 1:2], e0, AX.X, Op.add))
                    vd.wait(v)
                    vd.inc(v.tensor_scalar(s8[:, 2:3], s8[:, 0:1], 0.0, None, Op.is_gt))
                    vd.wait(v)
                    vd.inc(v.tensor_tensor(s8[:, 3:4], s8[:, 1:2], s8[:, 2:3], Op.add))
                    vd.wait(v)
                    vd.inc(v.reciprocal(s8[:, 4:5], s8[:, 3:4]))
                    vd.wait(v)
                    vd.inc(v.tensor_scalar(s8[:, 5:6], s8[:, 2:3], -1.0, 1.0, Op.mult, Op.add))
                    vd.wait(v)
                    vd.inc(v.tensor_tensor(s8[:, 6:7], s8[:, 4:5], s8[:, 5:6], Op.mult))
                    vd.wait(v)
                    vd.inc(v.tensor_scalar(s8[:, 7:8], s8[:, 2:3], 1.0 / SEQ, None, Op.mult))
                    vd.wait(v)
                    vd.inc(v.tensor_scalar(pr0[:], e0, s8[:, 6:7], s8[:, 7:8],
                                           Op.mult, Op.add))

                for blki in range(16):
                    @blk.tensor
                    def _(t):
                        vd.wait(t)
                        if blki > 0:
                            pass
                        ins = t.transpose(ptr[:, 16 * blki:16 * (blki + 1)],
                                          pr0[:, 128 * blki:128 * (blki + 1)],
                                          C["ident"][0:16, 0:16])
                        td.inc(ins) if blki == 15 else None

                @blk.vector
                def _(v):
                    td.wait(v)
                    vd.inc(v.tensor_copy(p0T[:], ptr[:, 0:256]))

                @blk.tensor
                def _(t):
                    vd.wait(t)
                    for h in range(16):
                        base = 64 * (h % 2)
                        col = h // 2
                        for blki in range(16):
                            ins = t.matmul(
                                psc0[base:base + 64, col:col + 1],
                                v_sb[:, 1024 * blki + 64 * h:1024 * blki + 64 * h + 64],
                                p0T[:].rearrange("p (w h) -> p w h", w=16)[:, blki, h:h + 1],
                                start=(blki == 0), stop=(blki == 15),
                                skip_group_check=True)
                    td.inc(ins)

                @ctx.act
                def _(a):
                    td.wait(a)
                    ad.inc(a.activation(ctx0[:], psc0[:, 0:8], AF.Identity))
                ctx.aflush()

            with ExitStack() as es:
                c2 = cbx.sub(es)
                vh = c2.sb("vh", [64, 2 * 4096], BF16)
                wov = c2.sb("wov", [128, CB * 1024], BF16)
                pT = c2.sb("pT_l1a", [64, 2 * 2048], BF16)
                s_sb = c2.sb("s_sb", [128, 2 * 608], F32)
                e_sb = s_sb
                gmb = c2.sb("gmb", [128, 2 * 608], BF16)
                prb = c2.sb("prb", [128, 2 * 608], BF16)
                st16 = c2.sb("st16L", [128, 2 * 96], F32)
                cbuf = c2.sb("cbufL", [128, 2 * D], BF16)
                cT = c2.sb("cTL", [128, 2 * CB * 128], BF16)
                zq = c2.sb("zqL", [128, 2 * D], F32)
                xbuf2 = c2.sb("xbuf2L", [128, 2 * D], F32)
                stt2 = c2.sb("stt2L", [128, 32], F32)
                psP = [c2.ps("psP0", [128, 1024], F32), c2.ps("psP1", [128, 1024], F32)]
                ppT = c2.ps("ppTL", [128, 2048], BF16)
                po = c2.ps("poL", [128, 1024], F32)

                @blk.gpsimd
                def _(g):
                    gmm = ctx.sem("g")
                    td.wait(g); vd.wait(g); ad.wait(g); sg.wait(g)
                    g.memset(pT[:], 0.0)
                    ins = g.memset(vh[:], 0.0)
                    gmm.inc(ins)

                @blk.sync
                def _(s):
                    phase_guard(ctx, s)
                    lx.dma(s.dma_start(out=v3(wov),
                                       in_=H["wtscr"][3072:4096, :].rearrange("(c p) n -> p c n", p=128)))

                @blk.vector
                def _(v):
                    sg.wait(v)

                @ctx.act
                def _(a):
                    sg.wait(a)

                HCOL = [38 * h if h < 8 else 512 + 38 * (h - 8) for h in range(16)]

                for pj in range(8):
                    gpair = [2 * pj, 2 * pj + 1]
                    if pj == 4:
                        cbx.emit_qT(1, po, use_half=False)

                    @blk.sync
                    def _(s):
                        vd.wait(s)
                        ad.wait(s)
                        sx.wait(s)
                        sg.wait(s)
                        ctx.sem("g").wait(s)
                        for w, gi in enumerate(gpair):
                            lx.dma(s.dma_start(out=xbuf2[:, w * D:(w + 1) * D],
                                               in_=xtile(src, b * TB + gi)))
                            vs3 = H["vscr"].rearrange("(a p) d -> a p d", p=32)
                            vhw = vh[:, w * 4096:(w + 1) * 4096]
                            lx.dma(s.dma_start(out=vhw[0:32, :].rearrange("p (j d) -> p j d", j=4),
                                               in_=vs3[4 * gi:4 * gi + 4, :, :].transpose([1, 0, 2])))
                            lx.dma(s.dma_start(out=vhw[32:36, :].rearrange("p (j d) -> p j d", j=4),
                                               in_=vs3[4 * gi + 1:4 * gi + 5, 0:4, :].transpose([1, 0, 2])))
                            lx.dma(s.dma_start(out=vhw[36:37, :].rearrange("p (j d) -> p j d", j=4),
                                               in_=H["vscr"][2:3, :].unsqueeze(1).to_broadcast([1, 4, D])))
                            lx.dma(s.dma_start(out=vhw[37:38, :].rearrange("p (j d) -> p j d", j=4),
                                               in_=mvrow[0:1, :].unsqueeze(1).to_broadcast([1, 4, D])))

                    sc_marks = []
                    for w, gi in enumerate(gpair):
                        qc = gi // 8
                        qcol = 128 * (gi - 8 * qc)

                        @blk.tensor
                        def _(t):
                            vd.wait(t)
                            ad.wait(t)
                            for h in range(16):
                                base = 64 * (h % 2)
                                fb = h // 2
                                pw = psP[w]
                                for j in range(4):
                                    a_w = 4 * gi + j
                                    t.matmul(pw[32 * j:32 * (j + 1), HCOL[h]:HCOL[h] + 36],
                                             v3(qTc)[:, fb, qcol + 32 * j:qcol + 32 * (j + 1)][base:base + 64, :],
                                             kT[base:base + 64, KW * fb + 32 * a_w:KW * fb + 32 * a_w + 36],
                                             start=True, stop=True, skip_group_check=True,
                                             tile_position=(base, 32 * j))
                                ins = t.matmul(pw[:, HCOL[h] + 36:HCOL[h] + 38],
                                               v3(qTc)[:, fb, qcol:qcol + 128][base:base + 64, :],
                                               kT[base:base + 64, KW * fb + 2:KW * fb + 4],
                                               start=True, stop=True, skip_group_check=True)
                            td.inc(ins)

                        mvar = 0 if gi == 0 else (2 if gi == 15 else 1)
                        mb1 = mb_sb[:, 38 * mvar:38 * (mvar + 1)]
                        mb1lo = mb_sb[:, 114 + 38 * mvar:114 + 38 * (mvar + 1)]

                        @blk.vector
                        def _(v):
                            td.wait(v)
                            vd.wait(v)
                            so = s_sb[:, 608 * w:608 * (w + 1)].rearrange("p (h x) -> p h x", x=38)
                            vd.inc(v.tensor_tensor(so[:, 0:8, :],
                                                   psP[w][:, 0:304].rearrange("p (h x) -> p h x", x=38),
                                                   mb1.unsqueeze(1).to_broadcast([128, 8, 38]), Op.add))
                            vd.inc(v.tensor_tensor(so[:, 8:16, :],
                                                   psP[w][:, 512:816].rearrange("p (h x) -> p h x", x=38),
                                                   mb1.unsqueeze(1).to_broadcast([128, 8, 38]), Op.add))
                            vd.wait(v)
                            vd.inc(v.tensor_tensor(so, so,
                                                   mb1lo.unsqueeze(1).to_broadcast([128, 16, 38]), Op.add))
                            vd.wait(v)
                            vd.inc(v.tensor_scalar(so[:, :, 36:37], so[:, :, 36:37],
                                                   cb_sb[:, gi:gi + 1], cb_sb[:, 16 + gi:17 + gi],
                                                   Op.add, Op.add))
                            sc_marks.append(vd.n)

                    S3 = [s_sb[:, 608 * w:608 * (w + 1)].rearrange("p (h x) -> p h x", x=38) for w in range(2)]
                    E3 = [e_sb[:, 608 * w:608 * (w + 1)].rearrange("p (h x) -> p h x", x=38) for w in range(2)]
                    P3 = [prb[:, 608 * w:608 * (w + 1)].rearrange("p (h x) -> p h x", x=38) for w in range(2)]
                    SS = [s_sb[:, 608 * w:608 * (w + 1)] for w in range(2)]
                    EE = [e_sb[:, 608 * w:608 * (w + 1)] for w in range(2)]
                    GG = [gmb[:, 608 * w:608 * (w + 1)] for w in range(2)]
                    ST = [st16[:, 96 * w:96 * (w + 1)] for w in range(2)]

                    @blk.vector
                    def _(v):
                        vd.wait(v)
                        for w in range(2):
                            vd.inc(v.tensor_scalar(GG[w][:], SS[w][:], 0.0, None, Op.is_ge))
                        vd.wait(v)
                        for w in range(2):
                            vd.inc(v.tensor_reduce(ST[w][:, 0:16], S3[w][:, :, 0:37], AX.X,
                                                   Op.max, negate=True))
                        vd.wait(v)
                        for w in range(2):
                            vd.inc(v.tensor_tensor(S3[w], S3[w],
                                                   ST[w][:, 0:16].unsqueeze(2).to_broadcast([128, 16, 38]),
                                                   Op.add))

                    @ctx.act
                    def _(a):
                        vd.wait(a)
                        for w in range(2):
                            ad.inc(a.activation(EE[w][:], SS[w][:], AF.Exp))

                    @blk.vector
                    def _(v):
                        ad.wait(v)
                        for w in range(2):
                            vd.inc(v.tensor_tensor(EE[w][:], EE[w][:], GG[w][:], Op.mult))
                        vd.wait(v)
                        for w in range(2):
                            vd.inc(v.tensor_reduce(ST[w][:, 16:32], E3[w][:, :, 0:37], AX.X, Op.add))
                        vd.wait(v)
                        for w in range(2):
                            vd.inc(v.tensor_scalar(ST[w][:, 32:48], ST[w][:, 0:16], 0.0, None, Op.is_gt))
                        vd.wait(v)
                        for w in range(2):
                            vd.inc(v.tensor_tensor(ST[w][:, 48:64], ST[w][:, 16:32], ST[w][:, 32:48], Op.add))
                        vd.wait(v)
                        for w in range(2):
                            vd.inc(v.reciprocal(ST[w][:, 64:80], ST[w][:, 48:64]))
                        vd.wait(v)
                        for w in range(2):
                            vd.inc(v.tensor_scalar(ST[w][:, 80:96], ST[w][:, 32:48], -1.0, 1.0,
                                                   Op.mult, Op.add))
                        vd.wait(v)
                        for w in range(2):
                            vd.inc(v.tensor_tensor(ST[w][:, 64:80], ST[w][:, 64:80], ST[w][:, 80:96], Op.mult))
                        vd.wait(v)
                        for w in range(2):
                            vd.inc(v.tensor_tensor(P3[w][:, :, 0:37], E3[w][:, :, 0:37],
                                                   ST[w][:, 64:80].unsqueeze(2).to_broadcast([128, 16, 37]),
                                                   Op.mult))
                        vd.wait(v)
                        for w in range(2):
                            vd.inc(v.tensor_scalar(P3[w][:, :, 37:38], ST[w][:, 32:48].unsqueeze(2),
                                                   1.0, None, Op.mult))

                    pow_ = [po, psP[0]]
                    evm, ctm, cpm2, opm = {}, {}, {}, {}

                    for w, gi in enumerate(gpair):
                        @blk.tensor
                        def _(t):
                            vd.wait(t)
                            if w == 0:
                                ad.wait(t)
                            else:
                                t.wait_ge(ad.h, evm[0])
                            for h in range(16):
                                ins = t.transpose(ppT[0:38, 128 * h:128 * (h + 1)],
                                                  prb[:, 608 * w + 38 * h:608 * w + 38 * (h + 1)],
                                                  C["ident"][:])
                            td.inc(ins)

                        @ctx.act
                        def _(a):
                            td.wait(a)
                            ctx.sem("g").wait(a)
                            ad.inc(a.activation(pT[0:38, 2048 * w:2048 * (w + 1)],
                                                ppT[0:38, :], AF.Identity))
                            evm[w] = ad.n

                    for w, gi in enumerate(gpair):
                        @blk.tensor
                        def _(t):
                            t.wait_ge(ad.h, evm[w])
                            lx.wait(t)
                            vd.wait(t)
                            pw = psP[w]
                            vhw = vh[:, w * 4096:(w + 1) * 4096]
                            for h in range(16):
                                for j in range(4):
                                    ins = t.matmul(pw[32 * j:32 * (j + 1), 64 * h:64 * (h + 1)],
                                                   pT[0:64, 2048 * w + 128 * h + 32 * j:2048 * w + 128 * h + 32 * (j + 1)],
                                                   vhw[0:64, 1024 * j + 64 * h:1024 * j + 64 * (h + 1)],
                                                   start=True, stop=True, skip_group_check=True,
                                                   tile_position=(0, 32 * j))
                            td.inc(ins)

                        @ctx.act
                        def _(a):
                            td.wait(a)
                            ad.inc(a.activation(cbuf[:, w * D:(w + 1) * D], psP[w][:, 0:1024],
                                                AF.Identity))
                            ctm[w] = ad.n

                    for w, gi in enumerate(gpair):
                        @blk.tensor
                        def _(t):
                            t.wait_ge(ad.h, ctm[w])
                            if w == 0:
                                vd.wait(t)
                            else:
                                t.wait_ge(vd.h, cpm2[0])
                            for cb in range(CB):
                                ins = t.transpose(ppT[:, 128 * cb:128 * (cb + 1)],
                                                  cbuf[:, w * D + 128 * cb:w * D + 128 * (cb + 1)],
                                                  C["ident"][:])
                            td.inc(ins)

                        @blk.vector
                        def _(v):
                            td.wait(v)
                            vd.inc(v.tensor_copy(cT[:, 1024 * w:1024 * (w + 1)], ppT[:, 0:1024]))
                            if gi == 0:
                                vd.wait(v)
                                vd.inc(v.tensor_copy(
                                    cT[:, 0:1024].rearrange("p (c n) -> p c n", c=CB)[:, :, 0:1],
                                    ctx0[:].unsqueeze(2)))
                            cpm2[w] = vd.n

                    for w, gi in enumerate(gpair):
                        @blk.tensor
                        def _(t):
                            t.wait_ge(vd.h, cpm2[w])
                            lx.wait(t)
                            if w == 1:
                                vd.wait(t)
                            dst_ps = pow_[w]
                            for cb in range(CB):
                                t.matmul(dst_ps[:, 0:512], cT[:, 1024 * w + 128 * cb:1024 * w + 128 * (cb + 1)],
                                         v3(wov)[:, cb, 0:512], start=(cb == 0), stop=False)
                                t.matmul(dst_ps[:, 512:1024], cT[:, 1024 * w + 128 * cb:1024 * w + 128 * (cb + 1)],
                                         v3(wov)[:, cb, 512:1024], start=(cb == 0), stop=False)
                            t.matmul(dst_ps[:, 0:512], C["onesM"][0:1, :], cr("l1_bo", 0, 512),
                                     start=False, stop=True)
                            ins = t.matmul(dst_ps[:, 512:1024], C["onesM"][0:1, :], cr("l1_bo", 512, 1024),
                                           start=False, stop=True)
                            td.inc(ins)
                            opm[w] = td.n

                    @blk.vector
                    def _(v):
                        v.wait_ge(td.h, opm[1])
                        ad.wait(v)
                        vd.wait(v)
                        for w in range(2):
                            vd.inc(v.tensor_copy(zq[:, w * D:(w + 1) * D], pow_[w][:, 0:1024]))

                    for w, gi in enumerate(gpair):
                        su2 = stt2[:, 16 * w:16 * w + 16]
                        zu2 = zq[:, w * D:(w + 1) * D]

                        @blk.vector
                        def _(v):
                            vd.wait(v)
                            if w == 1:
                                ad.wait(v)
                            vd.inc(v.bn_stats(su2[:, 0:6], zu2[:, 0:512]))
                            vd.inc(v.bn_stats(su2[:, 6:12], zu2[:, 512:1024]))
                            vd.wait(v)
                            vd.inc(v.bn_aggr(su2[:, 12:14], su2[:, 0:12]))

                        @ctx.act
                        def _(a):
                            vd.wait(a)
                            ad.inc(a.activation(su2[:, 14:15], su2[:, 13:14], AF.Ln,
                                                bias=C["eps5"][:, 0:1]))
                            ad.wait(a)
                            ad.inc(a.activation(su2[:, 14:15], su2[:, 14:15], AF.Exp,
                                                scale=-0.5))

                        @blk.vector
                        def _(v):
                            ad.wait(v)
                            vd.inc(v.tensor_scalar(zu2, zu2, su2[:, 12:13], su2[:, 14:15],
                                                   Op.subtract, Op.mult))
                            vd.wait(v)
                            vd.inc(v.tensor_tensor(xbuf2[:, w * D:(w + 1) * D], zu2,
                                                   xbuf2[:, w * D:(w + 1) * D], Op.add))

                        @blk.sync
                        def _(s):
                            vd.wait(s)
                            sx.dma(s.dma_start(out=xtile(dst, b * TB + gi),
                                               in_=xbuf2[:, w * D:(w + 1) * D]))
                ctx.aflush()
            ctx.aflush()
    ctx.aflush()


def emit_l1f(ctx, C, H, meta, src, dst):
    nc, blk = ctx.nc, ctx.blk
    BO = meta["boff"]
    lw, lx, sx = ctx.sem("lw"), ctx.sem("lx"), ctx.sem("sx")
    vd, ad, td = ctx.sem("v"), ctx.sem("a"), ctx.sem("t")
    lf = BO["l1f_b"]

    pk = ctx.sb("pk", [128, 2 * 1024], I32)
    tmp32 = ctx.sb("tmp32", [128, 1024], I32)
    tbuf = ctx.sb("ternrow", [128, 1024], BF16)
    ternT = ctx.sb("ternT", [128, CB * 1024], BF16)
    scr = ctx.sb("scr_l1f", [128, 1], F32)
    brep = ctx.sb("brepF2", [128, D], F32)
    xbuf = ctx.sb("xbufG", [128, 4 * D], F32)
    ybuf = ctx.sb("ybufG", [128, 4 * D], F32)
    ubuf = ctx.sb("ubufG", [128, 4 * D], F32)
    u0buf = ctx.sb("u0bufG", [128, 4 * D], F32)
    gbuf = ctx.sb("gbufG", [128, 4 * D], BF16)
    nbuf = ctx.sb("nbufG", [128, 4 * D], BF16)
    nT = ctx.sb("nTG", [128, D], BF16)
    oq = ctx.sb("oqG", [128, D], F32)
    zbuf = ctx.sb("zbufG", [128, 4 * D], F32)
    sqs = ctx.sb("sqsG", [128, D], F32)
    stt = ctx.sb("sttG", [128, 40], F32)
    stt2 = ctx.sb("stt2G", [128, 16], F32)
    ptr = ctx.ps("ptrG", [128, 1024], BF16)
    po = ctx.ps("poG", [128, 1024], F32)

    @blk.gpsimd
    def _(g):
        phase_guard(ctx, g)
        lw.dma(g.dma_start(
            out=pk[:].rearrange("p (c n) -> p c n", c=2),
            in_=H["L1_ffn_packed_weight"].rearrange("(c p n) -> p c n", p=128, n=1024)))

    @blk.sync
    def _(s):
        phase_guard(ctx, s)
        lx.dma(s.dma_start(out=scr[:], in_=H["L1_ffn_weight_scale"][None, :].to_broadcast([128, 1])))
        lx.dma(s.dma_start(out=brep[:], in_=H["ap_cpack"][0:1, lf:lf + D].to_broadcast([128, D])))

    for t8 in range(8):
        sh, half = 2 * (t8 // 2), t8 % 2

        @blk.vector
        def _(v):
            lw.wait(v)
            td.wait(v)
            vd.wait(v)
            vd.inc(v.tensor_scalar(tmp32[:], pk[:, 1024 * half:1024 * (half + 1)],
                                   sh, 3, Op.arith_shift_right, Op.bitwise_and))
            vd.wait(v)
            vd.inc(v.tensor_scalar(tbuf[:], tmp32[:], 1.0, None, Op.subtract))

        @blk.tensor
        def _(t):
            vd.wait(t)
            for cb in range(CB):
                ins = t.transpose(ptr[:, 128 * cb:128 * (cb + 1)],
                                  tbuf[:, 128 * cb:128 * (cb + 1)], C["ident"][:])
            td.inc(ins)

        @blk.vector
        def _(v):
            td.wait(v)
            vd.inc(v.tensor_copy(v3(ternT)[:, :, 128 * t8:128 * (t8 + 1)],
                                 ptr[:].rearrange("p (c n) -> p c n", c=CB)))

    PAIR = 4
    for tp in range(NB * TB // PAIR):
        @blk.sync
        def _(s):
            if tp == 0:
                phase_guard(ctx, s)
            vd.wait(s)
            ad.wait(s)
            sx.wait(s)
            for u in range(PAIR):
                lx.dma(s.dma_start(out=xbuf[:, u * D:(u + 1) * D],
                                   in_=xtile(src, PAIR * tp + u)))

        @blk.vector
        def _(v):
            lx.wait(v)
            ad.wait(v)
            vd.wait(v)
            for u in range(PAIR):
                vd.inc(v.tensor_scalar(zbuf[:, u * D:(u + 1) * D],
                                       xbuf[:, u * D:(u + 1) * D], 0.0, None,
                                       Op.add, Op.add,
                                       accum_out=stt[:, 8 * u:8 * u + 1]))

        @ctx.act
        def _(a):
            vd.wait(a)
            for u in range(PAIR):
                if u:
                    ad.wait(a)
                ad.inc(a.activation(sqs[:], xbuf[:, u * D:(u + 1) * D], AF.Square,
                                    accum_out=stt[:, 8 * u + 1:8 * u + 2]))

        for u in range(PAIR):
            ln_rstd(ctx, blk, stt[:, 8 * u:8 * u + 8], C["eps5"])

        @blk.vector
        def _(v):
            ad.wait(v)
            S = [stt[:, 8 * u_:8 * u_ + 8] for u_ in range(PAIR)]
            Y = [ybuf[:, u_ * D:(u_ + 1) * D] for u_ in range(PAIR)]
            U0 = [u0buf[:, u_ * D:(u_ + 1) * D] for u_ in range(PAIR)]
            UB = [ubuf[:, u_ * D:(u_ + 1) * D] for u_ in range(PAIR)]
            GB = [gbuf[:, u_ * D:(u_ + 1) * D] for u_ in range(PAIR)]
            NBF = [nbuf[:, u_ * D:(u_ + 1) * D] for u_ in range(PAIR)]
            for u in range(PAIR):
                vd.inc(v.tensor_scalar(Y[u], xbuf[:, u * D:(u + 1) * D],
                                       S[u][:, 2:3], S[u][:, 5:6],
                                       Op.subtract, Op.mult))
            vd.wait(v)
            for u in range(PAIR):
                vd.inc(v.tensor_reduce(S[u][:, 6:7], Y[u], AX.X, Op.max,
                                       apply_absolute_value=True))
            vd.wait(v)
            for u in range(PAIR):
                vd.inc(v.tensor_scalar(S[u][:, 7:8], S[u][:, 6:7], 1e-5, None, Op.max))
            vd.wait(v)
            for u in range(PAIR):
                vd.inc(v.reciprocal(stt2[:, 4 * u:4 * u + 1], S[u][:, 7:8]))
            vd.wait(v)
            for u in range(PAIR):
                vd.inc(v.tensor_scalar(stt2[:, 4 * u + 1:4 * u + 2],
                                       stt2[:, 4 * u:4 * u + 1], 127.0, None, Op.mult))
            vd.wait(v)
            for u in range(PAIR):
                vd.inc(v.tensor_scalar(stt2[:, 4 * u + 2:4 * u + 3], S[u][:, 7:8],
                                       scr[:, 0:1], -1.0 / 127.0, Op.mult, Op.mult))
            vd.wait(v)
            for u in range(PAIR):
                vd.inc(v.tensor_scalar(U0[u], Y[u], stt2[:, 4 * u + 1:4 * u + 2],
                                       128.5, Op.mult, Op.add))
            usrc = u0buf
            for bit in [128, 64, 32, 16, 8, 4, 2, 1]:
                vd.wait(v)
                vd.inc(v.tensor_scalar(gbuf[:], usrc[:], float(bit), float(bit),
                                       Op.is_ge, Op.mult))
                vd.wait(v)
                vd.inc(v.scalar_tensor_tensor(ubuf[:], gbuf[:], -1.0, usrc[:],
                                              Op.mult, Op.add))
                usrc = ubuf
            vd.wait(v)
            vd.inc(v.scalar_tensor_tensor(nbuf[:], u0buf[:], -1.0, ubuf[:],
                                          Op.mult, Op.add))
            vd.wait(v)
            vd.inc(v.tensor_scalar(nbuf[:], nbuf[:], 128.0, None, Op.add))

        for u in range(PAIR):
            @blk.tensor
            def _(t):
                vd.wait(t)
                ad.wait(t)
                for cb in range(CB):
                    ins = t.transpose(ptr[:, 128 * cb:128 * (cb + 1)],
                                      nbuf[:, u * D + 128 * cb:u * D + 128 * (cb + 1)],
                                      C["ident"][:])
                td.inc(ins)

            @ctx.act
            def _(a):
                td.wait(a)
                ad.inc(a.activation(nT[:], ptr[:], AF.Identity))

            @blk.tensor
            def _(t):
                ad.wait(t)
                ins = None
                for cb in range(CB):
                    t.matmul(po[:, 0:512], nT[:, 128 * cb:128 * (cb + 1)],
                             v3(ternT)[:, cb, 0:512],
                             start=(cb == 0), stop=(cb == CB - 1))
                    ins = t.matmul(po[:, 512:1024], nT[:, 128 * cb:128 * (cb + 1)],
                                   v3(ternT)[:, cb, 512:1024],
                                   start=(cb == 0), stop=(cb == CB - 1))
                td.inc(ins)

            @ctx.act
            def _(a):
                td.wait(a)
                vd.wait(a)
                ad.inc(a.activation(oq[:], po[:], AF.Identity,
                                    scale=stt2[:, 4 * u + 2:4 * u + 3]))

            @blk.vector
            def _(v):
                ad.wait(v)
                vd.wait(v)
                vd.inc(v.scalar_tensor_tensor(zbuf[:, 0:D], oq[:], 1.0, brep[:],
                                              Op.mult, Op.add,
                                              accum_out=stt[:, 32:33]))

            @ctx.act
            def _(a):
                vd.wait(a)
                ad.inc(a.activation(sqs[:], zbuf[:, 0:D], AF.Square,
                                    accum_out=stt[:, 33:34]))

            ln_rstd(ctx, blk, stt[:, 32:40], C["eps5"])

            @blk.vector
            def _(v):
                ad.wait(v)
                vd.inc(v.tensor_scalar(zbuf[:, 0:D], zbuf[:, 0:D],
                                       stt[:, 34:35], stt[:, 37:38],
                                       Op.subtract, Op.mult))
                vd.wait(v)
                vd.inc(v.tensor_tensor(xbuf[:, u * D:(u + 1) * D], zbuf[:, 0:D],
                                       xbuf[:, u * D:(u + 1) * D], Op.add))

            @blk.sync
            def _(s):
                vd.wait(s)
                sx.dma(s.dma_start(out=xtile(dst, PAIR * tp + u),
                                   in_=xbuf[:, u * D:(u + 1) * D]))
    ctx.aflush()



def emit_common_consts(ctx, H, meta):
    nc, blk = ctx.nc, ctx.blk
    C = {}
    C["ident"] = ctx.sb("identC", [128, 128], BF16)
    C["onesM"] = ctx.sb("onesMC", [32, 128], BF16)
    C["onesN"] = ctx.sb("onesNC", [32, 512], BF16)
    C["eps5"] = ctx.sb("eps5C", [128, 1], F32)
    C["eps12"] = ctx.sb("eps12C", [128, 1], F32)
    C["neg1"] = ctx.sb("neg1C", [128, 1], F32)
    C["ln8"] = ctx.sb("ln8C", [128, 1], F32)
    lc = ctx.sem("lc")
    gm = ctx.sem("g")
    vd = ctx.sem("v")

    @blk.gpsimd
    def _(g):
        g.memset(C["onesM"][:], 0.0)
        g.memset(C["eps5"][:], 1e-5)
        g.memset(C["eps12"][:], 1e-12)
        g.memset(C["neg1"][:], -1.0)
        g.memset(C["ln8"][:], LN_8INV)
        ins = g.memset(C["onesN"][:], 0.0)
        gm.inc(ins)

    @blk.sync
    def _(s):
        lc.dma(s.dma_start(out=C["ident"][:], in_=H["ap_ident"][:, :]))

    @blk.vector
    def _(v):
        gm.wait(v)
        lc.wait(v)
        vd.inc(v.tensor_scalar(C["onesM"][0:1, :], C["onesM"][0:1, :], 1.0, None, Op.add))
        vd.inc(v.tensor_scalar(C["onesN"][0:1, :], C["onesN"][0:1, :], 1.0, None, Op.add))
    return C


STAGE_FN = {"l0a": emit_l0a, "l0f": emit_l0f, "l1a": emit_l1a, "l1f": emit_l1f}


def build_stage(stages, meta):
    nc = bass.Bass("TRN2", target_bir_lowering=False)
    H = {}
    H["x_in"] = nc.dram_tensor("x_in", [T, D], F32, kind="ExternalInput").ap()
    H["y_out"] = nc.dram_tensor("y_out", [T, D], F32, kind="ExternalOutput").ap()
    H["xs"] = nc.dram_tensor("xs", [T, D], F32, kind="Internal").ap()
    H["ap_ident"] = nc.dram_tensor("c_ident", [128, 128], BF16, kind="ExternalInput").ap()
    H["ap_cpack"] = nc.dram_tensor("c_cpack", [1, meta["cw"]], F32, kind="ExternalInput").ap()
    for nm, shape, dt in [("L0_attn_Wq", [D, D], F32), ("L0_attn_Wk", [D, D], F32),
                          ("L0_attn_Wv", [D, D], F32), ("L0_attn_Wo", [D, D], F32),
                          ("L0_attn_W1", [D, 4096], F32), ("L0_attn_W2", [4096, D], F32),
                          ("L0_attn_g1", [D], F32), ("L0_attn_be1", [D], F32),
                          ("L0_attn_g2", [D], F32), ("L0_attn_be2", [D], F32),
                          ("L0_ffn_Win", [D, 64], F32), ("L0_ffn_Wout", [64, D], F32),
                          ("c_blk1", [64, 1024], F32), ("c_blk2", [1024, 64], F32),
                          ("L1_attn_wq", [D, D], F32), ("L1_attn_wk", [D, D], F32),
                          ("L1_attn_wv", [D, D], F32), ("L1_attn_wo", [D, D], F32),
                          ("c_maskbias", [128, 228], F32), ("c_colbias", [128, 32], F32),
                          ("c_r0bias", [16, 2 * SEQ], F32),
                          ("L1_ffn_packed_weight", [262144], I32),
                          ("L1_ffn_weight_scale", [1], F32)]:
        H[nm] = nc.dram_tensor(nm, shape, dt, kind="ExternalInput").ap()
    H["att_st"] = nc.dram_tensor("att_st", [T, D], BF16, kind="Internal").ap()
    H["wtscr"] = nc.dram_tensor("wtscr", [4 * D, D], BF16, kind="Internal").ap()
    H["vscr"] = nc.dram_tensor("vscr", [SEQ + 64, D], BF16, kind="Internal").ap()

    es = ExitStack()
    with es:
        blk = es.enter_context(nc.Block())
        ctx = Ctx(nc, es, blk)
        C = emit_common_consts(ctx, H, meta)
        n = len(stages)
        for i, st in enumerate(stages):
            src = H["x_in"] if i == 0 else H["xs"]
            dst = H["y_out"] if i == n - 1 else H["xs"]
            with ExitStack() as ses:
                STAGE_FN[st](ctx.sub(ses), C, H, meta, src, dst)
        sx = ctx.sem("sx")
        sx2 = ctx.sem("sx2")
        sg = ctx.sem("sg")

        @blk.sync
        def _(s):
            sx.wait(s)
            sx2.wait(s)
            sg.wait(s)
    return nc


WNAMES = ["L0_attn_Wq", "L0_attn_Wk", "L0_attn_Wv", "L0_attn_Wo",
          "L0_attn_W1", "L0_attn_W2", "L0_attn_g1", "L0_attn_be1",
          "L0_attn_g2", "L0_attn_be2", "L0_ffn_Win", "L0_ffn_Wout",
          "L1_attn_wq", "L1_attn_wk", "L1_attn_wv", "L1_attn_wo",
          "L1_ffn_packed_weight", "L1_ffn_weight_scale"]


def prep(inputs):
    consts = host_constants()
    lay = host_layout(inputs)
    meta = {"cw": lay["cpack"].shape[1], "boff": lay["_boff"]}
    x = np.ascontiguousarray(np.asarray(inputs["L0_attn_x"], dtype=np.float32))
    shards = [np.ascontiguousarray(x[2 * c:2 * c + 2].reshape(T, D))
              for c in range(NCORES)]
    base_map = {
        "c_ident": consts["ident"],
        "c_cpack": np.ascontiguousarray(lay["cpack"], dtype=np.float32),
        "c_maskbias": consts["maskbias"], "c_colbias": consts["colbias"],
        "c_r0bias": consts["r0bias"],
        "c_blk1": lay["blk1"], "c_blk2": lay["blk2"],
    }
    for nm in WNAMES:
        base_map[nm] = np.ascontiguousarray(np.asarray(inputs[nm]))
    return meta, shards, base_map


def run_stages(stage_groups, meta, shards, base_map):
    cur = shards
    for sg in stage_groups:
        nc = build_stage(sg, meta)
        in_maps = [dict(base_map, x_in=cur[c]) for c in range(NCORES)]
        res = run_bass_kernel_spmd(nc, in_maps, core_ids=list(range(NCORES)))
        cur = [np.asarray(res.results[c]["y_out"]) for c in range(NCORES)]
    return cur


def kernel(**inputs):
    meta, shards, base_map = prep(inputs)
    cur = run_stages([["l0a", "l0f", "l1a", "l1f"]], meta, shards, base_map)
    out = np.stack(cur, 0).reshape(16, SEQ, D)
    return np.ascontiguousarray(out.astype(np.float32))
